# version 09
# baseline (speedup 1.0000x reference)
import sys
sys.path.insert(0, "/opt/trn_rl_repo")
import numpy as np
import ml_dtypes
import concourse.bass as bass
import concourse.mybir as mybir
from concourse.bass_utils import run_bass_kernel_spmd

F32 = mybir.dt.float32
BF16 = mybir.dt.bfloat16
AF = mybir.ActivationFunctionType
ALU = mybir.AluOpType
AX = mybir.AxisListType

S, D = 16384, 2048
NCORE = 8
TPC = S // NCORE
NK, NV, DK, DV = 16, 32, 128, 128
C = 128
NCH = TPC // C
RT = 512
NR = TPC // RT
KR = RT // C
NJB = 64
NDC = D // 128
NEWTON = 6
MASKNEG = -1.0e6
NEXP = 8
F = 1024
NFB = F // 128

STAGE = "full"
SIM1 = False
DEV_TAPS = ()


class Builder:

    def __init__(self, nc):
        self.nc = nc
        self.semg = [nc.semaphore(f"tk{i}") for i in range(64)]
        self.sems = [g.__enter__() for g in self.semg]
        self.rng = {"tensor": list(range(0, 12)), "vector": list(range(12, 24)),
                    "scalar": list(range(24, 30)), "s": list(range(30, 40)),
                    "g": list(range(40, 63)), "c": [63]}
        self.pos = {k: 0 for k in self.rng}
        self.cnt = {i: 0 for i in range(64)}
        self.res = {}
        self.floor = []
        self.waited = {k: {} for k in ("sync", "tensor", "vector", "scalar", "gpsimd")}
        self.prog = {k: [] for k in ("sync", "tensor", "vector", "scalar", "gpsimd")}

    def _sem(self, cls):
        if cls in ("g", "s"):
            self.pos[cls] = (self.pos[cls] + 1) % len(self.rng[cls])
            return self.rng[cls][self.pos[cls]]
        idx = self.rng[cls][self.pos[cls]]
        if self.cnt[idx] > 20000:
            self.pos[cls] += 1
            idx = self.rng[cls][self.pos[cls]]
        return idx

    def step(self, emits, rd=(), wr=()):
        deps = {}
        def add(tok):
            for (si, v) in tok:
                if deps.get(si, -1) < v:
                    deps[si] = v
        add(self.floor)
        for n in rd:
            e = self.res.get(n)
            if e is None:
                self.res[n] = e = {"w": [], "r": []}
            add(e["w"])
        for n in wr:
            e = self.res.get(n)
            if e is None:
                self.res[n] = e = {"w": [], "r": []}
            add(e["w"])
            for t in e["r"]:
                add(t)
        toks = {}
        seen = set()
        for em in emits:
            ename, fn, ninc = em[0], em[1], em[2]
            cls = em[3] if len(em) > 3 else (
                "g" if ename == "gpsimd" else ("s" if ename == "sync" else ename))
            si = self._sem(cls)
            sem = self.sems[si]
            if cls in ("g", "s", "c") and self.cnt[si] > 0:
                if deps.get(si, -1) < self.cnt[si]:
                    deps[si] = self.cnt[si]
            if ename not in seen:
                seen.add(ename)
                dep_list = sorted(deps.items())
                w = self.waited[ename]
                for (dsi, dval) in dep_list:
                    if w.get(dsi, -1) >= dval:
                        continue
                    w[dsi] = dval
                    dsem = self.sems[dsi]
                    self.prog[ename].append(
                        lambda e, dsem=dsem, dval=dval: e.wait_ge(dsem, dval))
            self.prog[ename].append(lambda e, fn=fn, sem=sem: fn(e, sem))
            self.cnt[si] += ninc
            toks[si] = self.cnt[si]
        token = sorted(toks.items())
        for n in wr:
            e = self.res[n]
            e["w"] = token
            e["r"] = []
        for n in rd:
            self.res[n]["r"].append(token)
        self.last = token
        return token

    def barrier(self):
        tok = [(si, c) for si, c in self.cnt.items() if c > 0]
        self.floor = tok
        self.res = {}
        return tok


def build_program():
    nc = bass.Bass("TRN2", target_bir_lowering=False)
    dt = lambda n, s, d, k="ExternalInput": nc.dram_tensor(n, s, d, kind=k).ap()

    xT = dt("xT", [D, TPC + 3], F32)
    WqkvT = dt("WqkvT", [D, 8192], F32)
    WzT = dt("WzT", [D, 4096], F32)
    WbaT = dt("WbaT", [D, 64], F32)
    convw = dt("convw", [8192, 4], F32)
    dtb = dt("dtb", [128, NV], F32)
    alog = dt("alog", [1, NV], F32)
    normw = dt("normw", [128, NV], F32)
    WoutT = dt("WoutT", [4096, D], F32)
    rmsw = dt("rmsw", [128, NDC], F32)
    routw = dt("routw", [D, NEXP], F32)
    wg = dt("wg", [NEXP, D, F], F32)
    wu = dt("wu", [NEXP, D, F], F32)
    wd = dt("wd", [NEXP, F, D], F32)
    swg = dt("swg", [D, F], F32)
    swu = dt("swu", [D, F], F32)
    swd = dt("swd", [F, D], F32)
    identb = dt("identb", [128, 128], BF16)
    identb32 = dt("identb32", [128, NV * 128], BF16)
    identf = dt("identf", [128, 128], F32)
    trii = dt("trii", [128, 128], F32)
    onesf = dt("onesf", [1, 128], F32)
    monesf = dt("monesf", [1, 128], F32)
    mneg_nat4 = dt("mneg_nat4", [128, 512], F32)
    mneg_T4 = dt("mneg_T4", [128, 512], F32)
    mneg_iT4 = dt("mneg_iT4", [128, 512], F32)
    twoI4 = dt("twoI4", [128, 512], F32)
    onehot = dt("onehot", [128, NCORE], F32)
    onesb = dt("onesb", [1, 128], BF16)
    fconst = dt("fconst", [128, 6], F32)
    onescb = dt("onescb", [128, 1], BF16)
    identb4 = dt("identb4", [128, 512], BF16)

    outT = dt("outT", [D, TPC], F32, "ExternalOutput")

    arena_la = dt("arena_la", [NV, TPC], F32, "Internal")
    arena_lb = dt("arena_lb", [NV, TPC], F32, "Internal")
    arena_wr = dt("arena_wr", [128, 128], F32, "Internal")
    sp_ob = dt("sp_ob", [NV * NCH * 128, 128], BF16, "Internal")
    sp_h5 = dt("sp_h5", [NV * NCH * 128, 128], BF16, "Internal")
    sp_x1 = dt("sp_x1", [NDC * 128, TPC], F32, "Internal")
    sp_su = dt("sp_su", [(NEXP + 1) * NFB * 128, TPC], BF16, "Internal")
    sp_ffn = dt("sp_ffn", [NDC * 128, TPC], F32, "Internal")
    sp_z = dt("sp_z", [NV * 128, TPC], BF16, "Internal")
    cc_in = dt("cc_in", [2 * NV * 128, 128], BF16, "Internal")
    cc_out = dt("cc_out", [NCORE * 2 * NV * 128, 128], BF16, "Internal")

    taps = {}
    for name, shape, dtp in DEV_TAPS:
        taps[name] = dt("tap_" + name, list(shape), dtp, "ExternalOutput")

    sb = lambda n, s, d: nc.alloc_sbuf_tensor(n, s, d).ap()
    c_identb = sb("c_identb", [128, 128], BF16)
    c_identf = sb("c_identf", [128, 128], F32)
    c_trii = sb("c_trii", [128, 128], F32)
    c_onesf = sb("c_onesf", [1, 128], F32)
    c_monesf = sb("c_monesf", [1, 128], F32)
    c_mnegN = sb("c_mnegN", [128, 512], F32)
    c_mnegT = sb("c_mnegT", [128, 512], F32)
    c_mnegI = sb("c_mnegI", [128, 512], F32)
    c_twoI4 = sb("c_twoI4", [128, 512], F32)
    c_dtb = sb("c_dtb", [128, NV], F32)
    c_normw = sb("c_normw", [128, NV], F32)
    c_rmsw = sb("c_rmsw", [128, NDC], F32)
    c_onehot = sb("c_onehot", [128, NCORE], F32)
    c_alog = sb("c_alog", [1, NV], F32)
    exprow = sb("exprow", [1, NV], F32)
    expA = sb("expA", [128, NV], F32)
    convw_sb = sb("convw_sb", [128, NJB * 4], F32)
    wba_sb = sb("wba_sb", [128, 16 * 64], BF16)

    xtb = sb("xtb", [128, NDC * RT], BF16)
    xtail = sb("xtail", [128, NDC * 3], BF16)
    wjb = sb("wjb", [128, 2 * 2048], BF16)
    qkv = sb("qkv", [128, NJB * RT], BF16)
    c_onesb = sb("c_onesb", [1, 128], BF16)
    c_fconst = sb("c_fconst", [128, 6], F32)
    c_onescb = sb("c_onescb", [128, 1], BF16)
    c_identb4 = sb("c_identb4", [128, 512], BF16)
    for _i, _v in enumerate([1e-6, 1e-5, 1.0, 1.0 / 4096.0, 1.0 / 2048.0, -1.0]):
        nc.const_aps.register(F32, _v, c_fconst[:, _i:_i + 1])
    ab_guards = []
    def sba(n, s, d):
        g = nc.sbuf_tensor(n, s, d)
        ap = g.__enter__()
        ab_guards.append(g)
        return ap
    cpre = sba("cpre", [128, 515], F32)
    cacc = [sba("cacc0", [128, 512], F32), sba("cacc1", [128, 512], F32)]
    tails = sba("tails", [128, NJB * 3], F32)
    batile = cacc[1][:, 0:256]
    bcols = sba("bcols", [128, KR * NV], F32)
    gcols = sba("gcols", [128, KR * NV], F32)
    spa = cacc[1][:, 256:384]
    spb = cacc[1][:, 384:512]
    lacols = sba("lacols", [128, KR * NV], F32)
    nlacols = sba("nlacols", [128, KR * NV], F32)
    lbcols = sba("lbcols", [128, KR * NV], F32)
    elacols = sba("elacols", [128, KR * NV], F32)
    bwcols = sba("bwcols", [128, KR * NV], F32)
    ewbcols = sba("ewbcols", [128, KR * NV], F32)
    accols = sba("accols", [128, KR * NV], F32)
    lacrow = sba("lacrow", [1, NV], F32)
    rowstage = sba("rowstage", [32, 2 * 128], F32)
    laro2 = [sba("laroPA", [1, RT], F32), sba("laroPB", [1, RT], F32)]
    lbro2 = [sba("lbroPA", [1, RT], F32), sba("lbroPB", [1, RT], F32)]
    sqtile = cacc[0]
    rsqrow = sba("rsqrow", [1, 512], F32)

    W2 = lambda n, s, d: [sba(n + "A", s, d), sba(n + "B", s, d)]
    w_eT2 = W2("w_eT", [128, 512], BF16)
    w_eN2 = W2("w_eN", [128, 512], BF16)
    w_An2 = W2("w_An", [128, 512], BF16)
    w_X2 = W2("w_X", [128, 512], BF16)
    w_Xt2 = W2("w_Xt", [128, 512], BF16)
    w_Z2 = W2("w_Z", [128, 512], BF16)
    w_PmT2 = W2("w_PmT", [128, 512], BF16)
    w_QwT2 = W2("w_QwT", [128, 512], BF16)
    w_Kr2 = W2("w_Kr", [128, 512], BF16)
    w_Vr2 = W2("w_Vr", [128, 512], BF16)
    w_RV2 = W2("w_RV", [128, 1024], BF16)
    w_Wb2 = W2("w_Wb", [128, 512], BF16)
    w_UG2 = W2("w_UG", [128, 1024], BF16)
    w_McT2 = W2("w_McT", [128, 512], BF16)
    w_Nc2 = W2("w_Nc", [128, 512], F32)
    w_H32 = W2("w_H3", [128, 512], BF16)
    w_Ob2 = W2("w_Ob", [128, 512], BF16)
    w_H52 = W2("w_H5", [128, 512], BF16)
    kkS2 = W2("kkS", [128, 512], BF16)
    kqS2 = W2("kqS", [128, 512], BF16)
    elarow2 = W2("elarowP", [1, RT], F32)
    Qs_f = sba("Qs_f", [128, NV * 128], F32)
    Qs_b = sba("Qs_b", [128, NV * 128], BF16)
    Pi_b = sba("Pi_b", [128, NV * 128], BF16)

    psg = []
    def psum(n, shape, dtp):
        g = nc.psum_tensor(n, shape, dtp)
        ap = g.__enter__()
        psg.append(g)
        return ap
    p = [psum(f"p{i}", [128, 512], F32) for i in range(7)]
    pbg = nc.psum_tensor("pb", [128, 1024], BF16)
    pb = pbg.__enter__()

    bld = Builder(nc)
    st = bld.step

    def dma(e, sem, out, in_):
        return e.dma_start(out=out, in_=in_).then_inc(sem, 16)

    def ld_consts(e, sem):
        dma(e, sem, c_identb[:], identb)
        dma(e, sem, c_identf[:], identf)
        dma(e, sem, c_trii[:], trii)
        dma(e, sem, c_onesf[:], onesf)
        dma(e, sem, c_monesf[:], monesf)
        dma(e, sem, c_mnegN[:], mneg_nat4)
        dma(e, sem, c_mnegT[:], mneg_T4)
        dma(e, sem, c_mnegI[:], mneg_iT4)
        dma(e, sem, c_twoI4[:], twoI4)
        dma(e, sem, c_dtb[:], dtb)
        dma(e, sem, c_normw[:], normw)
        dma(e, sem, c_rmsw[:], rmsw)
        dma(e, sem, c_onehot[:], onehot)
        dma(e, sem, c_alog[:], alog)
        dma(e, sem, convw_sb[:], convw.rearrange("(j p) t -> p j t", p=128))
        dma(e, sem, Pi_b[:], identb32)
        dma(e, sem, c_onesb[:], onesb)
        dma(e, sem, c_fconst[:], fconst)
        dma(e, sem, c_onescb[:], onescb)
        dma(e, sem, c_identb4[:], identb4)

    def ld_casts(e, sem):
        dma(e, sem, wba_sb[:], WbaT.rearrange("(k p) j -> p k j", p=128))
        dma(e, sem, xtail[:], xT[:, 0:3].rearrange("(k p) t -> p k t", p=128))
    st([("sync", ld_consts, 16 * 20), ("gpsimd", ld_casts, 32)])
    bld.barrier()

    st([("scalar", lambda e, s: e.activation(out=exprow[:], in_=c_alog[:],
                                             func=AF.Exp).then_inc(s, 1), 1)],
       wr=["exprow"])
    st([("tensor", lambda e, s: e.matmul(p[0][:, 0:NV], c_onesf[:], exprow[:],
                                         start=True, stop=True).then_inc(s, 1), 1)],
       rd=["exprow"], wr=["p0"])
    def setup_v(e, s):
        e.tensor_scalar(out=expA[:], in0=p[0][:, 0:NV], scalar1=-1.0, scalar2=None,
                        op0=ALU.mult).then_inc(s, 1)
        e.tensor_scalar(out=c_normw[:], in0=c_normw[:], scalar1=1.0, scalar2=None,
                        op0=ALU.add).then_inc(s, 1)
        e.memset(Qs_f[:], 0.0).then_inc(s, 1)
        e.memset(Qs_b[:], 0.0).then_inc(s, 1)
        e.memset(tails[:], 0.0).then_inc(s, 1)
    st([("vector", setup_v, 5)], rd=["p0"], wr=["expA"])
    bld.barrier()

    for r in range(NR):
        c0 = 3 + r * RT

        def ld_x(e, sem, c0=c0):
            for k in range(NDC):
                dma(e, sem, xtb[:, k * RT:(k + 1) * RT],
                    xT[k * 128:(k + 1) * 128, c0:c0 + RT])
        st([("gpsimd", ld_x, 16 * NDC)], wr=["xtb"])

        for cc in range(KR):
            def mm_ba(e, sem, cc=cc):
                for k in range(NDC):
                    ii = e.matmul(p[0][:, 0:64],
                             xtb[:, k * RT + cc * C: k * RT + cc * C + C],
                             wba_sb[:, k * 64:(k + 1) * 64],
                             start=(k == 0), stop=(k == NDC - 1))
                ii.then_inc(sem, 1)
            st([("tensor", mm_ba, 1)], rd=["xtb"], wr=["p0"])
            st([("vector", lambda e, s, cc=cc: e.tensor_copy(
                out=batile[:, cc * 64:(cc + 1) * 64], in_=p[0][:, 0:64]).then_inc(s, 1), 1)],
               rd=["p0"], wr=["cacc1"])
        def act_beta(e, s):
            for cc in range(KR):
                e.activation(out=bcols[:, cc * NV:(cc + 1) * NV],
                             in_=batile[:, cc * 64: cc * 64 + NV],
                             func=AF.Sigmoid).then_inc(s, 1)
        st([("scalar", act_beta, KR)], rd=["cacc1"], wr=["bcols"])
        def sp1(e, s):
            for cc in range(KR):
                e.tensor_tensor(out=spa[:, cc * NV:(cc + 1) * NV],
                                in0=batile[:, cc * 64 + NV: cc * 64 + 64],
                                in1=c_dtb[:], op=ALU.add).then_inc(s, 1)
        st([("vector", sp1, KR)], rd=["cacc1"], wr=["cacc1"])
        st([("scalar", lambda e, s: e.activation(out=spb[:], in_=spa[:],
                                                 func=AF.Abs).then_inc(s, 1), 1)],
           rd=["cacc1"], wr=["cacc1"])
        st([("scalar", lambda e, s: e.activation(out=spb[:], in_=spb[:],
                                                 func=AF.Exp,
                                                 scale=-1.0).then_inc(s, 1), 1)],
           rd=["cacc1"], wr=["cacc1"])
        st([("scalar", lambda e, s: e.activation(out=spb[:], in_=spb[:], func=AF.Ln,
                                                 bias=1.0).then_inc(s, 1), 1)],
           rd=["cacc1"], wr=["cacc1"])
        st([("vector", lambda e, s: e.scalar_tensor_tensor(
            out=spa[:], in0=spa[:], scalar=0.0, in1=spb[:],
            op0=ALU.max, op1=ALU.add).then_inc(s, 1), 1)],
           rd=["cacc1"], wr=["cacc1"])
        def gmul(e, s):
            for cc in range(KR):
                e.tensor_tensor(out=gcols[:, cc * NV:(cc + 1) * NV],
                                in0=spa[:, cc * NV:(cc + 1) * NV],
                                in1=expA[:], op=ALU.mult).then_inc(s, 1)
        st([("vector", gmul, KR)], rd=["cacc1", "expA"], wr=["gcols"])
        for cc in range(KR):
            def mm_la(e, s, cc=cc):
                ii = e.matmul(p[1][:, 0:NV], c_trii[:], gcols[:, cc * NV:(cc + 1) * NV],
                         start=True, stop=True)
                ii = e.matmul(p[2][0:NV, 0:128], gcols[:, cc * NV:(cc + 1) * NV], c_trii[:],
                         start=True, stop=True)
                ii = e.matmul(p[3][0:1, 0:NV], c_trii[:, 127:128],
                         gcols[:, cc * NV:(cc + 1) * NV], start=True, stop=True)
                ii.then_inc(s, 1)
            st([("tensor", mm_la, 1)], rd=["gcols"], wr=["p1", "p2", "p3"])
            def ev_la(e, s, cc=cc):
                e.tensor_copy(out=lacols[:, cc * NV:(cc + 1) * NV],
                              in_=p[1][:, 0:NV]).then_inc(s, 1)
                e.tensor_copy(out=rowstage[0:NV, 0:128],
                              in_=p[2][0:NV, 0:128]).then_inc(s, 1)
                e.tensor_copy(out=lacrow[:], in_=p[3][0:1, 0:NV]).then_inc(s, 1)
            st([("vector", ev_la, 3)], rd=["p1", "p2", "p3"], wr=["lacols", "rowstage", "lacrow"])
            def aw_la(e, s, cc=cc, r=r):
                dma(e, s, arena_la[:, r * RT + cc * C: r * RT + (cc + 1) * C],
                    rowstage[0:NV, 0:128])
            st([("gpsimd", aw_la, 16)], rd=["rowstage"], wr=["Ala"])
            def mm_acb(e, s):
                ii = e.matmul(p[3][:, 0:NV], c_onesf[:], lacrow[:],
                         start=True, stop=True).then_inc(s, 1)
            st([("tensor", mm_acb, 1)], rd=["lacrow"], wr=["p3"])
            def ev_acb(e, s, cc=cc):
                e.tensor_tensor(out=ewbcols[:, cc * NV:(cc + 1) * NV],
                                in0=p[3][:, 0:NV], in1=lacols[:, cc * NV:(cc + 1) * NV],
                                op=ALU.subtract).then_inc(s, 1)
                e.tensor_copy(out=accols[:, cc * NV:(cc + 1) * NV],
                              in_=p[3][:, 0:NV]).then_inc(s, 1)
            st([("vector", ev_acb, 2)], rd=["p3", "lacols"], wr=["ewbcols", "accols"])
        def d1(e, s):
            e.tensor_scalar(out=nlacols[:], in0=lacols[:], scalar1=-1.0, scalar2=None,
                            op0=ALU.mult).then_inc(s, 1)
        st([("vector", d1, 1)], rd=["lacols"], wr=["nlacols"])
        def d2(e, s):
            e.activation(out=elacols[:], in_=lacols[:], func=AF.Exp).then_inc(s, 1)
            e.activation(out=lbcols[:], in_=bcols[:], func=AF.Ln).then_inc(s, 1)
            e.activation(out=ewbcols[:], in_=ewbcols[:], func=AF.Exp).then_inc(s, 1)
            e.activation(out=accols[:], in_=accols[:], func=AF.Exp).then_inc(s, 1)
        st([("scalar", d2, 4)], rd=["lacols", "bcols", "ewbcols", "accols"], wr=["elacols", "lbcols", "ewbcols", "accols"])
        def d3(e, s):
            e.tensor_tensor(out=lbcols[:], in0=lbcols[:], in1=lacols[:],
                            op=ALU.add).then_inc(s, 1)
            e.tensor_tensor(out=bwcols[:], in0=bcols[:], in1=elacols[:],
                            op=ALU.mult).then_inc(s, 1)
        st([("vector", d3, 2)], rd=["lbcols", "lacols", "bcols", "elacols"], wr=["lbcols", "bwcols"])
        for cc in range(KR):
            def mm_lbr(e, s, cc=cc):
                ii = e.matmul(p[2][0:NV, 0:128], lbcols[:, cc * NV:(cc + 1) * NV],
                         c_identf[:], start=True, stop=True).then_inc(s, 1)
            st([("tensor", mm_lbr, 1)], rd=["lbcols"], wr=["p2"])
            st([("vector", lambda e, s, cc=cc: e.tensor_copy(
                out=rowstage[0:NV, 128:256],
                in_=p[2][0:NV, 0:128]).then_inc(s, 1), 1)],
               rd=["p2"], wr=["rowstage"])
            def aw_lb(e, s, cc=cc, r=r):
                dma(e, s, arena_lb[:, r * RT + cc * C: r * RT + (cc + 1) * C],
                    rowstage[0:NV, 128:256])
            st([("gpsimd", aw_lb, 16)], rd=["rowstage"], wr=["Alb"])

        for jb in range(NJB):
            buf = jb % 2
            def ld_w(e, sem, jb=jb, buf=buf):
                dma(e, sem, wjb[:, buf * 2048:(buf + 1) * 2048],
                    WqkvT[:, jb * 128:(jb + 1) * 128]
                    .rearrange("(k p) j -> p k j", p=128))
            st([("gpsimd", ld_w, 16)], wr=[f"wjb{buf}"])
            def mm_proj(e, sem, buf=buf, r=r):
                for k in range(NDC):
                    ii = e.matmul(p[4][:],
                             wjb[:, buf * 2048 + k * 128: buf * 2048 + (k + 1) * 128],
                             xtb[:, k * RT:(k + 1) * RT],
                             start=(k == 0), stop=(k == NDC - 1))
                if r == 0:
                    for k in range(NDC):
                        ii = e.matmul(p[5][:, 0:3],
                                 wjb[:, buf * 2048 + k * 128: buf * 2048 + (k + 1) * 128],
                                 xtail[:, k * 3:(k + 1) * 3],
                                 start=(k == 0), stop=(k == NDC - 1))
                ii.then_inc(sem, 1)
            st([("tensor", mm_proj, 1)], rd=[f"wjb{buf}", "xtb", "xtail"], wr=["p4"] + (["p5"] if r == 0 else []))
            if r == 0:
                st([("vector", lambda e, s, jb=jb: e.tensor_copy(
                    out=tails[:, jb * 3:(jb + 1) * 3],
                    in_=p[5][:, 0:3]).then_inc(s, 1), 1)],
                   rd=["p5"], wr=[f"tl{jb}"])
            def conv1(e, s, jb=jb):
                e.tensor_copy(out=cpre[:, 0:3],
                              in_=tails[:, jb * 3:(jb + 1) * 3]).then_inc(s, 1)
                e.tensor_copy(out=cpre[:, 3:515], in_=p[4][:]).then_inc(s, 1)
            st([("vector", conv1, 2)], rd=[f"tl{jb}", "p4"], wr=["cpre"])
            def conv2a(e, s, jb=jb):
                e.tensor_scalar(out=cacc[jb % 2][:], in0=cpre[:, 0:512],
                                scalar1=convw_sb[:, jb * 4: jb * 4 + 1], scalar2=None,
                                op0=ALU.mult).then_inc(s, 1)
            st([("vector", conv2a, 1)], rd=["cpre"], wr=[f"cacc{jb % 2}"])
            for tap in (1, 2, 3):
                def conv2b(e, s, jb=jb, tap=tap):
                    e.scalar_tensor_tensor(
                        out=cacc[jb % 2][:], in0=cpre[:, tap:tap + 512],
                        scalar=convw_sb[:, jb * 4 + tap: jb * 4 + tap + 1],
                        in1=cacc[jb % 2][:], op0=ALU.mult, op1=ALU.add).then_inc(s, 1)
                st([("vector", conv2b, 1)], rd=["cpre", f"cacc{jb % 2}"], wr=[f"cacc{jb % 2}"])
            def tails_upd(e, s, jb=jb):
                e.tensor_copy(out=tails[:, jb * 3:(jb + 1) * 3],
                              in_=cpre[:, 512:515]).then_inc(s, 1)
            st([("vector", tails_upd, 1)], rd=["cpre"], wr=[f"tl{jb}"])
            def conv3(e, s, jb=jb):
                e.activation(out=qkv[:, jb * RT:(jb + 1) * RT],
                             in_=cacc[jb % 2][:], func=AF.Silu).then_inc(s, 1)
            st([("scalar", conv3, 1)], rd=[f"cacc{jb % 2}"], wr=[f"qk{jb}"])

        rsq2 = [rsqrow, laro2[0]]
        rsqn = ["rsqrow", "laro0"]
        sqt2 = [cacc[0], cacc[1]]
        sqtn = ["cacc0", "cacc1"]
        def emit_l2(jb, stq):
            st = lambda emits, rd=(), wr=(): stq.append((emits, rd, wr))
            jp = jb % 2
            def sq(e, s, jb=jb, jp=jp):
                e.tensor_tensor(out=sqt2[jp][:], in0=qkv[:, jb * RT:(jb + 1) * RT],
                                in1=qkv[:, jb * RT:(jb + 1) * RT],
                                op=ALU.mult).then_inc(s, 1)
            st([("vector", sq, 1)], rd=[f"qk{jb}"], wr=[sqtn[jp]])
            def mm_ss(e, s, jp=jp):
                e.matmul(p[5 - jp][0:1, 0:512], c_trii[:, 127:128], sqt2[jp][:],
                         start=True, stop=True).then_inc(s, 1)
            st([("tensor", mm_ss, 1)], rd=[sqtn[jp]], wr=[f"p{5 - jp}"])
            def rsq1(e, s, jp=jp):
                e.activation(out=rsq2[jp][0:1, :], in_=p[5 - jp][0:1, 0:512],
                             func=AF.Sqrt, bias=1e-6).then_inc(s, 1)
            st([("scalar", rsq1, 1)], rd=[f"p{5 - jp}"], wr=[rsqn[jp]])
            def rsq2f(e, s, jp=jp):
                e.reciprocal(out=rsq2[jp][0:1, :], in_=rsq2[jp][0:1, :]).then_inc(s, 1)
            st([("vector", rsq2f, 1)], rd=[rsqn[jp]], wr=[rsqn[jp]])
            def mm_rb(e, s, jp=jp):
                e.matmul(p[5 - jp][:, 0:512], c_onesf[:], rsq2[jp][0:1, :],
                         start=True, stop=True).then_inc(s, 1)
            st([("tensor", mm_rb, 1)], rd=[rsqn[jp]], wr=[f"p{5 - jp}"])
            def l2m(e, s, jb=jb, jp=jp):
                e.tensor_tensor(out=qkv[:, jb * RT:(jb + 1) * RT],
                                in0=qkv[:, jb * RT:(jb + 1) * RT],
                                in1=p[5 - jp][:, 0:512], op=ALU.mult).then_inc(s, 1)
            st([("vector", l2m, 1)], rd=[f"p{5 - jp}", f"qk{jb}"], wr=[f"qk{jb}"])
        for jb0 in range(0, 32, 2):
            la_, lb_ = [], []
            emit_l2(jb0, la_)
            emit_l2(jb0 + 1, lb_)
            for i in range(len(la_)):
                bld.step(la_[i][0], rd=la_[i][1], wr=la_[i][2])
                bld.step(lb_[i][0], rd=lb_[i][1], wr=lb_[i][2])

        if ("qkv_r%d" % r) in taps:
            st([("gpsimd", lambda e, s, r=r: dma(e, s, taps["qkv_r%d" % r], qkv[:]), 16)], rd=[f"qk{j}" for j in range(NJB)])
        if ("gla_r%d" % r) in taps:
            def tap_gla(e, s, r=r):
                dma(e, s, taps["gla_r%d" % r][0:128, :], gcols[:])
                dma(e, s, taps["gla_r%d" % r][128:256, :], lacols[:])
                dma(e, s, taps["gla_r%d" % r][256:384, :], bcols[:])
                dma(e, s, taps["gla_r%d" % r][384:512, :], ewbcols[:])
            st([("gpsimd", tap_gla, 64)], rd=["gcols", "lacols", "bcols", "ewbcols"])
        if STAGE == "A":
            continue

        def emit_head(h, stq):
            st = lambda emits, rd=(), wr=(): stq.append((emits, rd, wr))
            par = h % 2
            w_eT, w_eN, w_An, w_X, w_Xt = (w_eT2[par], w_eN2[par], w_An2[par],
                                           w_X2[par], w_Xt2[par])
            w_Z, w_PmT, w_QwT, w_Kr, w_Vr = (w_Z2[par], w_PmT2[par], w_QwT2[par],
                                             w_Kr2[par], w_Vr2[par])
            w_RV, w_Wb, w_UG, w_McT = w_RV2[par], w_Wb2[par], w_UG2[par], w_McT2[par]
            w_Nc, w_H3, w_Ob, w_H5 = w_Nc2[par], w_H32[par], w_Ob2[par], w_H52[par]
            kkS, kqS, elarow = kkS2[par], kqS2[par], elarow2[par]
            P = lambda n: n + str(par)
            pArg = p[2] if par == 0 else p[6]
            nArg = "p2" if par == 0 else "p6"
            def ld_rows(e, s, r=r, h=h):
                dma(e, s, laro2[par][0:1, :], arena_la[h:h + 1, r * RT:(r + 1) * RT])
                dma(e, s, lbro2[par][0:1, :], arena_lb[h:h + 1, r * RT:(r + 1) * RT])
            st([("sync", ld_rows, 32)], rd=["Ala", "Alb"], wr=[P("laro"), P("lbro")])
            kh = h // 2
            qT = qkv[:, kh * RT:(kh + 1) * RT]
            kT = qkv[:, (16 + kh) * RT:(16 + kh + 1) * RT]
            vT = qkv[:, (32 + h) * RT:(32 + h + 1) * RT]
            hc0 = h * NCH + r * KR

            def mm_kq(e, s, qT=qT, kT=kT):
                for cc in range(KR):
                    ii = e.matmul(p[0][:, cc * C:(cc + 1) * C], kT[:, cc * C:(cc + 1) * C],
                                  qT[:, cc * C:(cc + 1) * C], start=True, stop=True)
                    ii = e.matmul(p[1][:, cc * C:(cc + 1) * C], kT[:, cc * C:(cc + 1) * C],
                                  kT[:, cc * C:(cc + 1) * C], start=True, stop=True)
                ii.then_inc(s, 1)
            st([("tensor", mm_kq, 1)], rd=[f"qk{kh}", f"qk{16 + kh}"], wr=["p0", "p1"])
            def ev_kq(e, s, kqS=kqS, kkS=kkS):
                e.copy(out=kqS[:], in_=p[0][:]).then_inc(s, 1)
                e.copy(out=kkS[:], in_=p[1][:]).then_inc(s, 1)
            st([("scalar", ev_kq, 2)], rd=["p0", "p1"], wr=[P("kq"), P("kk")])

            def mm_argT(e, s, h=h, pArg=pArg, par=par):
                ii = e.matmul(pArg[:], c_onesf[:], lbro2[par][0:1, :],
                              start=True, stop=False)
                ii = e.matmul(pArg[:], c_identf[:], c_mnegT[:], start=False, stop=True)
                ii.then_inc(s, 1)
            st([("tensor", mm_argT, 1)], rd=[P("lbro")], wr=[nArg])
            def act_eT(e, s, h=h, w_eT=w_eT, pArg=pArg):
                for cc in range(KR):
                    e.activation(out=w_eT[:, cc * C:(cc + 1) * C],
                                 in_=pArg[:, cc * C:(cc + 1) * C], func=AF.Exp,
                                 bias=nlacols[:, cc * NV + h: cc * NV + h + 1]
                                 ).then_inc(s, 1)
            st([("scalar", act_eT, KR)], rd=[nArg, "nlacols"], wr=[P("eT")])
            def mm_argN(e, s, h=h, pArg=pArg, par=par):
                ii = e.matmul(pArg[:], c_monesf[:], laro2[par][0:1, :],
                              start=True, stop=False)
                ii = e.matmul(pArg[:], c_identf[:], c_mnegN[:], start=False, stop=True)
                ii.then_inc(s, 1)
            st([("tensor", mm_argN, 1)], rd=[P("laro")], wr=[nArg])
            def act_eN(e, s, h=h, w_eN=w_eN, pArg=pArg):
                for cc in range(KR):
                    e.activation(out=w_eN[:, cc * C:(cc + 1) * C],
                                 in_=pArg[:, cc * C:(cc + 1) * C], func=AF.Exp,
                                 bias=lbcols[:, cc * NV + h: cc * NV + h + 1]
                                 ).then_inc(s, 1)
            st([("scalar", act_eN, KR)], rd=[nArg, "lbcols"], wr=[P("eN")])
            def tt_mb(e, s, w_eT=w_eT, w_eN=w_eN, kkS=kkS):
                e.tensor_tensor(out=w_eT[:], in0=w_eT[:], in1=kkS[:],
                                op=ALU.mult).then_inc(s, 1)
                e.tensor_tensor(out=w_eN[:], in0=w_eN[:], in1=kkS[:],
                                op=ALU.mult).then_inc(s, 1)
            st([("vector", tt_mb, 2)], rd=[P("eT"), P("eN"), P("kk")], wr=[P("eT"), P("eN")])
            def tt_axt(e, s, w_An=w_An, w_X=w_X, w_Xt=w_Xt, w_eT=w_eT, w_eN=w_eN):
                e.tensor_tensor(out=w_An[:], in0=c_identb4[:], in1=w_eN[:],
                                op=ALU.add).then_inc(s, 1)
                e.tensor_tensor(out=w_X[:], in0=c_identb4[:], in1=w_eT[:],
                                op=ALU.subtract).then_inc(s, 1)
                e.tensor_tensor(out=w_Xt[:], in0=c_identb4[:], in1=w_eN[:],
                                op=ALU.subtract).then_inc(s, 1)
            st([("vector", tt_axt, 3)], rd=[P("eT"), P("eN")], wr=[P("An"), P("X"), P("Xt")])
            def mm_argP(e, s, h=h, pArg=pArg, par=par):
                ii = e.matmul(pArg[:], c_onesf[:], laro2[par][0:1, :],
                              start=True, stop=False)
                ii = e.matmul(pArg[:], c_identf[:], c_mnegI[:], start=False, stop=True)
                ii.then_inc(s, 1)
            st([("tensor", mm_argP, 1)], rd=[P("laro")], wr=[nArg])
            def act_eP(e, s, h=h, w_eT=w_eT, pArg=pArg):
                for cc in range(KR):
                    e.activation(out=w_eT[:, cc * C:(cc + 1) * C],
                                 in_=pArg[:, cc * C:(cc + 1) * C], func=AF.Exp,
                                 bias=nlacols[:, cc * NV + h: cc * NV + h + 1]
                                 ).then_inc(s, 1)
            st([("scalar", act_eP, KR)], rd=[nArg, "nlacols"], wr=[P("eT")])
            def tt_pm(e, s, w_PmT=w_PmT, w_eT=w_eT, kqS=kqS):
                e.tensor_tensor(out=w_PmT[:], in0=w_eT[:], in1=kqS[:],
                                op=ALU.mult).then_inc(s, 1)
            st([("vector", tt_pm, 1)], rd=[P("eT"), P("kq")], wr=[P("PmT")])
            def act_elar(e, s, h=h, elarow=elarow, par=par):
                e.activation(out=elarow[:], in_=laro2[par][0:1, :],
                             func=AF.Exp).then_inc(s, 1)
            st([("scalar", act_elar, 1)], rd=[P("laro")], wr=[P("elar")])
            def mm_elab(e, s, elarow=elarow, pArg=pArg):
                e.matmul(pArg[:], c_onesf[:], elarow[:],
                         start=True, stop=True).then_inc(s, 1)
            st([("tensor", mm_elab, 1)], rd=[P("elar")], wr=[nArg])
            def tt_qw(e, s, qT=qT, w_QwT=w_QwT, pArg=pArg):
                e.tensor_tensor(out=w_QwT[:], in0=qT, in1=pArg[:],
                                op=ALU.mult).then_inc(s, 1)
            st([("vector", tt_qw, 1)], rd=[f"qk{kh}", nArg], wr=[P("QwT")])

            for it in range(NEWTON):
                def mm_n1(e, s, w_An=w_An, w_X=w_X):
                    for cc in range(KR):
                        sl = slice(cc * C, (cc + 1) * C)
                        ii = e.matmul(p[3][:, sl], w_An[:, sl], w_X[:, sl],
                                      start=True, stop=True)
                    ii.then_inc(s, 1)
                st([("tensor", mm_n1, 1)], rd=[P("An"), P("X")], wr=["p3"])
                def tt_z(e, s, w_Z=w_Z):
                    e.tensor_tensor(out=w_Z[:], in0=c_twoI4[:], in1=p[3][:],
                                    op=ALU.subtract).then_inc(s, 1)
                st([("vector", tt_z, 1)], rd=["p3"], wr=[P("Z")])
                def mm_n23(e, s, w_Xt=w_Xt, w_Z=w_Z):
                    for cc in range(KR):
                        sl = slice(cc * C, (cc + 1) * C)
                        ii = e.matmul(p[4][:, sl], w_Xt[:, sl], w_Z[:, sl],
                                      start=True, stop=True)
                        ii = e.matmul(p[5][:, sl], w_Z[:, sl], w_Xt[:, sl],
                                      start=True, stop=True)
                    ii.then_inc(s, 1)
                st([("tensor", mm_n23, 1)], rd=[P("Xt"), P("Z")], wr=["p4", "p5"])
                def ev_xxt(e, s, w_X=w_X, w_Xt=w_Xt):
                    e.copy(out=w_X[:], in_=p[4][:]).then_inc(s, 1)
                    e.copy(out=w_Xt[:], in_=p[5][:]).then_inc(s, 1)
                st([("scalar", ev_xxt, 2)], rd=["p4", "p5"], wr=[P("X"), P("Xt")])

            def mm_tr(e, s, kT=kT, vT=vT):
                for cc in range(KR):
                    ii = e.transpose(pb[:, cc * C:(cc + 1) * C],
                                     kT[:, cc * C:(cc + 1) * C], c_identb[:])
                    ii = e.transpose(pb[:, 512 + cc * C: 512 + (cc + 1) * C],
                                     vT[:, cc * C:(cc + 1) * C], c_identb[:])
                ii.then_inc(s, 1)
            st([("tensor", mm_tr, 1)], rd=[f"qk{16 + kh}", f"qk{32 + h}"], wr=["pb"])
            def ev_tr(e, s, w_Kr=w_Kr, w_Vr=w_Vr):
                e.copy(out=w_Kr[:], in_=pb[:, 0:512]).then_inc(s, 1)
                e.copy(out=w_Vr[:], in_=pb[:, 512:1024]).then_inc(s, 1)
            st([("scalar", ev_tr, 2)], rd=["pb"], wr=[P("Kr"), P("Vr")])
            def ts_rv(e, s, h=h, w_RV=w_RV, w_Wb=w_Wb, w_Kr=w_Kr, w_Vr=w_Vr):
                for cc in range(KR):
                    e.tensor_scalar(out=w_RV[:, cc * 256: cc * 256 + 128],
                                    in0=w_Vr[:, cc * C:(cc + 1) * C],
                                    scalar1=bcols[:, cc * NV + h: cc * NV + h + 1],
                                    scalar2=None, op0=ALU.mult).then_inc(s, 1)
                    e.tensor_scalar(out=w_RV[:, cc * 256 + 128: cc * 256 + 256],
                                    in0=w_Kr[:, cc * C:(cc + 1) * C],
                                    scalar1=bwcols[:, cc * NV + h: cc * NV + h + 1],
                                    scalar2=None, op0=ALU.mult).then_inc(s, 1)
                    e.tensor_scalar(out=w_Wb[:, cc * C:(cc + 1) * C],
                                    in0=w_Kr[:, cc * C:(cc + 1) * C],
                                    scalar1=ewbcols[:, cc * NV + h: cc * NV + h + 1],
                                    scalar2=None, op0=ALU.mult).then_inc(s, 1)
            st([("vector", ts_rv, 3 * KR)],
               rd=[P("Kr"), P("Vr"), "bcols", "bwcols", "ewbcols"], wr=[P("RV"), P("Wb")])
            def mm_ug(e, s, w_X=w_X, w_RV=w_RV):
                for cc in range(KR):
                    pd = p[0] if cc < 2 else p[1]
                    off = (cc % 2) * 256
                    ii = e.matmul(pd[:, off:off + 256], w_X[:, cc * C:(cc + 1) * C],
                                  w_RV[:, cc * 256:(cc + 1) * 256], start=True, stop=True)
                ii.then_inc(s, 1)
            st([("tensor", mm_ug, 1)], rd=[P("X"), P("RV")], wr=["p0", "p1"])
            def ev_ug(e, s, w_UG=w_UG):
                e.tensor_copy(out=w_UG[:, 0:512], in_=p[0][:]).then_inc(s, 1)
                e.tensor_copy(out=w_UG[:, 512:1024], in_=p[1][:]).then_inc(s, 1)
            st([("vector", ev_ug, 2)], rd=["p0", "p1"], wr=[P("UG")])
            def ugU(cc, w_UG=w_UG):
                return w_UG[:, cc * 256: cc * 256 + 128]
            def ugG(cc, w_UG=w_UG):
                return w_UG[:, cc * 256 + 128: cc * 256 + 256]
            def mm_mc(e, s, ugU=ugU, ugG=ugG, w_Wb=w_Wb):
                for cc in range(KR):
                    sl = slice(cc * C, (cc + 1) * C)
                    ii = e.matmul(p[0][:, sl], ugG(cc), w_Wb[:, sl], start=True, stop=True)
                    ii = e.matmul(p[1][:, sl], w_Wb[:, sl], ugU(cc), start=True, stop=True)
                ii.then_inc(s, 1)
            st([("tensor", mm_mc, 1)], rd=[P("UG"), P("Wb")], wr=["p0", "p1"])
            def ev_mc(e, s, h=h, w_McT=w_McT, w_Nc=w_Nc):
                for cc in range(KR):
                    sl = slice(cc * C, (cc + 1) * C)
                    e.scalar_tensor_tensor(
                        out=w_McT[:, sl], in0=c_identf[:],
                        scalar=accols[:, cc * NV + h: cc * NV + h + 1],
                        in1=p[0][:, sl], op0=ALU.mult, op1=ALU.subtract).then_inc(s, 1)
                e.tensor_copy(out=w_Nc[:], in_=p[1][:]).then_inc(s, 1)
            st([("vector", ev_mc, KR + 1)], rd=["p0", "p1", "accols"], wr=[P("McT"), P("Nc")])
            def mm_h3(e, s, ugG=ugG, w_PmT=w_PmT):
                for cc in range(KR):
                    sl = slice(cc * C, (cc + 1) * C)
                    ii = e.matmul(p[0][:, sl], ugG(cc), w_PmT[:, sl], start=True, stop=True)
                ii.then_inc(s, 1)
            st([("tensor", mm_h3, 1)], rd=[P("UG"), P("PmT")], wr=["p0"])
            def ev_h3(e, s, w_QwT=w_QwT, w_H3=w_H3):
                e.tensor_tensor(out=w_H3[:], in0=w_QwT[:], in1=p[0][:],
                                op=ALU.subtract).then_inc(s, 1)
            st([("vector", ev_h3, 1)], rd=[P("QwT"), "p0"], wr=[P("H3")])

            for cc in range(KR):
                sl = slice(cc * C, (cc + 1) * C)
                hsl = slice(h * 128, (h + 1) * 128)
                def mm_sw(e, s, cc=cc, sl=sl, hsl=hsl, ugU=ugU, w_PmT=w_PmT,
                          w_H3=w_H3, w_McT=w_McT):
                    ii = e.matmul(p[1][:, sl], Pi_b[:, hsl], w_H3[:, sl],
                                  start=True, stop=True)
                    ii = e.matmul(p[0][:, sl], ugU(cc), w_PmT[:, sl],
                                  start=True, stop=False)
                    ii = e.matmul(p[0][:, sl], Qs_b[:, hsl], w_H3[:, sl],
                                  start=False, stop=True)
                    ii = e.matmul(p[2][:, 0:128], w_McT[:, sl], Qs_b[:, hsl],
                                  start=True, stop=True)
                    ii = e.matmul(p[2][:, 128:256], w_McT[:, sl], Pi_b[:, hsl],
                                  start=True, stop=True)
                    ii.then_inc(s, 1)
                st([("tensor", mm_sw, 1)],
                   rd=[f"Pi{h}", f"Qb{h}", P("UG"), P("PmT"), P("H3"), P("McT")],
                   wr=["p0", "p1", "p2"])
                def ev_sw(e, s, cc=cc, sl=sl, hsl=hsl, w_H5=w_H5, w_Ob=w_Ob, w_Nc=w_Nc):
                    e.tensor_tensor(out=Qs_f[:, hsl], in0=w_Nc[:, sl],
                                    in1=p[2][:, 0:128], op=ALU.add).then_inc(s, 1)
                    e.tensor_copy(out=Pi_b[:, hsl], in_=p[2][:, 128:256]).then_inc(s, 1)
                st([("vector", ev_sw, 2)], rd=["p2", P("Nc")], wr=[f"Qs{h}", f"Pi{h}"])
                def ev_sw_s(e, s, cc=cc, sl=sl, w_H5=w_H5, w_Ob=w_Ob):
                    e.copy(out=w_H5[:, sl], in_=p[1][:, sl]).then_inc(s, 1)
                    e.copy(out=w_Ob[:, sl], in_=p[0][:, sl]).then_inc(s, 1)
                st([("scalar", ev_sw_s, 2)], rd=["p0", "p1"], wr=[P("H5"), P("Ob")])
                def ev_sw2(e, s, hsl=hsl):
                    e.copy(out=Qs_b[:, hsl], in_=Qs_f[:, hsl]).then_inc(s, 1)
                st([("scalar", ev_sw2, 1)], rd=[f"Qs{h}"], wr=[f"Qb{h}"])
            def sp_w(e, s, hc0=hc0, w_Ob=w_Ob, w_H5=w_H5):
                dma(e, s, sp_ob[hc0 * 128:(hc0 + KR) * 128, :]
                    .rearrange("(cc p) t -> p cc t", p=128), w_Ob[:])
                dma(e, s, sp_h5[hc0 * 128:(hc0 + KR) * 128, :]
                    .rearrange("(cc p) t -> p cc t", p=128), w_H5[:])
            st([("gpsimd", sp_w, 32)], rd=[P("Ob"), P("H5")], wr=["spob", "sph5"])
        RUNS = ([1, 2, 2, 2, 1, 1, 2, 1, 1, 1, 1] + [4] * NEWTON
                + [2, 1, 2, 2, 2] + [4] * KR + [1])
        for h0 in range(0, NV, 2):
            qa, qb = [], []
            emit_head(h0, qa)
            emit_head(h0 + 1, qb)
            assert len(qa) == sum(RUNS), (len(qa), sum(RUNS))
            pos = 0
            for rl in RUNS:
                for q in (qa, qb):
                    for i in range(pos, pos + rl):
                        bld.step(q[i][0], rd=q[i][1], wr=q[i][2])
                pos += rl


    if STAGE == "A":
        bld.barrier()
        st([("gpsimd", lambda e, s: dma(e, s, outT[0:128, 0:RT], xtb[:, 0:RT]), 16)])
        return _finish(nc, bld)

    if STAGE == "B":
        def tap_b(e, s):
            if "ob" in taps:
                dma(e, s, taps["ob"], sp_ob)
            if "h5" in taps:
                dma(e, s, taps["h5"], sp_h5)
            if "qsf" in taps:
                dma(e, s, taps["qsf"], Qs_f[:])
            if "pib" in taps:
                dma(e, s, taps["pib"], Pi_b[:])
        bld.barrier()
        st([("gpsimd", tap_b, 64)])
        st([("gpsimd", lambda e, s: dma(e, s, outT[0:128, 0:RT], xtb[:, 0:RT]), 16)])
        return _finish(nc, bld)

    for h in range(NV):
        hsl = slice(h * 128, (h + 1) * 128)
        def mm_mt(e, s, hsl=hsl):
            ii = e.transpose(pb[:, 0:128], Pi_b[:, hsl], c_identb[:])
            ii.then_inc(s, 1)
        st([("tensor", mm_mt, 1)], rd=[f"Pi{h}"], wr=["pb"])
        wOb = w_Ob2[h % 2]
        def ev_mt(e, s, hsl=hsl, wOb=wOb):
            e.tensor_copy(out=wOb[:, 0:128], in_=pb[:, 0:128]).then_inc(s, 1)
            e.tensor_copy(out=wOb[:, 128:256], in_=Qs_f[:, hsl]).then_inc(s, 1)
        st([("vector", ev_mt, 2)], rd=["pb", f"Qs{h}"], wr=[f"Ob{h % 2}"])
        def st_mt(e, s, h=h, wOb=wOb):
            dma(e, s, cc_in[h * 128:(h + 1) * 128, :], wOb[:, 0:128])
            dma(e, s, cc_in[(NV + h) * 128:(NV + h + 1) * 128, :], wOb[:, 128:256])
        st([("gpsimd", st_mt, 32)], rd=[f"Ob{h % 2}"], wr=["ccin"])
    bld.barrier()
    if SIM1:
        def coll(e, s):
            for j in range(NCORE):
                dma(e, s, cc_out[j * 2 * NV * 128:(j + 1) * 2 * NV * 128, :], cc_in)
        st([("gpsimd", coll, 16 * NCORE)], rd=["ccin"], wr=["ccout"])
    else:
        def coll(e, s):
            e.collective_compute("AllGather", ALU.bypass,
                                 replica_groups=[list(range(NCORE))],
                                 ins=[cc_in.opt()], outs=[cc_out.opt()]).then_inc(s)
        st([("gpsimd", coll, 1, "c")], rd=["ccin"], wr=["ccout"])
    for g in reversed(ab_guards):
        g.__exit__(None, None, None)
    Sin = sb("Sin", [128, NV * 128], BF16)
    gat = sb("gat", [128, NV * 512], BF16)
    xres = sb("xres", [128, 512], F32)
    sums = sb("sums", [1, 512], F32)
    sumss = sb("sumss", [1, 512], F32)
    rstd = sb("rstd", [1, 512], F32)
    obt = sb("obt", [128, 512], BF16)
    h5t = sb("h5t", [128, 512], BF16)
    zsc2 = [sb("zscA", [128, 512], BF16), sb("zscB", [128, 512], BF16)]
    sq2 = sb("sq2", [128, 512], BF16)
    x1t = sb("x1t", [128, 512], F32)
    csq = sb("csq", [128, 512], F32)
    routb = sb("routb", [128, NEXP * NDC], BF16)
    wtok = sb("wtok", [128, 16 * 8], F32)
    wtokb = sb("wtokb", [128, 16 * 8], BF16)
    wrow = sb("wrow", [1, TPC], BF16)
    lg8 = sb("lg8", [128, 8], F32)
    pr8 = sb("pr8", [128, 8], F32)
    m18 = sb("m18", [128, 1], F32)
    m28 = sb("m28", [128, 1], F32)
    ssum = sb("ssum", [128, 1], F32)
    sug = sb("sug", [128, 512], F32)
    fot = sb("fot", [128, 512], F32)
    pc_guards = []
    def sbc(n, s, d):
        g = nc.sbuf_tensor(n, s, d)
        ap = g.__enter__()
        pc_guards.append(g)
        return ap
    MtJ = sbc("MtJ", [128, NV * 128], BF16)
    NJ = sbc("NJ", [128, NV * 128], BF16)
    Sacc = sbc("Sacc", [128, NV * 128], BF16)

    zq, pq = [], []
    def _mkq(q):
        return lambda emits, rd=(), wr=(): q.append((emits, rd, wr))
    stq_append = _mkq(zq)
    for tt in range(NR):
        def ld_xc0(e, s, tt=tt):
            for k in range(NDC):
                dma(e, s, xtb[:, k * RT:(k + 1) * RT],
                    xT[k * 128:(k + 1) * 128, 3 + tt * RT: 3 + (tt + 1) * RT])
        stq_append([("gpsimd", ld_xc0, 16 * NDC)], wr=["xtb"])
        for vc in range(NV):
            zbuf = vc % 2
            def ld_wz(e, s, vc=vc, zbuf=zbuf):
                dma(e, s, wjb[:, zbuf * 2048:(zbuf + 1) * 2048],
                    WzT[:, vc * 128:(vc + 1) * 128]
                    .rearrange("(k p) j -> p k j", p=128))
            stq_append([("gpsimd", ld_wz, 16)], wr=[f"wjb{zbuf}"])
            def mm_z(e, s, zbuf=zbuf):
                for k in range(NDC):
                    ii = e.matmul(p[4 + zbuf][:],
                                  wjb[:, zbuf * 2048 + k * 128: zbuf * 2048 + (k + 1) * 128],
                                  xtb[:, k * RT:(k + 1) * RT],
                                  start=(k == 0), stop=(k == NDC - 1))
                ii.then_inc(s, 1)
            stq_append([("tensor", mm_z, 1)], rd=[f"wjb{zbuf}", "xtb"], wr=[f"p{4 + zbuf}"])
            def ev_z(e, s, zbuf=zbuf):
                e.activation(out=zsc2[zbuf][:], in_=p[4 + zbuf][:],
                             func=AF.Silu).then_inc(s, 1)
            stq_append([("scalar", ev_z, 1)], rd=[f"p{4 + zbuf}"], wr=[f"zsl{zbuf}"])
            def sp_zw(e, s, vc=vc, tt=tt, zbuf=zbuf):
                dma(e, s, sp_z[vc * 128:(vc + 1) * 128, tt * RT:(tt + 1) * RT],
                    zsc2[zbuf][:])
            stq_append([("gpsimd", sp_zw, 16)], rd=[f"zsl{zbuf}"], wr=["spz"])
    stq_append = _mkq(pq)
    def init_sin(e, s):
        e.memset(Sacc[:], 0.0).then_inc(s, 1)
        e.memset(Sin[:], 0.0).then_inc(s, 1)
    stq_append([("vector", init_sin, 2)], wr=["SaccA", "SinA"])
    for j in range(NCORE - 1):
        def ld_mj(e, s, j=j):
            dma(e, s, MtJ[:], cc_out[j * 8192:(j * 8192 + 4096), :]
                .rearrange("(h p) c -> p h c", p=128))
            dma(e, s, NJ[:], cc_out[(j * 8192 + 4096):(j + 1) * 8192, :]
                .rearrange("(h p) c -> p h c", p=128))
        stq_append([("gpsimd", ld_mj, 32)], rd=["ccout"] + [f"Sc{h}" for h in range(NV)], wr=["MtJ", "NJ"])
        for h in range(NV):
            hsl = slice(h * 128, (h + 1) * 128)
            def mm_pc(e, s, hsl=hsl):
                ii = e.matmul(p[2][:, 0:128], MtJ[:, hsl], Sacc[:, hsl],
                         start=True, stop=True).then_inc(s, 1)
            stq_append([("tensor", mm_pc, 1)], rd=["MtJ", f"Sc{h}", "SaccA"], wr=["p2"])
            def ev_pc(e, s, hsl=hsl):
                e.tensor_tensor(out=Sacc[:, hsl], in0=NJ[:, hsl], in1=p[2][:, 0:128],
                                op=ALU.add).then_inc(s, 1)
            stq_append([("vector", ev_pc, 1)], rd=["NJ", "p2", "SaccA"], wr=[f"Sc{h}"])
            def ev_pc2(e, s, hsl=hsl, j=j):
                e.scalar_tensor_tensor(out=Sin[:, hsl], in0=Sacc[:, hsl],
                                       scalar=c_onehot[:, j + 1: j + 2],
                                       in1=Sin[:, hsl], op0=ALU.mult,
                                       op1=ALU.add).then_inc(s, 1)
            stq_append([("vector", ev_pc2, 1)], rd=[f"Sc{h}", "SinA"], wr=[f"Si{h}"])

    zi = pi = 0
    while zi < len(zq) or pi < len(pq):
        if zi < len(zq):
            bld.step(zq[zi][0], rd=zq[zi][1], wr=zq[zi][2])
            zi += 1
        for _ in range(2):
            if pi < len(pq):
                bld.step(pq[pi][0], rd=pq[pi][1], wr=pq[pi][2])
                pi += 1
    bld.barrier()
    for g in reversed(pc_guards):
        g.__exit__(None, None, None)
    sut2 = [sb("sutA", [128, NFB * 512], BF16), sb("sutB", [128, NFB * 512], BF16)]
    wgu_g = nc.sbuf_tensor("wguA", [128, 2 * 2048], BF16)
    wgu_g2 = nc.sbuf_tensor("wguB", [128, 2 * 2048], BF16)
    wgu2 = [wgu_g.__enter__(), wgu_g2.__enter__()]
    if "sin" in taps:
        st([("gpsimd", lambda e, s: dma(e, s, taps["sin"], Sin[:]), 16)],
           rd=[f"Si{h}" for h in range(NV)])

    for tt in range(NR):
        for vc in range(NV):
            def ld_oh(e, s, vc=vc, tt=tt):
                base = vc * NCH + tt * KR
                dma(e, s, obt[:], sp_ob[base * 128:(base + KR) * 128, :]
                    .rearrange("(cc p) t -> p cc t", p=128))
                dma(e, s, h5t[:], sp_h5[base * 128:(base + KR) * 128, :]
                    .rearrange("(cc p) t -> p cc t", p=128))
            st([("gpsimd", ld_oh, 32)], rd=["spob", "sph5"], wr=["obt", "h5t"])
            def mm_corr(e, s, vc=vc):
                for cc in range(KR):
                    sl = slice(cc * C, (cc + 1) * C)
                    ii = e.matmul(p[3][:, sl], Sin[:, vc * 128:(vc + 1) * 128],
                             h5t[:, sl], start=True, stop=True)
                ii.then_inc(s, 1)
            st([("tensor", mm_corr, 1)], rd=[f"Si{vc}", "h5t"], wr=["p3"])
            def ev_corr(e, s, vc=vc):
                e.tensor_tensor(out=gat[:, vc * RT:(vc + 1) * RT], in0=obt[:],
                                in1=p[3][:], op=ALU.add).then_inc(s, 1)
            st([("vector", ev_corr, 1)], rd=["obt", "p3"], wr=[f"gat{vc}"])
        for vc in range(NV):
            def sqv(e, s, vc=vc):
                e.tensor_tensor(out=sq2[:], in0=gat[:, vc * RT:(vc + 1) * RT],
                                in1=gat[:, vc * RT:(vc + 1) * RT],
                                op=ALU.mult).then_inc(s, 1)
            st([("vector", sqv, 1)], rd=[f"gat{vc}"], wr=["sq2"])
            def mm_sq(e, s, vc=vc):
                ii = e.matmul(p[5][0:1, 0:512], c_onescb[:], sq2[:],
                         start=(vc == 0), stop=(vc == NV - 1)).then_inc(s, 1)
            st([("tensor", mm_sq, 1)], rd=["sq2"], wr=["p5"])
        def rstd1(e, s):
            e.activation(out=rstd[:], in_=p[5][0:1, 0:512], func=AF.Sqrt,
                         bias=1e-6, scale=1.0 / (NV * DV)).then_inc(s, 1)
        st([("scalar", rstd1, 1)], rd=["p5"], wr=["rstd"])
        st([("vector", lambda e, s: e.reciprocal(out=rstd[:],
                                                 in_=rstd[:]).then_inc(s, 1), 1)],
           rd=["rstd"], wr=["rstd"])
        def mm_rb2(e, s):
            ii = e.matmul(p[5][:, 0:512], c_onesf[:], rstd[:],
                     start=True, stop=True).then_inc(s, 1)
        st([("tensor", mm_rb2, 1)], rd=["rstd"], wr=["p5"])
        for vc in range(NV):
            zbuf = vc % 2
            def ld_zt(e, s, vc=vc, tt=tt, zbuf=zbuf):
                dma(e, s, zsc2[zbuf][:], sp_z[vc * 128:(vc + 1) * 128,
                                              tt * RT:(tt + 1) * RT])
            st([("gpsimd", ld_zt, 16)], rd=["spz"], wr=[f"zsl{zbuf}"])
            def gate2(e, s, vc=vc):
                e.scalar_tensor_tensor(out=gat[:, vc * RT:(vc + 1) * RT],
                                       in0=gat[:, vc * RT:(vc + 1) * RT],
                                       scalar=c_normw[:, vc:vc + 1],
                                       in1=p[5][:, 0:512], op0=ALU.mult,
                                       op1=ALU.mult).then_inc(s, 1)
            st([("vector", gate2, 1)], rd=[f"gat{vc}", "p5"], wr=[f"gat{vc}"])
            def gate3(e, s, vc=vc, zbuf=zbuf):
                e.tensor_tensor(out=gat[:, vc * RT:(vc + 1) * RT],
                                in0=gat[:, vc * RT:(vc + 1) * RT],
                                in1=zsc2[zbuf][:], op=ALU.mult).then_inc(s, 1)
            st([("vector", gate3, 1)], rd=[f"gat{vc}", f"zsl{zbuf}"], wr=[f"gat{vc}"])
        for dc in range(NDC):
            wob = wjb if dc % 2 == 0 else wgu2[0]
            wobn = ["wjb0", "wjb1"] if dc % 2 == 0 else ["wgu0"]
            def ld_wo(e, s, dc=dc, wob=wob):
                dma(e, s, wob[:, 0:2048], WoutT[:, dc * 128:(dc + 1) * 128]
                    .rearrange("(v p) j -> p v j", p=128)[:, 0:16, :])
                dma(e, s, wob[:, 2048:4096], WoutT[:, dc * 128:(dc + 1) * 128]
                    .rearrange("(v p) j -> p v j", p=128)[:, 16:32, :])
            st([("gpsimd", ld_wo, 32)], wr=wobn)
            def ld_xr(e, s, dc=dc, tt=tt):
                dma(e, s, xres[:], xT[dc * 128:(dc + 1) * 128,
                                      3 + tt * RT: 3 + (tt + 1) * RT])
            st([("sync", ld_xr, 16)], wr=["xres"])
            def mm_wo(e, s, wob=wob, dc=dc):
                for vc in range(NV):
                    ii = e.matmul(p[4 - dc % 2][:], wob[:, vc * 128:(vc + 1) * 128],
                             gat[:, vc * RT:(vc + 1) * RT],
                             start=(vc == 0), stop=(vc == NV - 1))
                ii.then_inc(s, 1)
            st([("tensor", mm_wo, 1)], rd=wobn + [f"gat{v}" for v in range(NV)],
               wr=[f"p{4 - dc % 2}"])
            def ev_wo(e, s, dc=dc):
                e.tensor_tensor(out=x1t[:], in0=xres[:], in1=p[4 - dc % 2][:],
                                op=ALU.add).then_inc(s, 1)
            st([("vector", ev_wo, 1)], rd=["xres", f"p{4 - dc % 2}"], wr=["x1t"])
            st([("vector", lambda e, s: e.tensor_copy(out=sq2[:],
                                                      in_=x1t[:]).then_inc(s, 1), 1)],
               rd=["x1t"], wr=["sq2"])
            def spw(e, s, dc=dc, tt=tt):
                dma(e, s, sp_x1[dc * 128:(dc + 1) * 128, tt * RT:(tt + 1) * RT], x1t[:])
            st([("gpsimd", spw, 16)], rd=["x1t"], wr=["spx1"])
            def mm_st1(e, s, dc=dc):
                ii = e.matmul(p[5][0:1, 0:512], c_onescb[:], sq2[:],
                         start=(dc == 0), stop=(dc == NDC - 1)).then_inc(s, 1)
            st([("tensor", mm_st1, 1)], rd=["sq2"], wr=["p5"])
            def sq_x(e, s):
                e.tensor_tensor(out=csq[:], in0=sq2[:], in1=sq2[:],
                                op=ALU.mult).then_inc(s, 1)
            st([("vector", sq_x, 1)], rd=["sq2"], wr=["csq"])
            def mm_st2(e, s, dc=dc):
                ii = e.matmul(p[2][0:1, 0:512], c_trii[:, 127:128], csq[:],
                         start=(dc == 0), stop=(dc == NDC - 1)).then_inc(s, 1)
            st([("tensor", mm_st2, 1)], rd=["csq"], wr=["p2"])
        def ln_f1(e, s):
            e.tensor_scalar(out=sums[:], in0=p[5][0:1, 0:512], scalar1=1.0 / D,
                            scalar2=None, op0=ALU.mult).then_inc(s, 1)
            e.tensor_scalar(out=sumss[:], in0=p[2][0:1, 0:512], scalar1=1.0 / D,
                            scalar2=None, op0=ALU.mult).then_inc(s, 1)
        st([("vector", ln_f1, 2)], rd=["p5", "p2"], wr=["sums", "sumss"])
        st([("vector", lambda e, s: e.tensor_tensor(out=rstd[:], in0=sums[:],
                                                    in1=sums[:],
                                                    op=ALU.mult).then_inc(s, 1), 1)],
           rd=["sums"], wr=["rstd"])
        st([("vector", lambda e, s: e.tensor_tensor(out=rstd[:], in0=sumss[:],
                                                    in1=rstd[:],
                                                    op=ALU.subtract).then_inc(s, 1), 1)],
           rd=["sumss", "rstd"], wr=["rstd"])
        st([("scalar", lambda e, s: e.activation(out=rstd[:], in_=rstd[:],
                                                 func=AF.Sqrt, bias=1e-5
                                                 ).then_inc(s, 1), 1)],
           rd=["rstd"], wr=["rstd"])
        st([("vector", lambda e, s: e.reciprocal(out=rstd[:],
                                                 in_=rstd[:]).then_inc(s, 1), 1)],
           rd=["rstd"], wr=["rstd"])
        def mm_lb2(e, s):
            ii = e.matmul(p[4][:, 0:512], c_onesf[:], sums[:], start=True, stop=True)
            ii = e.matmul(p[3][:, 0:512], c_onesf[:], rstd[:], start=True, stop=True)
            ii.then_inc(s, 1)
        st([("tensor", mm_lb2, 1)], rd=["sums", "rstd"], wr=["p4", "p3"])
        for dc in range(NDC):
            def ld_x1(e, s, dc=dc, tt=tt):
                dma(e, s, xres[:], sp_x1[dc * 128:(dc + 1) * 128,
                                         tt * RT:(tt + 1) * RT])
            st([("sync", ld_x1, 16)], rd=["spx1"], wr=["xres"])
            def nx1(e, s):
                e.tensor_tensor(out=x1t[:], in0=xres[:], in1=p[4][:, 0:512],
                                op=ALU.subtract).then_inc(s, 1)
            st([("vector", nx1, 1)], rd=["xres", "p4"], wr=["x1t"])
            def nx2(e, s):
                e.tensor_tensor(out=x1t[:], in0=x1t[:], in1=p[3][:, 0:512],
                                op=ALU.mult).then_inc(s, 1)
            st([("vector", nx2, 1)], rd=["x1t", "p3"], wr=["x1t"])
            def spw2(e, s, dc=dc, tt=tt):
                dma(e, s, sp_x1[dc * 128:(dc + 1) * 128, tt * RT:(tt + 1) * RT], x1t[:])
            st([("gpsimd", spw2, 16)], rd=["x1t"], wr=["spx1"])
            st([("vector", lambda e, s: e.tensor_copy(out=sq2[:],
                                                      in_=x1t[:]).then_inc(s, 1), 1)],
               rd=["x1t"], wr=["sq2"])
            def sq_y(e, s):
                e.tensor_tensor(out=csq[:], in0=sq2[:], in1=sq2[:],
                                op=ALU.mult).then_inc(s, 1)
            st([("vector", sq_y, 1)], rd=["sq2"], wr=["csq"])
            def mm_st3(e, s, dc=dc):
                ii = e.matmul(p[2][0:1, 0:512], c_trii[:, 127:128], csq[:],
                         start=(dc == 0), stop=(dc == NDC - 1)).then_inc(s, 1)
            st([("tensor", mm_st3, 1)], rd=["csq"], wr=["p2"])
        st([("scalar", lambda e, s: e.activation(out=rstd[:], in_=p[2][0:1, 0:512],
                                                 func=AF.Sqrt, bias=1e-6,
                                                 scale=1.0 / D).then_inc(s, 1), 1)],
           rd=["p2"], wr=["rstd"])
        st([("vector", lambda e, s: e.reciprocal(out=rstd[:],
                                                 in_=rstd[:]).then_inc(s, 1), 1)],
           rd=["rstd"], wr=["rstd"])
        def mm_rb3(e, s):
            ii = e.matmul(p[3][:, 0:512], c_onesf[:], rstd[:],
                     start=True, stop=True).then_inc(s, 1)
        st([("tensor", mm_rb3, 1)], rd=["rstd"], wr=["p3"])
        for dc in range(NDC):
            def ld_x2(e, s, dc=dc, tt=tt):
                dma(e, s, xres[:], sp_x1[dc * 128:(dc + 1) * 128,
                                         tt * RT:(tt + 1) * RT])
            st([("sync", ld_x2, 16)], rd=["spx1"], wr=["xres"])
            def hmk(e, s, dc=dc, tt=tt):
                e.scalar_tensor_tensor(
                    out=qkv[:, (dc * NR + tt) * RT:(dc * NR + tt + 1) * RT],
                    in0=xres[:], scalar=c_rmsw[:, dc:dc + 1], in1=p[3][:, 0:512],
                    op0=ALU.mult, op1=ALU.mult).then_inc(s, 1)
            st([("vector", hmk, 1)], rd=["xres", "p3"], wr=[f"qk{dc * NR + tt}"])

    def htile(dc, tt):
        return qkv[:, (dc * NR + tt) * RT:(dc * NR + tt + 1) * RT]

    if "hfull" in taps:
        st([("gpsimd", lambda e, s: dma(e, s, taps["hfull"], qkv[:]), 16)],
           rd=[f"qk{j}" for j in range(NJB)])

    st([("gpsimd", lambda e, s: dma(e, s, routb[:],
                                    routw.rearrange("(k p) e -> p k e", p=128)), 16)],
       wr=["routb"])
    for tg in range(16):
        tt, cc = tg // KR, tg % KR
        def mm_ro(e, s, tt=tt, cc=cc):
            for k in range(NDC):
                ii = e.matmul(p[0][:, 0:8], htile(k, tt)[:, cc * C:(cc + 1) * C],
                         routb[:, k * 8:(k + 1) * 8],
                         start=(k == 0), stop=(k == NDC - 1))
            ii.then_inc(s, 1)
        st([("tensor", mm_ro, 1)], rd=["routb"] + [f"qk{k * NR + tt}" for k in range(NDC)], wr=["p0"])
        def ro1(e, s):
            e.reduce_max(out=m18[:], in_=p[0][:, 0:8], axis=AX.X).then_inc(s, 1)
        st([("vector", ro1, 1)], rd=["p0"], wr=["m18"])
        def ro1b(e, s):
            e.tensor_scalar(out=m28[:], in0=m18[:], scalar1=-1.0, scalar2=None,
                            op0=ALU.mult).then_inc(s, 1)
        st([("vector", ro1b, 1)], rd=["m18"], wr=["m28"])
        def ro2(e, s):
            e.activation(out=lg8[:], in_=p[0][:, 0:8], func=AF.Exp,
                         bias=m28[:]).then_inc(s, 1)
        st([("scalar", ro2, 1)], rd=["p0", "m28"], wr=["lg8"])
        st([("vector", lambda e, s: e.reduce_sum(out=ssum[:], in_=lg8[:],
                                                 axis=AX.X).then_inc(s, 1), 1)],
           rd=["lg8"], wr=["ssum"])
        st([("vector", lambda e, s: e.reciprocal(out=ssum[:],
                                                 in_=ssum[:]).then_inc(s, 1), 1)],
           rd=["ssum"], wr=["ssum"])
        st([("vector", lambda e, s: e.tensor_scalar(
            out=pr8[:], in0=lg8[:], scalar1=ssum[:], scalar2=None,
            op0=ALU.mult).then_inc(s, 1), 1)],
           rd=["lg8", "ssum"], wr=["pr8"])
        st([("vector", lambda e, s: e.reduce_max(out=m18[:], in_=pr8[:],
                                                 axis=AX.X).then_inc(s, 1), 1)],
           rd=["pr8"], wr=["m18"])
        def ro7(e, s):
            e.tensor_scalar(out=lg8[:], in0=pr8[:], scalar1=m18[:], scalar2=-2.0,
                            op0=ALU.is_ge, op1=ALU.mult).then_inc(s, 1)
        st([("vector", ro7, 1)], rd=["pr8", "m18"], wr=["lg8"])
        st([("vector", lambda e, s: e.tensor_tensor(out=lg8[:], in0=pr8[:],
                                                    in1=lg8[:],
                                                    op=ALU.add).then_inc(s, 1), 1)],
           rd=["pr8", "lg8"], wr=["lg8"])
        st([("vector", lambda e, s: e.reduce_max(out=m28[:], in_=lg8[:],
                                                 axis=AX.X).then_inc(s, 1), 1)],
           rd=["lg8"], wr=["m28"])
        def ro10(e, s):
            e.tensor_scalar(out=lg8[:], in0=pr8[:], scalar1=m28[:], scalar2=None,
                            op0=ALU.is_ge).then_inc(s, 1)
        st([("vector", ro10, 1)], rd=["pr8", "m28"], wr=["lg8"])
        st([("vector", lambda e, s: e.tensor_tensor(out=m18[:], in0=m18[:],
                                                    in1=m28[:],
                                                    op=ALU.add).then_inc(s, 1), 1)],
           rd=["m18", "m28"], wr=["m18"])
        st([("vector", lambda e, s: e.reciprocal(out=m18[:],
                                                 in_=m18[:]).then_inc(s, 1), 1)],
           rd=["m18"], wr=["m18"])
        st([("vector", lambda e, s: e.tensor_tensor(out=lg8[:], in0=pr8[:],
                                                    in1=lg8[:],
                                                    op=ALU.mult).then_inc(s, 1), 1)],
           rd=["pr8", "lg8"], wr=["lg8"])
        def ro14(e, s, tg=tg):
            e.tensor_scalar(out=wtok[:, tg * 8:(tg + 1) * 8], in0=lg8[:],
                            scalar1=m18[:], scalar2=None,
                            op0=ALU.mult).then_inc(s, 1)
        st([("vector", ro14, 1)], rd=["lg8", "m18"], wr=["wtok"])
    st([("vector", lambda e, s: e.tensor_copy(out=wtokb[:],
                                              in_=wtok[:]).then_inc(s, 1), 1)],
       rd=["wtok"], wr=["wtokb"])
    def mm_wt(e, s):
        ii = e.transpose(pb[:, 0:128], wtokb[:], c_identb[:])
        ii.then_inc(s, 1)
    st([("tensor", mm_wt, 1)], rd=["wtokb"], wr=["pb"])
    st([("vector", lambda e, s: e.tensor_copy(out=wtokb[:],
                                              in_=pb[:, 0:128]).then_inc(s, 1), 1)],
       rd=["pb"], wr=["wtokb"])
    def wr_w(e, s):
        dma(e, s, arena_wr, wtokb[:])
    st([("gpsimd", wr_w, 16)], rd=["wtokb"], wr=["Awr"])
    if "wrow" in taps:
        def tap_wr(e, s):
            dma(e, s, taps["wrow"], arena_wr)
        st([("gpsimd", tap_wr, 16)], rd=["Awr"])

    for e_i in range(NEXP + 1):
        if e_i < NEXP:
            def ld_wrow(e, s, e_i=e_i):
                dma(e, s, wrow[0:1, :],
                    arena_wr.rearrange("(tg e) t -> e tg t", e=8)[e_i:e_i + 1, :, :])
            st([("gpsimd", ld_wrow, 16)], rd=["Awr"], wr=["wrow"])
        for fb in range(NFB):
            gbuf = (e_i * NFB + fb) % 2
            wgu = wgu2[gbuf]
            def ld_wgu(e, s, e_i=e_i, fb=fb, wgu=wgu):
                if e_i < NEXP:
                    src_g = wg[e_i, :, fb * 128:(fb + 1) * 128]
                    src_u = wu[e_i, :, fb * 128:(fb + 1) * 128]
                else:
                    src_g = swg[:, fb * 128:(fb + 1) * 128]
                    src_u = swu[:, fb * 128:(fb + 1) * 128]
                dma(e, s, wgu[:, 0:2048], src_g.rearrange("(k p) j -> p k j", p=128))
                dma(e, s, wgu[:, 2048:4096], src_u.rearrange("(k p) j -> p k j", p=128))
            st([("gpsimd", ld_wgu, 32)], wr=[f"wgu{gbuf}"])
            for tt in range(NR):
                def mm_gu(e, s, tt=tt, wgu=wgu):
                    for k in range(NDC):
                        ii = e.matmul(p[0][:], wgu[:, k * 128:(k + 1) * 128],
                                 htile(k, tt), start=(k == 0), stop=(k == NDC - 1))
                    for k in range(NDC):
                        ii = e.matmul(p[1][:], wgu[:, 2048 + k * 128: 2048 + (k + 1) * 128],
                                 htile(k, tt), start=(k == 0), stop=(k == NDC - 1))
                    ii.then_inc(s, 1)
                st([("tensor", mm_gu, 1)], rd=[f"wgu{gbuf}"] + [f"qk{k * NR + tt}" for k in range(NDC)], wr=["p0", "p1"])
                st([("scalar", lambda e, s: e.activation(
                    out=sug[:], in_=p[0][:], func=AF.Silu).then_inc(s, 1), 1)],
                   rd=["p0"], wr=["sug"])
                if e_i < NEXP:
                    def mm_wb2(e, s, tt=tt):
                        ii = e.matmul(p[2][:, 0:512], c_onesb[:],
                                 wrow[0:1, tt * RT:(tt + 1) * RT],
                                 start=True, stop=True).then_inc(s, 1)
                    st([("tensor", mm_wb2, 1)], rd=["wrow"], wr=["p2"])
                    def ev_gu2(e, s):
                        e.tensor_tensor(out=sug[:], in0=sug[:], in1=p[1][:],
                                        op=ALU.mult).then_inc(s, 1)
                    st([("vector", ev_gu2, 1)], rd=["sug", "p1"], wr=["sug"])
                    def ev_gu3(e, s):
                        e.tensor_tensor(out=sq2[:], in0=sug[:], in1=p[2][:, 0:512],
                                        op=ALU.mult).then_inc(s, 1)
                    st([("vector", ev_gu3, 1)], rd=["sug", "p2"], wr=["sq2"])
                else:
                    def ev_gu2s(e, s):
                        e.tensor_tensor(out=sq2[:], in0=sug[:], in1=p[1][:],
                                        op=ALU.mult).then_inc(s, 1)
                    st([("vector", ev_gu2s, 1)], rd=["sug", "p1"], wr=["sq2"])
                def sp_su_w(e, s, e_i=e_i, fb=fb, tt=tt):
                    dma(e, s, sp_su[(e_i * NFB + fb) * 128:(e_i * NFB + fb + 1) * 128,
                                    tt * RT:(tt + 1) * RT], sq2[:])
                st([("gpsimd", sp_su_w, 16)], rd=["sq2"], wr=["spsu"])

    bld.barrier()
    wgu_g2.__exit__(None, None, None)
    wgu_g.__exit__(None, None, None)
    wdb2 = [sb("wdbA", [128, 4096], BF16), sb("wdbB", [128, 4096], BF16)]
    pbg.__exit__(None, None, None)
    p7 = psum("p7", [128, 512], F32)
    pall = p + [p7]
    for half in range(2):
        for tt in range(NR):
            for e_i in range(NEXP + 1):
                sbuf_i = e_i % 2
                sut = sut2[sbuf_i]
                def ld_su(e, s, e_i=e_i, tt=tt, sut=sut):
                    dma(e, s, sut[:], sp_su[e_i * NFB * 128:(e_i + 1) * NFB * 128,
                                            tt * RT:(tt + 1) * RT]
                        .rearrange("(f p) t -> p f t", p=128))
                st([("gpsimd", ld_su, 16)], rd=["spsu"], wr=[f"sut{sbuf_i}"])
                for fh in range(2):
                    dbuf = (e_i * 2 + fh) % 2
                    wdb = wdb2[dbuf]
                    def ld_wd(e, s, e_i=e_i, half=half, fh=fh, wdb=wdb):
                        src_ = (wd[e_i, fh * 512:(fh + 1) * 512,
                                   half * 1024:(half + 1) * 1024] if e_i < NEXP
                                else swd[fh * 512:(fh + 1) * 512,
                                         half * 1024:(half + 1) * 1024])
                        dma(e, s, wdb[:], src_.rearrange("(f p) j -> p f j", p=128))
                    st([("gpsimd", ld_wd, 16)], wr=[f"wdb{dbuf}"])
                    def mm_dn(e, s, e_i=e_i, fh=fh, wdb=wdb, sut=sut):
                        for dc in range(8):
                            for f in range(4):
                                ii = e.matmul(pall[dc][:],
                                         wdb[:, (f * 8 + dc) * 128:(f * 8 + dc + 1) * 128],
                                         sut[:, (fh * 4 + f) * RT:(fh * 4 + f + 1) * RT],
                                         start=(e_i == 0 and fh == 0 and f == 0),
                                         stop=(e_i == NEXP and fh == 1 and f == 3))
                        ii.then_inc(s, 1)
                    st([("tensor", mm_dn, 1)], rd=[f"wdb{dbuf}", f"sut{sbuf_i}"], wr=[f"pd{i}" for i in range(8)])
            for dc in range(8):
                def ev_dn(e, s, dc=dc):
                    e.tensor_copy(out=fot[:], in_=pall[dc][:]).then_inc(s, 1)
                st([("vector", ev_dn, 1)], rd=[f"pd{dc}"], wr=["fot"])
                def st_dn(e, s, dc=dc, half=half, tt=tt):
                    dcg = half * 8 + dc
                    dma(e, s, sp_ffn[dcg * 128:(dcg + 1) * 128,
                                     tt * RT:(tt + 1) * RT], fot[:])
                st([("gpsimd", st_dn, 16)], rd=["fot"], wr=["spffn"])

    for tt in range(NR):
        for dc in range(NDC):
            def ld_f(e, s, dc=dc, tt=tt):
                dma(e, s, xres[:], sp_x1[dc * 128:(dc + 1) * 128,
                                         tt * RT:(tt + 1) * RT])
                dma(e, s, fot[:], sp_ffn[dc * 128:(dc + 1) * 128,
                                         tt * RT:(tt + 1) * RT])
            st([("sync", ld_f, 32)], rd=["spx1", "spffn"], wr=["xres", "fot"])
            def add_f(e, s):
                e.tensor_tensor(out=x1t[:], in0=xres[:], in1=fot[:],
                                op=ALU.add).then_inc(s, 1)
            st([("vector", add_f, 1)], rd=["xres", "fot"], wr=["x1t"])
            def spf(e, s, dc=dc, tt=tt):
                dma(e, s, sp_ffn[dc * 128:(dc + 1) * 128,
                                 tt * RT:(tt + 1) * RT], x1t[:])
            st([("gpsimd", spf, 16)], rd=["x1t"], wr=["spffn"])
            st([("vector", lambda e, s: e.tensor_copy(out=sq2[:],
                                                      in_=x1t[:]).then_inc(s, 1), 1)],
               rd=["x1t"], wr=["sq2"])
            def mm_sf1(e, s, dc=dc):
                ii = e.matmul(p[5][0:1, 0:512], c_onescb[:], sq2[:],
                         start=(dc == 0), stop=(dc == NDC - 1)).then_inc(s, 1)
            st([("tensor", mm_sf1, 1)], rd=["sq2"], wr=["p5"])
            def sq_f(e, s):
                e.tensor_tensor(out=csq[:], in0=sq2[:], in1=sq2[:],
                                op=ALU.mult).then_inc(s, 1)
            st([("vector", sq_f, 1)], rd=["sq2"], wr=["csq"])
            def mm_sf2(e, s, dc=dc):
                ii = e.matmul(p[2][0:1, 0:512], c_trii[:, 127:128], csq[:],
                         start=(dc == 0), stop=(dc == NDC - 1)).then_inc(s, 1)
            st([("tensor", mm_sf2, 1)], rd=["csq"], wr=["p2"])
        def ln2_f1(e, s):
            e.tensor_scalar(out=sums[:], in0=p[5][0:1, 0:512], scalar1=1.0 / D,
                            scalar2=None, op0=ALU.mult).then_inc(s, 1)
            e.tensor_scalar(out=sumss[:], in0=p[2][0:1, 0:512], scalar1=1.0 / D,
                            scalar2=None, op0=ALU.mult).then_inc(s, 1)
        st([("vector", ln2_f1, 2)], rd=["p5", "p2"], wr=["sums", "sumss"])
        st([("vector", lambda e, s: e.tensor_tensor(out=rstd[:], in0=sums[:],
                                                    in1=sums[:],
                                                    op=ALU.mult).then_inc(s, 1), 1)],
           rd=["sums"], wr=["rstd"])
        st([("vector", lambda e, s: e.tensor_tensor(out=rstd[:], in0=sumss[:],
                                                    in1=rstd[:],
                                                    op=ALU.subtract).then_inc(s, 1), 1)],
           rd=["sumss", "rstd"], wr=["rstd"])
        st([("scalar", lambda e, s: e.activation(out=rstd[:], in_=rstd[:],
                                                 func=AF.Sqrt,
                                                 bias=1e-5).then_inc(s, 1), 1)],
           rd=["rstd"], wr=["rstd"])
        st([("vector", lambda e, s: e.reciprocal(out=rstd[:],
                                                 in_=rstd[:]).then_inc(s, 1), 1)],
           rd=["rstd"], wr=["rstd"])
        def mm_lbf(e, s):
            ii = e.matmul(p[4][:, 0:512], c_onesf[:], sums[:], start=True, stop=True)
            ii = e.matmul(p[3][:, 0:512], c_onesf[:], rstd[:], start=True, stop=True)
            ii.then_inc(s, 1)
        st([("tensor", mm_lbf, 1)], rd=["sums", "rstd"], wr=["p4", "p3"])
        for dc in range(NDC):
            def ld_f2(e, s, dc=dc, tt=tt):
                dma(e, s, fot[:], sp_ffn[dc * 128:(dc + 1) * 128,
                                         tt * RT:(tt + 1) * RT])
            st([("sync", ld_f2, 16)], rd=["spffn"], wr=["fot"])
            def of1(e, s):
                e.tensor_tensor(out=x1t[:], in0=fot[:], in1=p[4][:, 0:512],
                                op=ALU.subtract).then_inc(s, 1)
            st([("vector", of1, 1)], rd=["fot", "p4"], wr=["x1t"])
            def of2(e, s):
                e.tensor_tensor(out=x1t[:], in0=x1t[:], in1=p[3][:, 0:512],
                                op=ALU.mult).then_inc(s, 1)
            st([("vector", of2, 1)], rd=["x1t", "p3"], wr=["x1t"])
            def of3(e, s, dc=dc, tt=tt):
                dma(e, s, outT[dc * 128:(dc + 1) * 128, tt * RT:(tt + 1) * RT], x1t[:])
            st([("gpsimd", of3, 16)], rd=["x1t"], wr=["outT"])

    return _finish(nc, bld)


def _finish(nc, bld):
    for si, val in sorted(bld.cnt.items()):
        if val > 0:
            bld.prog["gpsimd"].append(
                lambda e, si=si, val=val: e.wait_ge(bld.sems[si], val))

    def run_for(name):
        def _f(e):
            for fn in bld.prog[name]:
                fn(e)
        return _f
    with nc.Block() as block:
        block.sync(run_for("sync"))
        block.tensor(run_for("tensor"))
        block.vector(run_for("vector"))
        block.scalar(run_for("scalar"))
        block.gpsimd(run_for("gpsimd"))
    return nc


def _prep_inputs(inputs):
    f32 = np.float32
    x = np.ascontiguousarray(np.asarray(inputs["L0_attn_x"], f32).reshape(S, D))
    xTf = np.ascontiguousarray(x.T)
    Wqkvz = np.asarray(inputs["L0_attn_W_qkvz"], f32)
    WqkvT = np.ascontiguousarray(Wqkvz[:8192].T)
    WzT = np.ascontiguousarray(Wqkvz[8192:].T)
    WbaT = np.ascontiguousarray(np.asarray(inputs["L0_attn_W_ba"], f32).T)
    convw_ = np.ascontiguousarray(np.asarray(inputs["L0_attn_conv_w"], f32)[:, 0, :])
    dtb_ = np.ascontiguousarray(
        np.tile(np.asarray(inputs["L0_attn_dt_bias"], f32)[None, :], (128, 1)))
    alog_ = np.ascontiguousarray(np.asarray(inputs["L0_attn_A_log"], f32)[None, :])
    normw_ = np.ascontiguousarray(
        np.asarray(inputs["L0_attn_norm_w"], f32).reshape(NV, 128).T)
    WoutT = np.ascontiguousarray(np.asarray(inputs["L0_attn_W_out"], f32).T)
    rmsw_ = np.ascontiguousarray(
        np.asarray(inputs["L0_ffn_rms_weight"], f32).reshape(NDC, 128).T)
    routw_ = np.ascontiguousarray(np.asarray(inputs["L0_ffn_router_w"], f32))
    wg_ = np.ascontiguousarray(np.asarray(inputs["L0_ffn_w_gate"], f32))
    wu_ = np.ascontiguousarray(np.asarray(inputs["L0_ffn_w_up"], f32))
    wd_ = np.ascontiguousarray(np.asarray(inputs["L0_ffn_w_down"], f32))
    swg_ = np.ascontiguousarray(np.asarray(inputs["L0_ffn_sw_gate"], f32))
    swu_ = np.ascontiguousarray(np.asarray(inputs["L0_ffn_sw_up"], f32))
    swd_ = np.ascontiguousarray(np.asarray(inputs["L0_ffn_sw_down"], f32))

    bf = ml_dtypes.bfloat16
    ii, jj = np.arange(128)[:, None], np.arange(128)[None, :]
    mn_nat = np.where(jj < ii, 0.0, MASKNEG).astype(f32)
    mn_T = np.where(ii < jj, 0.0, MASKNEG).astype(f32)
    mn_iT = np.where(ii <= jj, 0.0, MASKNEG).astype(f32)
    consts = {
        "identb": np.eye(128, dtype=bf),
        "identb32": np.tile(np.eye(128, dtype=bf), (1, NV)),
        "identf": np.eye(128, dtype=f32),
        "trii": np.triu(np.ones((128, 128), f32)),
        "onesf": np.ones((1, 128), f32),
        "monesf": -np.ones((1, 128), f32),
        "onesb": np.ones((1, 128), bf),
        "fconst": np.tile(np.array([[1e-6, 1e-5, 1.0, 1.0 / 4096.0, 1.0 / 2048.0,
                                     -1.0]], f32), (128, 1)),
        "onescb": np.ones((128, 1), bf),
        "identb4": np.tile(np.eye(128, dtype=bf), (1, 4)),
        "mneg_nat4": np.tile(mn_nat, (1, 4)),
        "mneg_T4": np.tile(mn_T, (1, 4)),
        "mneg_iT4": np.tile(mn_iT, (1, 4)),
        "twoI4": np.tile(2.0 * np.eye(128, dtype=f32), (1, 4)),
    }
    in_maps = []
    for cix in range(NCORE):
        lo = cix * TPC
        halo = np.zeros((D, 3), f32)
        if lo >= 3:
            halo = xTf[:, lo - 3:lo]
        xslice = np.ascontiguousarray(np.concatenate([halo, xTf[:, lo:lo + TPC]], 1))
        oh = np.zeros((128, NCORE), f32)
        oh[:, cix] = 1.0
        m = {
            "xT": xslice, "WqkvT": WqkvT, "WzT": WzT, "WbaT": WbaT,
            "convw": convw_, "dtb": dtb_, "alog": alog_, "normw": normw_,
            "WoutT": WoutT, "rmsw": rmsw_, "routw": routw_,
            "wg": wg_, "wu": wu_, "wd": wd_, "swg": swg_, "swu": swu_, "swd": swd_,
            "onehot": np.ascontiguousarray(oh),
        }
        m.update(consts)
        in_maps.append(m)
    return in_maps


def kernel(**inputs):
    in_maps = _prep_inputs(inputs)
    nc = build_program()
    res = run_bass_kernel_spmd(nc, in_maps, core_ids=list(range(NCORE)))
    out = np.empty((S, D), np.float32)
    for cix in range(NCORE):
        out[cix * TPC:(cix + 1) * TPC, :] = res.results[cix]["outT"].T
    return out.reshape(1, S, D)


# version 10 (final)
# speedup vs baseline: 1.0605x; 1.0605x over previous
import sys
sys.path.insert(0, "/opt/trn_rl_repo")
import numpy as np
import ml_dtypes
import concourse.bass as bass
import concourse.mybir as mybir
from concourse.bass_utils import run_bass_kernel_spmd

F32 = mybir.dt.float32
BF16 = mybir.dt.bfloat16
AF = mybir.ActivationFunctionType
ALU = mybir.AluOpType
AX = mybir.AxisListType

S, D = 16384, 2048
NCORE = 8
TPC = S // NCORE
NK, NV, DK, DV = 16, 32, 128, 128
C = 128
NCH = TPC // C
RT = 512
NR = TPC // RT
KR = RT // C
NJB = 64
NDC = D // 128
NEWTON = 6
MASKNEG = -1.0e6
NEXP = 8
F = 1024
NFB = F // 128

STAGE = "full"
SIM1 = False
DEV_TAPS = ()


class Builder:

    def __init__(self, nc):
        self.nc = nc
        self.semg = [nc.semaphore(f"tk{i}") for i in range(64)]
        self.sems = [g.__enter__() for g in self.semg]
        self.rng = {"tensor": list(range(0, 12)), "vector": list(range(12, 24)),
                    "scalar": list(range(24, 30)), "s": list(range(30, 40)),
                    "g": list(range(40, 63)), "c": [63]}
        self.pos = {k: 0 for k in self.rng}
        self.cnt = {i: 0 for i in range(64)}
        self.res = {}
        self.floor = []
        self.waited = {k: {} for k in ("sync", "tensor", "vector", "scalar", "gpsimd")}
        self.prog = {k: [] for k in ("sync", "tensor", "vector", "scalar", "gpsimd")}

    def _sem(self, cls):
        if cls in ("g", "s"):
            self.pos[cls] = (self.pos[cls] + 1) % len(self.rng[cls])
            return self.rng[cls][self.pos[cls]]
        idx = self.rng[cls][self.pos[cls]]
        if self.cnt[idx] > 20000:
            self.pos[cls] += 1
            idx = self.rng[cls][self.pos[cls]]
        return idx

    def step(self, emits, rd=(), wr=()):
        deps = {}
        def add(tok):
            for (si, v) in tok:
                if deps.get(si, -1) < v:
                    deps[si] = v
        add(self.floor)
        for n in rd:
            e = self.res.get(n)
            if e is None:
                self.res[n] = e = {"w": [], "r": []}
            add(e["w"])
        for n in wr:
            e = self.res.get(n)
            if e is None:
                self.res[n] = e = {"w": [], "r": []}
            add(e["w"])
            for t in e["r"]:
                add(t)
        toks = {}
        seen = set()
        for em in emits:
            ename, fn, ninc = em[0], em[1], em[2]
            cls = em[3] if len(em) > 3 else (
                "g" if ename == "gpsimd" else ("s" if ename == "sync" else ename))
            si = self._sem(cls)
            sem = self.sems[si]
            if cls in ("g", "s", "c") and self.cnt[si] > 0:
                if deps.get(si, -1) < self.cnt[si]:
                    deps[si] = self.cnt[si]
            if ename not in seen:
                seen.add(ename)
                dep_list = sorted(deps.items())
                w = self.waited[ename]
                for (dsi, dval) in dep_list:
                    if w.get(dsi, -1) >= dval:
                        continue
                    w[dsi] = dval
                    dsem = self.sems[dsi]
                    self.prog[ename].append(
                        lambda e, dsem=dsem, dval=dval: e.wait_ge(dsem, dval))
            self.prog[ename].append(lambda e, fn=fn, sem=sem: fn(e, sem))
            self.cnt[si] += ninc
            toks[si] = self.cnt[si]
        token = sorted(toks.items())
        for n in wr:
            e = self.res[n]
            e["w"] = token
            e["r"] = []
        for n in rd:
            self.res[n]["r"].append(token)
        self.last = token
        return token

    def barrier(self):
        tok = [(si, c) for si, c in self.cnt.items() if c > 0]
        self.floor = tok
        self.res = {}
        return tok


def build_program():
    nc = bass.Bass("TRN2", target_bir_lowering=False)
    dt = lambda n, s, d, k="ExternalInput": nc.dram_tensor(n, s, d, kind=k).ap()

    xT = dt("xT", [D, TPC + 3], F32)
    WqkvT = dt("WqkvT", [D, 8192], F32)
    WzT = dt("WzT", [D, 4096], F32)
    WbaT = dt("WbaT", [D, 64], F32)
    convw = dt("convw", [8192, 4], F32)
    dtb = dt("dtb", [128, NV], F32)
    alog = dt("alog", [1, NV], F32)
    normw = dt("normw", [128, NV], F32)
    WoutT = dt("WoutT", [4096, D], F32)
    rmsw = dt("rmsw", [128, NDC], F32)
    routw = dt("routw", [D, NEXP], F32)
    wg = dt("wg", [NEXP, D, F], F32)
    wu = dt("wu", [NEXP, D, F], F32)
    wd = dt("wd", [NEXP, F, D], F32)
    swg = dt("swg", [D, F], F32)
    swu = dt("swu", [D, F], F32)
    swd = dt("swd", [F, D], F32)
    identb = dt("identb", [128, 128], BF16)
    identb32 = dt("identb32", [128, NV * 128], BF16)
    identf = dt("identf", [128, 128], F32)
    trii = dt("trii", [128, 128], F32)
    onesf = dt("onesf", [1, 128], F32)
    monesf = dt("monesf", [1, 128], F32)
    mneg_nat4 = dt("mneg_nat4", [128, 512], F32)
    mneg_T4 = dt("mneg_T4", [128, 512], F32)
    mneg_iT4 = dt("mneg_iT4", [128, 512], F32)
    twoI4 = dt("twoI4", [128, 512], F32)
    onehot = dt("onehot", [128, NCORE], F32)
    onesb = dt("onesb", [1, 128], BF16)
    fconst = dt("fconst", [128, 6], F32)
    onescb = dt("onescb", [128, 1], BF16)
    identb4 = dt("identb4", [128, 512], BF16)

    outT = dt("outT", [D, TPC], F32, "ExternalOutput")

    arena_la = dt("arena_la", [NV, TPC], F32, "Internal")
    arena_lb = dt("arena_lb", [NV, TPC], F32, "Internal")
    arena_wr = dt("arena_wr", [128, 128], F32, "Internal")
    sp_ob = dt("sp_ob", [NV * NCH * 128, 128], BF16, "Internal")
    sp_h5 = dt("sp_h5", [NV * NCH * 128, 128], BF16, "Internal")
    sp_x1 = dt("sp_x1", [NDC * 128, TPC], F32, "Internal")
    sp_su = dt("sp_su", [(NEXP + 1) * NFB * 128, TPC], BF16, "Internal")
    sp_ffn = dt("sp_ffn", [NDC * 128, TPC], F32, "Internal")
    cc_in = dt("cc_in", [2 * NV * 128, 128], BF16, "Internal")
    cc_out = dt("cc_out", [NCORE * 2 * NV * 128, 128], BF16, "Internal")

    taps = {}
    for name, shape, dtp in DEV_TAPS:
        taps[name] = dt("tap_" + name, list(shape), dtp, "ExternalOutput")

    sb = lambda n, s, d: nc.alloc_sbuf_tensor(n, s, d).ap()
    c_identb = sb("c_identb", [128, 128], BF16)
    c_identf = sb("c_identf", [128, 128], F32)
    c_trii = sb("c_trii", [128, 128], F32)
    c_onesf = sb("c_onesf", [1, 128], F32)
    c_monesf = sb("c_monesf", [1, 128], F32)
    c_mnegN = sb("c_mnegN", [128, 512], F32)
    c_mnegT = sb("c_mnegT", [128, 512], F32)
    c_mnegI = sb("c_mnegI", [128, 512], F32)
    c_twoI4 = sb("c_twoI4", [128, 512], F32)
    c_dtb = sb("c_dtb", [128, NV], F32)
    c_normw = sb("c_normw", [128, NV], F32)
    c_rmsw = sb("c_rmsw", [128, NDC], F32)
    c_onehot = sb("c_onehot", [128, NCORE], F32)
    c_alog = sb("c_alog", [1, NV], F32)
    exprow = sb("exprow", [1, NV], F32)
    expA = sb("expA", [128, NV], F32)
    convw_sb = sb("convw_sb", [128, NJB * 4], F32)
    wba_sb = sb("wba_sb", [128, 16 * 64], BF16)

    xtb = sb("xtb", [128, NDC * RT], BF16)
    xtail = sb("xtail", [128, NDC * 3], BF16)
    wjb = sb("wjb", [128, 2 * 2048], BF16)
    qkv = sb("qkv", [128, NJB * RT], BF16)
    c_onesb = sb("c_onesb", [1, 128], BF16)
    c_fconst = sb("c_fconst", [128, 6], F32)
    c_onescb = sb("c_onescb", [128, 1], BF16)
    c_identb4 = sb("c_identb4", [128, 512], BF16)
    for _i, _v in enumerate([1e-6, 1e-5, 1.0, 1.0 / 4096.0, 1.0 / 2048.0, -1.0]):
        nc.const_aps.register(F32, _v, c_fconst[:, _i:_i + 1])
    ab_guards = []
    def sba(n, s, d):
        g = nc.sbuf_tensor(n, s, d)
        ap = g.__enter__()
        ab_guards.append(g)
        return ap
    cpre = sba("cpre", [128, 515], F32)
    cacc = [sba("cacc0", [128, 512], F32), sba("cacc1", [128, 512], F32)]
    tails = sba("tails", [128, NJB * 3], F32)
    batile = cacc[1][:, 0:256]
    bcols = sba("bcols", [128, KR * NV], F32)
    gcols = sba("gcols", [128, KR * NV], F32)
    spa = cacc[1][:, 256:384]
    spb = cacc[1][:, 384:512]
    lacols = sba("lacols", [128, KR * NV], F32)
    nlacols = sba("nlacols", [128, KR * NV], F32)
    lbcols = sba("lbcols", [128, KR * NV], F32)
    elacols = sba("elacols", [128, KR * NV], F32)
    bwcols = sba("bwcols", [128, KR * NV], F32)
    ewbcols = sba("ewbcols", [128, KR * NV], F32)
    accols = sba("accols", [128, KR * NV], F32)
    lacrow = sba("lacrow", [1, NV], F32)
    rowstage = sba("rowstage", [32, 2 * 128], F32)
    laro2 = [sba("laroPA", [1, RT], F32), sba("laroPB", [1, RT], F32)]
    lbro2 = [sba("lbroPA", [1, RT], F32), sba("lbroPB", [1, RT], F32)]
    sqtile = cacc[0]
    rsqrow = sba("rsqrow", [1, 512], F32)

    W2 = lambda n, s, d: [sba(n + "A", s, d), sba(n + "B", s, d)]
    w_eT2 = W2("w_eT", [128, 512], BF16)
    w_eN2 = W2("w_eN", [128, 512], BF16)
    w_An2 = W2("w_An", [128, 512], BF16)
    w_X2 = W2("w_X", [128, 512], BF16)
    w_Xt2 = W2("w_Xt", [128, 512], BF16)
    w_Z2 = W2("w_Z", [128, 512], BF16)
    w_PmT2 = W2("w_PmT", [128, 512], BF16)
    w_QwT2 = W2("w_QwT", [128, 512], BF16)
    w_Kr2 = W2("w_Kr", [128, 512], BF16)
    w_Vr2 = W2("w_Vr", [128, 512], BF16)
    w_RV2 = W2("w_RV", [128, 1024], BF16)
    w_Wb2 = W2("w_Wb", [128, 512], BF16)
    w_UG2 = W2("w_UG", [128, 1024], BF16)
    w_McT2 = W2("w_McT", [128, 512], BF16)
    w_Nc2 = W2("w_Nc", [128, 512], F32)
    w_H32 = W2("w_H3", [128, 512], BF16)
    w_Ob2 = W2("w_Ob", [128, 512], BF16)
    w_H52 = W2("w_H5", [128, 512], BF16)
    kkS2 = W2("kkS", [128, 512], BF16)
    kqS2 = W2("kqS", [128, 512], BF16)
    elarow2 = W2("elarowP", [1, RT], F32)
    Qs_f = sba("Qs_f", [128, NV * 128], F32)
    Qs_b = sba("Qs_b", [128, NV * 128], BF16)
    Pi_b = sba("Pi_b", [128, NV * 128], BF16)

    psg = []
    def psum(n, shape, dtp):
        g = nc.psum_tensor(n, shape, dtp)
        ap = g.__enter__()
        psg.append(g)
        return ap
    p = [psum(f"p{i}", [128, 512], F32) for i in range(7)]
    pbg = nc.psum_tensor("pb", [128, 1024], BF16)
    pb = pbg.__enter__()

    bld = Builder(nc)
    st = bld.step

    def dma(e, sem, out, in_):
        return e.dma_start(out=out, in_=in_).then_inc(sem, 16)

    def ld_consts(e, sem):
        dma(e, sem, c_identb[:], identb)
        dma(e, sem, c_identf[:], identf)
        dma(e, sem, c_trii[:], trii)
        dma(e, sem, c_onesf[:], onesf)
        dma(e, sem, c_monesf[:], monesf)
        dma(e, sem, c_mnegN[:], mneg_nat4)
        dma(e, sem, c_mnegT[:], mneg_T4)
        dma(e, sem, c_mnegI[:], mneg_iT4)
        dma(e, sem, c_twoI4[:], twoI4)
        dma(e, sem, c_dtb[:], dtb)
        dma(e, sem, c_normw[:], normw)
        dma(e, sem, c_rmsw[:], rmsw)
        dma(e, sem, c_onehot[:], onehot)
        dma(e, sem, c_alog[:], alog)
        dma(e, sem, convw_sb[:], convw.rearrange("(j p) t -> p j t", p=128))
        dma(e, sem, Pi_b[:], identb32)
        dma(e, sem, c_onesb[:], onesb)
        dma(e, sem, c_fconst[:], fconst)
        dma(e, sem, c_onescb[:], onescb)
        dma(e, sem, c_identb4[:], identb4)

    def ld_casts(e, sem):
        dma(e, sem, wba_sb[:], WbaT.rearrange("(k p) j -> p k j", p=128))
        dma(e, sem, xtail[:], xT[:, 0:3].rearrange("(k p) t -> p k t", p=128))
    st([("sync", ld_consts, 16 * 20), ("gpsimd", ld_casts, 32)])
    bld.barrier()

    st([("scalar", lambda e, s: e.activation(out=exprow[:], in_=c_alog[:],
                                             func=AF.Exp).then_inc(s, 1), 1)],
       wr=["exprow"])
    st([("tensor", lambda e, s: e.matmul(p[0][:, 0:NV], c_onesf[:], exprow[:],
                                         start=True, stop=True).then_inc(s, 1), 1)],
       rd=["exprow"], wr=["p0"])
    def setup_v(e, s):
        e.tensor_scalar(out=expA[:], in0=p[0][:, 0:NV], scalar1=-1.0, scalar2=None,
                        op0=ALU.mult).then_inc(s, 1)
        e.tensor_scalar(out=c_normw[:], in0=c_normw[:], scalar1=1.0, scalar2=None,
                        op0=ALU.add).then_inc(s, 1)
        e.memset(Qs_f[:], 0.0).then_inc(s, 1)
        e.memset(Qs_b[:], 0.0).then_inc(s, 1)
        e.memset(tails[:], 0.0).then_inc(s, 1)
    st([("vector", setup_v, 5)], rd=["p0"], wr=["expA"])
    bld.barrier()

    for r in range(NR):
        c0 = 3 + r * RT

        def ld_x(e, sem, c0=c0):
            for k in range(NDC):
                dma(e, sem, xtb[:, k * RT:(k + 1) * RT],
                    xT[k * 128:(k + 1) * 128, c0:c0 + RT])
        st([("gpsimd", ld_x, 16 * NDC)], wr=["xtb"])

        for cc in range(KR):
            def mm_ba(e, sem, cc=cc):
                for k in range(NDC):
                    ii = e.matmul(p[0][:, 0:64],
                             xtb[:, k * RT + cc * C: k * RT + cc * C + C],
                             wba_sb[:, k * 64:(k + 1) * 64],
                             start=(k == 0), stop=(k == NDC - 1))
                ii.then_inc(sem, 1)
            st([("tensor", mm_ba, 1)], rd=["xtb"], wr=["p0"])
            st([("vector", lambda e, s, cc=cc: e.tensor_copy(
                out=batile[:, cc * 64:(cc + 1) * 64], in_=p[0][:, 0:64]).then_inc(s, 1), 1)],
               rd=["p0"], wr=["cacc1"])
        def act_beta(e, s):
            for cc in range(KR):
                e.activation(out=bcols[:, cc * NV:(cc + 1) * NV],
                             in_=batile[:, cc * 64: cc * 64 + NV],
                             func=AF.Sigmoid).then_inc(s, 1)
        st([("scalar", act_beta, KR)], rd=["cacc1"], wr=["bcols"])
        def sp1(e, s):
            for cc in range(KR):
                e.tensor_tensor(out=spa[:, cc * NV:(cc + 1) * NV],
                                in0=batile[:, cc * 64 + NV: cc * 64 + 64],
                                in1=c_dtb[:], op=ALU.add).then_inc(s, 1)
        st([("vector", sp1, KR)], rd=["cacc1"], wr=["cacc1"])
        st([("scalar", lambda e, s: e.activation(out=spb[:], in_=spa[:],
                                                 func=AF.Abs).then_inc(s, 1), 1)],
           rd=["cacc1"], wr=["cacc1"])
        st([("scalar", lambda e, s: e.activation(out=spb[:], in_=spb[:],
                                                 func=AF.Exp,
                                                 scale=-1.0).then_inc(s, 1), 1)],
           rd=["cacc1"], wr=["cacc1"])
        st([("scalar", lambda e, s: e.activation(out=spb[:], in_=spb[:], func=AF.Ln,
                                                 bias=1.0).then_inc(s, 1), 1)],
           rd=["cacc1"], wr=["cacc1"])
        st([("vector", lambda e, s: e.scalar_tensor_tensor(
            out=spa[:], in0=spa[:], scalar=0.0, in1=spb[:],
            op0=ALU.max, op1=ALU.add).then_inc(s, 1), 1)],
           rd=["cacc1"], wr=["cacc1"])
        def gmul(e, s):
            for cc in range(KR):
                e.tensor_tensor(out=gcols[:, cc * NV:(cc + 1) * NV],
                                in0=spa[:, cc * NV:(cc + 1) * NV],
                                in1=expA[:], op=ALU.mult).then_inc(s, 1)
        st([("vector", gmul, KR)], rd=["cacc1", "expA"], wr=["gcols"])
        for cc in range(KR):
            def mm_la(e, s, cc=cc):
                ii = e.matmul(p[1][:, 0:NV], c_trii[:], gcols[:, cc * NV:(cc + 1) * NV],
                         start=True, stop=True)
                ii = e.matmul(p[2][0:NV, 0:128], gcols[:, cc * NV:(cc + 1) * NV], c_trii[:],
                         start=True, stop=True)
                ii = e.matmul(p[3][0:1, 0:NV], c_trii[:, 127:128],
                         gcols[:, cc * NV:(cc + 1) * NV], start=True, stop=True)
                ii.then_inc(s, 1)
            st([("tensor", mm_la, 1)], rd=["gcols"], wr=["p1", "p2", "p3"])
            def ev_la(e, s, cc=cc):
                e.tensor_copy(out=lacols[:, cc * NV:(cc + 1) * NV],
                              in_=p[1][:, 0:NV]).then_inc(s, 1)
                e.tensor_copy(out=rowstage[0:NV, 0:128],
                              in_=p[2][0:NV, 0:128]).then_inc(s, 1)
                e.tensor_copy(out=lacrow[:], in_=p[3][0:1, 0:NV]).then_inc(s, 1)
            st([("vector", ev_la, 3)], rd=["p1", "p2", "p3"], wr=["lacols", "rowstage", "lacrow"])
            def aw_la(e, s, cc=cc, r=r):
                dma(e, s, arena_la[:, r * RT + cc * C: r * RT + (cc + 1) * C],
                    rowstage[0:NV, 0:128])
            st([("gpsimd", aw_la, 16)], rd=["rowstage"], wr=["Ala"])
            def mm_acb(e, s):
                ii = e.matmul(p[3][:, 0:NV], c_onesf[:], lacrow[:],
                         start=True, stop=True).then_inc(s, 1)
            st([("tensor", mm_acb, 1)], rd=["lacrow"], wr=["p3"])
            def ev_acb(e, s, cc=cc):
                e.tensor_tensor(out=ewbcols[:, cc * NV:(cc + 1) * NV],
                                in0=p[3][:, 0:NV], in1=lacols[:, cc * NV:(cc + 1) * NV],
                                op=ALU.subtract).then_inc(s, 1)
                e.tensor_copy(out=accols[:, cc * NV:(cc + 1) * NV],
                              in_=p[3][:, 0:NV]).then_inc(s, 1)
            st([("vector", ev_acb, 2)], rd=["p3", "lacols"], wr=["ewbcols", "accols"])
        def d1(e, s):
            e.tensor_scalar(out=nlacols[:], in0=lacols[:], scalar1=-1.0, scalar2=None,
                            op0=ALU.mult).then_inc(s, 1)
        st([("vector", d1, 1)], rd=["lacols"], wr=["nlacols"])
        def d2(e, s):
            e.activation(out=elacols[:], in_=lacols[:], func=AF.Exp).then_inc(s, 1)
            e.activation(out=lbcols[:], in_=bcols[:], func=AF.Ln).then_inc(s, 1)
            e.activation(out=ewbcols[:], in_=ewbcols[:], func=AF.Exp).then_inc(s, 1)
            e.activation(out=accols[:], in_=accols[:], func=AF.Exp).then_inc(s, 1)
        st([("scalar", d2, 4)], rd=["lacols", "bcols", "ewbcols", "accols"], wr=["elacols", "lbcols", "ewbcols", "accols"])
        def d3(e, s):
            e.tensor_tensor(out=lbcols[:], in0=lbcols[:], in1=lacols[:],
                            op=ALU.add).then_inc(s, 1)
            e.tensor_tensor(out=bwcols[:], in0=bcols[:], in1=elacols[:],
                            op=ALU.mult).then_inc(s, 1)
        st([("vector", d3, 2)], rd=["lbcols", "lacols", "bcols", "elacols"], wr=["lbcols", "bwcols"])
        for cc in range(KR):
            def mm_lbr(e, s, cc=cc):
                ii = e.matmul(p[2][0:NV, 0:128], lbcols[:, cc * NV:(cc + 1) * NV],
                         c_identf[:], start=True, stop=True).then_inc(s, 1)
            st([("tensor", mm_lbr, 1)], rd=["lbcols"], wr=["p2"])
            st([("vector", lambda e, s, cc=cc: e.tensor_copy(
                out=rowstage[0:NV, 128:256],
                in_=p[2][0:NV, 0:128]).then_inc(s, 1), 1)],
               rd=["p2"], wr=["rowstage"])
            def aw_lb(e, s, cc=cc, r=r):
                dma(e, s, arena_lb[:, r * RT + cc * C: r * RT + (cc + 1) * C],
                    rowstage[0:NV, 128:256])
            st([("gpsimd", aw_lb, 16)], rd=["rowstage"], wr=["Alb"])

        for jb in range(NJB):
            buf = jb % 2
            def ld_w(e, sem, jb=jb, buf=buf):
                dma(e, sem, wjb[:, buf * 2048:(buf + 1) * 2048],
                    WqkvT[:, jb * 128:(jb + 1) * 128]
                    .rearrange("(k p) j -> p k j", p=128))
            st([("gpsimd", ld_w, 16)], wr=[f"wjb{buf}"])
            pp = p[4] if jb % 2 == 0 else p[3]
            ppn = "p4" if jb % 2 == 0 else "p3"
            def mm_proj(e, sem, buf=buf, r=r, pp=pp):
                for k in range(NDC):
                    ii = e.matmul(pp[:],
                             wjb[:, buf * 2048 + k * 128: buf * 2048 + (k + 1) * 128],
                             xtb[:, k * RT:(k + 1) * RT],
                             start=(k == 0), stop=(k == NDC - 1))
                if r == 0:
                    for k in range(NDC):
                        ii = e.matmul(p[5][:, 0:3],
                                 wjb[:, buf * 2048 + k * 128: buf * 2048 + (k + 1) * 128],
                                 xtail[:, k * 3:(k + 1) * 3],
                                 start=(k == 0), stop=(k == NDC - 1))
                ii.then_inc(sem, 1)
            st([("tensor", mm_proj, 1)], rd=[f"wjb{buf}", "xtb", "xtail"], wr=[ppn] + (["p5"] if r == 0 else []))
            if r == 0:
                st([("vector", lambda e, s, jb=jb: e.tensor_copy(
                    out=tails[:, jb * 3:(jb + 1) * 3],
                    in_=p[5][:, 0:3]).then_inc(s, 1), 1)],
                   rd=["p5"], wr=[f"tl{jb}"])
            def conv1(e, s, jb=jb, pp=pp):
                e.tensor_copy(out=cpre[:, 0:3],
                              in_=tails[:, jb * 3:(jb + 1) * 3]).then_inc(s, 1)
                e.tensor_copy(out=cpre[:, 3:515], in_=pp[:]).then_inc(s, 1)
            st([("vector", conv1, 2)], rd=[f"tl{jb}", ppn], wr=["cpre"])
            def conv2a(e, s, jb=jb):
                e.tensor_scalar(out=cacc[jb % 2][:], in0=cpre[:, 0:512],
                                scalar1=convw_sb[:, jb * 4: jb * 4 + 1], scalar2=None,
                                op0=ALU.mult).then_inc(s, 1)
            st([("vector", conv2a, 1)], rd=["cpre"], wr=[f"cacc{jb % 2}"])
            for tap in (1, 2, 3):
                def conv2b(e, s, jb=jb, tap=tap):
                    e.scalar_tensor_tensor(
                        out=cacc[jb % 2][:], in0=cpre[:, tap:tap + 512],
                        scalar=convw_sb[:, jb * 4 + tap: jb * 4 + tap + 1],
                        in1=cacc[jb % 2][:], op0=ALU.mult, op1=ALU.add).then_inc(s, 1)
                st([("vector", conv2b, 1)], rd=["cpre", f"cacc{jb % 2}"], wr=[f"cacc{jb % 2}"])
            def tails_upd(e, s, jb=jb):
                e.tensor_copy(out=tails[:, jb * 3:(jb + 1) * 3],
                              in_=cpre[:, 512:515]).then_inc(s, 1)
            st([("vector", tails_upd, 1)], rd=["cpre"], wr=[f"tl{jb}"])
            def conv3(e, s, jb=jb):
                e.activation(out=qkv[:, jb * RT:(jb + 1) * RT],
                             in_=cacc[jb % 2][:], func=AF.Silu).then_inc(s, 1)
            st([("scalar", conv3, 1)], rd=[f"cacc{jb % 2}"], wr=[f"qk{jb}"])

        rsq2 = [rsqrow, laro2[0]]
        rsqn = ["rsqrow", "laro0"]
        sqt2 = [cacc[0], cacc[1]]
        sqtn = ["cacc0", "cacc1"]
        def emit_l2(jb, stq):
            st = lambda emits, rd=(), wr=(): stq.append((emits, rd, wr))
            jp = jb % 2
            def sq(e, s, jb=jb, jp=jp):
                e.tensor_tensor(out=sqt2[jp][:], in0=qkv[:, jb * RT:(jb + 1) * RT],
                                in1=qkv[:, jb * RT:(jb + 1) * RT],
                                op=ALU.mult).then_inc(s, 1)
            st([("vector", sq, 1)], rd=[f"qk{jb}"], wr=[sqtn[jp]])
            def mm_ss(e, s, jp=jp):
                e.matmul(p[5 - jp][0:1, 0:512], c_trii[:, 127:128], sqt2[jp][:],
                         start=True, stop=True).then_inc(s, 1)
            st([("tensor", mm_ss, 1)], rd=[sqtn[jp]], wr=[f"p{5 - jp}"])
            def rsq1(e, s, jp=jp):
                e.activation(out=rsq2[jp][0:1, :], in_=p[5 - jp][0:1, 0:512],
                             func=AF.Sqrt, bias=1e-6).then_inc(s, 1)
            st([("scalar", rsq1, 1)], rd=[f"p{5 - jp}"], wr=[rsqn[jp]])
            def rsq2f(e, s, jp=jp):
                e.reciprocal(out=rsq2[jp][0:1, :], in_=rsq2[jp][0:1, :]).then_inc(s, 1)
            st([("vector", rsq2f, 1)], rd=[rsqn[jp]], wr=[rsqn[jp]])
            def mm_rb(e, s, jp=jp):
                e.matmul(p[5 - jp][:, 0:512], c_onesf[:], rsq2[jp][0:1, :],
                         start=True, stop=True).then_inc(s, 1)
            st([("tensor", mm_rb, 1)], rd=[rsqn[jp]], wr=[f"p{5 - jp}"])
            def l2m(e, s, jb=jb, jp=jp):
                e.tensor_tensor(out=qkv[:, jb * RT:(jb + 1) * RT],
                                in0=qkv[:, jb * RT:(jb + 1) * RT],
                                in1=p[5 - jp][:, 0:512], op=ALU.mult).then_inc(s, 1)
            st([("vector", l2m, 1)], rd=[f"p{5 - jp}", f"qk{jb}"], wr=[f"qk{jb}"])
        for jb0 in range(0, 32, 2):
            la_, lb_ = [], []
            emit_l2(jb0, la_)
            emit_l2(jb0 + 1, lb_)
            for i in range(len(la_)):
                bld.step(la_[i][0], rd=la_[i][1], wr=la_[i][2])
                bld.step(lb_[i][0], rd=lb_[i][1], wr=lb_[i][2])

        if ("qkv_r%d" % r) in taps:
            st([("gpsimd", lambda e, s, r=r: dma(e, s, taps["qkv_r%d" % r], qkv[:]), 16)], rd=[f"qk{j}" for j in range(NJB)])
        if ("gla_r%d" % r) in taps:
            def tap_gla(e, s, r=r):
                dma(e, s, taps["gla_r%d" % r][0:128, :], gcols[:])
                dma(e, s, taps["gla_r%d" % r][128:256, :], lacols[:])
                dma(e, s, taps["gla_r%d" % r][256:384, :], bcols[:])
                dma(e, s, taps["gla_r%d" % r][384:512, :], ewbcols[:])
            st([("gpsimd", tap_gla, 64)], rd=["gcols", "lacols", "bcols", "ewbcols"])
        if STAGE == "A":
            continue

        def emit_head(h, stq):
            st = lambda emits, rd=(), wr=(): stq.append((emits, rd, wr))
            par = h % 2
            w_eT, w_eN, w_An, w_X, w_Xt = (w_eT2[par], w_eN2[par], w_An2[par],
                                           w_X2[par], w_Xt2[par])
            w_Z, w_PmT, w_QwT, w_Kr, w_Vr = (w_Z2[par], w_PmT2[par], w_QwT2[par],
                                             w_Kr2[par], w_Vr2[par])
            w_RV, w_Wb, w_UG, w_McT = w_RV2[par], w_Wb2[par], w_UG2[par], w_McT2[par]
            w_Nc, w_H3, w_Ob, w_H5 = w_Nc2[par], w_H32[par], w_Ob2[par], w_H52[par]
            kkS, kqS, elarow = kkS2[par], kqS2[par], elarow2[par]
            P = lambda n: n + str(par)
            pArg = p[2] if par == 0 else p[6]
            nArg = "p2" if par == 0 else "p6"
            def ld_rows(e, s, r=r, h=h):
                dma(e, s, laro2[par][0:1, :], arena_la[h:h + 1, r * RT:(r + 1) * RT])
                dma(e, s, lbro2[par][0:1, :], arena_lb[h:h + 1, r * RT:(r + 1) * RT])
            st([("sync", ld_rows, 32)], rd=["Ala", "Alb"], wr=[P("laro"), P("lbro")])
            kh = h // 2
            qT = qkv[:, kh * RT:(kh + 1) * RT]
            kT = qkv[:, (16 + kh) * RT:(16 + kh + 1) * RT]
            vT = qkv[:, (32 + h) * RT:(32 + h + 1) * RT]
            hc0 = h * NCH + r * KR

            def mm_kq(e, s, qT=qT, kT=kT):
                for cc in range(KR):
                    ii = e.matmul(p[0][:, cc * C:(cc + 1) * C], kT[:, cc * C:(cc + 1) * C],
                                  qT[:, cc * C:(cc + 1) * C], start=True, stop=True)
                    ii = e.matmul(p[1][:, cc * C:(cc + 1) * C], kT[:, cc * C:(cc + 1) * C],
                                  kT[:, cc * C:(cc + 1) * C], start=True, stop=True)
                ii.then_inc(s, 1)
            st([("tensor", mm_kq, 1)], rd=[f"qk{kh}", f"qk{16 + kh}"], wr=["p0", "p1"])
            def ev_kq(e, s, kqS=kqS, kkS=kkS):
                e.copy(out=kqS[:], in_=p[0][:]).then_inc(s, 1)
                e.copy(out=kkS[:], in_=p[1][:]).then_inc(s, 1)
            st([("scalar", ev_kq, 2)], rd=["p0", "p1"], wr=[P("kq"), P("kk")])

            def mm_argT(e, s, h=h, pArg=pArg, par=par):
                ii = e.matmul(pArg[:], c_onesf[:], lbro2[par][0:1, :],
                              start=True, stop=False)
                ii = e.matmul(pArg[:], c_identf[:], c_mnegT[:], start=False, stop=True)
                ii.then_inc(s, 1)
            st([("tensor", mm_argT, 1)], rd=[P("lbro")], wr=[nArg])
            def act_eT(e, s, h=h, w_eT=w_eT, pArg=pArg):
                for cc in range(KR):
                    e.activation(out=w_eT[:, cc * C:(cc + 1) * C],
                                 in_=pArg[:, cc * C:(cc + 1) * C], func=AF.Exp,
                                 bias=nlacols[:, cc * NV + h: cc * NV + h + 1]
                                 ).then_inc(s, 1)
            st([("scalar", act_eT, KR)], rd=[nArg, "nlacols"], wr=[P("eT")])
            def mm_argN(e, s, h=h, pArg=pArg, par=par):
                ii = e.matmul(pArg[:], c_monesf[:], laro2[par][0:1, :],
                              start=True, stop=False)
                ii = e.matmul(pArg[:], c_identf[:], c_mnegN[:], start=False, stop=True)
                ii.then_inc(s, 1)
            st([("tensor", mm_argN, 1)], rd=[P("laro")], wr=[nArg])
            def act_eN(e, s, h=h, w_eN=w_eN, pArg=pArg):
                for cc in range(KR):
                    e.activation(out=w_eN[:, cc * C:(cc + 1) * C],
                                 in_=pArg[:, cc * C:(cc + 1) * C], func=AF.Exp,
                                 bias=lbcols[:, cc * NV + h: cc * NV + h + 1]
                                 ).then_inc(s, 1)
            st([("scalar", act_eN, KR)], rd=[nArg, "lbcols"], wr=[P("eN")])
            def tt_mb(e, s, w_eT=w_eT, w_eN=w_eN, kkS=kkS):
                e.tensor_tensor(out=w_eT[:], in0=w_eT[:], in1=kkS[:],
                                op=ALU.mult).then_inc(s, 1)
                e.tensor_tensor(out=w_eN[:], in0=w_eN[:], in1=kkS[:],
                                op=ALU.mult).then_inc(s, 1)
            st([("vector", tt_mb, 2)], rd=[P("eT"), P("eN"), P("kk")], wr=[P("eT"), P("eN")])
            def tt_axt(e, s, w_An=w_An, w_X=w_X, w_Xt=w_Xt, w_eT=w_eT, w_eN=w_eN):
                e.tensor_tensor(out=w_An[:], in0=c_identb4[:], in1=w_eN[:],
                                op=ALU.add).then_inc(s, 1)
                e.tensor_tensor(out=w_X[:], in0=c_identb4[:], in1=w_eT[:],
                                op=ALU.subtract).then_inc(s, 1)
                e.tensor_tensor(out=w_Xt[:], in0=c_identb4[:], in1=w_eN[:],
                                op=ALU.subtract).then_inc(s, 1)
            st([("vector", tt_axt, 3)], rd=[P("eT"), P("eN")], wr=[P("An"), P("X"), P("Xt")])
            def mm_argP(e, s, h=h, pArg=pArg, par=par):
                ii = e.matmul(pArg[:], c_onesf[:], laro2[par][0:1, :],
                              start=True, stop=False)
                ii = e.matmul(pArg[:], c_identf[:], c_mnegI[:], start=False, stop=True)
                ii.then_inc(s, 1)
            st([("tensor", mm_argP, 1)], rd=[P("laro")], wr=[nArg])
            def act_eP(e, s, h=h, w_eT=w_eT, pArg=pArg):
                for cc in range(KR):
                    e.activation(out=w_eT[:, cc * C:(cc + 1) * C],
                                 in_=pArg[:, cc * C:(cc + 1) * C], func=AF.Exp,
                                 bias=nlacols[:, cc * NV + h: cc * NV + h + 1]
                                 ).then_inc(s, 1)
            st([("scalar", act_eP, KR)], rd=[nArg, "nlacols"], wr=[P("eT")])
            def tt_pm(e, s, w_PmT=w_PmT, w_eT=w_eT, kqS=kqS):
                e.tensor_tensor(out=w_PmT[:], in0=w_eT[:], in1=kqS[:],
                                op=ALU.mult).then_inc(s, 1)
            st([("vector", tt_pm, 1)], rd=[P("eT"), P("kq")], wr=[P("PmT")])
            def act_elar(e, s, h=h, elarow=elarow, par=par):
                e.activation(out=elarow[:], in_=laro2[par][0:1, :],
                             func=AF.Exp).then_inc(s, 1)
            st([("scalar", act_elar, 1)], rd=[P("laro")], wr=[P("elar")])
            def mm_elab(e, s, elarow=elarow, pArg=pArg):
                e.matmul(pArg[:], c_onesf[:], elarow[:],
                         start=True, stop=True).then_inc(s, 1)
            st([("tensor", mm_elab, 1)], rd=[P("elar")], wr=[nArg])
            def tt_qw(e, s, qT=qT, w_QwT=w_QwT, pArg=pArg):
                e.tensor_tensor(out=w_QwT[:], in0=qT, in1=pArg[:],
                                op=ALU.mult).then_inc(s, 1)
            st([("vector", tt_qw, 1)], rd=[f"qk{kh}", nArg], wr=[P("QwT")])

            for it in range(NEWTON):
                def mm_n1(e, s, w_An=w_An, w_X=w_X):
                    for cc in range(KR):
                        sl = slice(cc * C, (cc + 1) * C)
                        ii = e.matmul(p[3][:, sl], w_An[:, sl], w_X[:, sl],
                                      start=True, stop=True)
                    ii.then_inc(s, 1)
                st([("tensor", mm_n1, 1)], rd=[P("An"), P("X")], wr=["p3"])
                def tt_z(e, s, w_Z=w_Z):
                    e.tensor_tensor(out=w_Z[:], in0=c_twoI4[:], in1=p[3][:],
                                    op=ALU.subtract).then_inc(s, 1)
                st([("vector", tt_z, 1)], rd=["p3"], wr=[P("Z")])
                def mm_n23(e, s, w_Xt=w_Xt, w_Z=w_Z):
                    for cc in range(KR):
                        sl = slice(cc * C, (cc + 1) * C)
                        ii = e.matmul(p[4][:, sl], w_Xt[:, sl], w_Z[:, sl],
                                      start=True, stop=True)
                        ii = e.matmul(p[5][:, sl], w_Z[:, sl], w_Xt[:, sl],
                                      start=True, stop=True)
                    ii.then_inc(s, 1)
                st([("tensor", mm_n23, 1)], rd=[P("Xt"), P("Z")], wr=["p4", "p5"])
                def ev_xxt(e, s, w_X=w_X, w_Xt=w_Xt):
                    e.copy(out=w_X[:], in_=p[4][:]).then_inc(s, 1)
                    e.copy(out=w_Xt[:], in_=p[5][:]).then_inc(s, 1)
                st([("scalar", ev_xxt, 2)], rd=["p4", "p5"], wr=[P("X"), P("Xt")])

            def mm_tr(e, s, kT=kT, vT=vT):
                for cc in range(KR):
                    ii = e.transpose(pb[:, cc * C:(cc + 1) * C],
                                     kT[:, cc * C:(cc + 1) * C], c_identb[:])
                    ii = e.transpose(pb[:, 512 + cc * C: 512 + (cc + 1) * C],
                                     vT[:, cc * C:(cc + 1) * C], c_identb[:])
                ii.then_inc(s, 1)
            st([("tensor", mm_tr, 1)], rd=[f"qk{16 + kh}", f"qk{32 + h}"], wr=["pb"])
            def ev_tr(e, s, w_Kr=w_Kr, w_Vr=w_Vr):
                e.copy(out=w_Kr[:], in_=pb[:, 0:512]).then_inc(s, 1)
                e.copy(out=w_Vr[:], in_=pb[:, 512:1024]).then_inc(s, 1)
            st([("scalar", ev_tr, 2)], rd=["pb"], wr=[P("Kr"), P("Vr")])
            def ts_rv(e, s, h=h, w_RV=w_RV, w_Wb=w_Wb, w_Kr=w_Kr, w_Vr=w_Vr):
                for cc in range(KR):
                    e.tensor_scalar(out=w_RV[:, cc * 256: cc * 256 + 128],
                                    in0=w_Vr[:, cc * C:(cc + 1) * C],
                                    scalar1=bcols[:, cc * NV + h: cc * NV + h + 1],
                                    scalar2=None, op0=ALU.mult).then_inc(s, 1)
                    e.tensor_scalar(out=w_RV[:, cc * 256 + 128: cc * 256 + 256],
                                    in0=w_Kr[:, cc * C:(cc + 1) * C],
                                    scalar1=bwcols[:, cc * NV + h: cc * NV + h + 1],
                                    scalar2=None, op0=ALU.mult).then_inc(s, 1)
                    e.tensor_scalar(out=w_Wb[:, cc * C:(cc + 1) * C],
                                    in0=w_Kr[:, cc * C:(cc + 1) * C],
                                    scalar1=ewbcols[:, cc * NV + h: cc * NV + h + 1],
                                    scalar2=None, op0=ALU.mult).then_inc(s, 1)
            st([("vector", ts_rv, 3 * KR)],
               rd=[P("Kr"), P("Vr"), "bcols", "bwcols", "ewbcols"], wr=[P("RV"), P("Wb")])
            def mm_ug(e, s, w_X=w_X, w_RV=w_RV):
                for cc in range(KR):
                    pd = p[0] if cc < 2 else p[1]
                    off = (cc % 2) * 256
                    ii = e.matmul(pd[:, off:off + 256], w_X[:, cc * C:(cc + 1) * C],
                                  w_RV[:, cc * 256:(cc + 1) * 256], start=True, stop=True)
                ii.then_inc(s, 1)
            st([("tensor", mm_ug, 1)], rd=[P("X"), P("RV")], wr=["p0", "p1"])
            def ev_ug(e, s, w_UG=w_UG):
                e.tensor_copy(out=w_UG[:, 0:512], in_=p[0][:]).then_inc(s, 1)
                e.tensor_copy(out=w_UG[:, 512:1024], in_=p[1][:]).then_inc(s, 1)
            st([("vector", ev_ug, 2)], rd=["p0", "p1"], wr=[P("UG")])
            def ugU(cc, w_UG=w_UG):
                return w_UG[:, cc * 256: cc * 256 + 128]
            def ugG(cc, w_UG=w_UG):
                return w_UG[:, cc * 256 + 128: cc * 256 + 256]
            def mm_mc(e, s, ugU=ugU, ugG=ugG, w_Wb=w_Wb):
                for cc in range(KR):
                    sl = slice(cc * C, (cc + 1) * C)
                    ii = e.matmul(p[0][:, sl], ugG(cc), w_Wb[:, sl], start=True, stop=True)
                    ii = e.matmul(p[1][:, sl], w_Wb[:, sl], ugU(cc), start=True, stop=True)
                ii.then_inc(s, 1)
            st([("tensor", mm_mc, 1)], rd=[P("UG"), P("Wb")], wr=["p0", "p1"])
            def ev_mc(e, s, h=h, w_McT=w_McT, w_Nc=w_Nc):
                for cc in range(KR):
                    sl = slice(cc * C, (cc + 1) * C)
                    e.scalar_tensor_tensor(
                        out=w_McT[:, sl], in0=c_identf[:],
                        scalar=accols[:, cc * NV + h: cc * NV + h + 1],
                        in1=p[0][:, sl], op0=ALU.mult, op1=ALU.subtract).then_inc(s, 1)
                e.tensor_copy(out=w_Nc[:], in_=p[1][:]).then_inc(s, 1)
            st([("vector", ev_mc, KR + 1)], rd=["p0", "p1", "accols"], wr=[P("McT"), P("Nc")])
            def mm_h3(e, s, ugG=ugG, w_PmT=w_PmT):
                for cc in range(KR):
                    sl = slice(cc * C, (cc + 1) * C)
                    ii = e.matmul(p[0][:, sl], ugG(cc), w_PmT[:, sl], start=True, stop=True)
                ii.then_inc(s, 1)
            st([("tensor", mm_h3, 1)], rd=[P("UG"), P("PmT")], wr=["p0"])
            def ev_h3(e, s, w_QwT=w_QwT, w_H3=w_H3):
                e.tensor_tensor(out=w_H3[:], in0=w_QwT[:], in1=p[0][:],
                                op=ALU.subtract).then_inc(s, 1)
            st([("vector", ev_h3, 1)], rd=[P("QwT"), "p0"], wr=[P("H3")])

            for cc in range(KR):
                sl = slice(cc * C, (cc + 1) * C)
                hsl = slice(h * 128, (h + 1) * 128)
                def mm_sw(e, s, cc=cc, sl=sl, hsl=hsl, ugU=ugU, w_PmT=w_PmT,
                          w_H3=w_H3, w_McT=w_McT):
                    ii = e.matmul(p[1][:, sl], Pi_b[:, hsl], w_H3[:, sl],
                                  start=True, stop=True)
                    ii = e.matmul(p[0][:, sl], ugU(cc), w_PmT[:, sl],
                                  start=True, stop=False)
                    ii = e.matmul(p[0][:, sl], Qs_b[:, hsl], w_H3[:, sl],
                                  start=False, stop=True)
                    ii = e.matmul(p[2][:, 0:128], w_McT[:, sl], Qs_b[:, hsl],
                                  start=True, stop=True)
                    ii = e.matmul(p[2][:, 128:256], w_McT[:, sl], Pi_b[:, hsl],
                                  start=True, stop=True)
                    ii.then_inc(s, 1)
                st([("tensor", mm_sw, 1)],
                   rd=[f"Pi{h}", f"Qb{h}", P("UG"), P("PmT"), P("H3"), P("McT")],
                   wr=["p0", "p1", "p2"])
                def ev_sw(e, s, cc=cc, sl=sl, hsl=hsl, w_H5=w_H5, w_Ob=w_Ob, w_Nc=w_Nc):
                    e.tensor_tensor(out=Qs_f[:, hsl], in0=w_Nc[:, sl],
                                    in1=p[2][:, 0:128], op=ALU.add).then_inc(s, 1)
                    e.tensor_copy(out=Pi_b[:, hsl], in_=p[2][:, 128:256]).then_inc(s, 1)
                st([("vector", ev_sw, 2)], rd=["p2", P("Nc")], wr=[f"Qs{h}", f"Pi{h}"])
                def ev_sw_s(e, s, cc=cc, sl=sl, w_H5=w_H5, w_Ob=w_Ob):
                    e.copy(out=w_H5[:, sl], in_=p[1][:, sl]).then_inc(s, 1)
                    e.copy(out=w_Ob[:, sl], in_=p[0][:, sl]).then_inc(s, 1)
                st([("scalar", ev_sw_s, 2)], rd=["p0", "p1"], wr=[P("H5"), P("Ob")])
                def ev_sw2(e, s, hsl=hsl):
                    e.copy(out=Qs_b[:, hsl], in_=Qs_f[:, hsl]).then_inc(s, 1)
                st([("scalar", ev_sw2, 1)], rd=[f"Qs{h}"], wr=[f"Qb{h}"])
            def sp_w(e, s, hc0=hc0, w_Ob=w_Ob, w_H5=w_H5):
                dma(e, s, sp_ob[hc0 * 128:(hc0 + KR) * 128, :]
                    .rearrange("(cc p) t -> p cc t", p=128), w_Ob[:])
                dma(e, s, sp_h5[hc0 * 128:(hc0 + KR) * 128, :]
                    .rearrange("(cc p) t -> p cc t", p=128), w_H5[:])
            st([("gpsimd", sp_w, 32)], rd=[P("Ob"), P("H5")], wr=["spob", "sph5"])
        RUNS = ([1, 2, 2, 2, 1, 1, 2, 1, 1, 1, 1] + [4] * NEWTON
                + [2, 1, 2, 2, 2] + [4] * KR + [1])
        for h0 in range(0, NV, 2):
            qa, qb = [], []
            emit_head(h0, qa)
            emit_head(h0 + 1, qb)
            assert len(qa) == sum(RUNS), (len(qa), sum(RUNS))
            pos = 0
            for rl in RUNS:
                for q in (qa, qb):
                    for i in range(pos, pos + rl):
                        bld.step(q[i][0], rd=q[i][1], wr=q[i][2])
                pos += rl


    if STAGE == "A":
        bld.barrier()
        st([("gpsimd", lambda e, s: dma(e, s, outT[0:128, 0:RT], xtb[:, 0:RT]), 16)])
        return _finish(nc, bld)

    if STAGE == "B":
        def tap_b(e, s):
            if "ob" in taps:
                dma(e, s, taps["ob"], sp_ob)
            if "h5" in taps:
                dma(e, s, taps["h5"], sp_h5)
            if "qsf" in taps:
                dma(e, s, taps["qsf"], Qs_f[:])
            if "pib" in taps:
                dma(e, s, taps["pib"], Pi_b[:])
        bld.barrier()
        st([("gpsimd", tap_b, 64)])
        st([("gpsimd", lambda e, s: dma(e, s, outT[0:128, 0:RT], xtb[:, 0:RT]), 16)])
        return _finish(nc, bld)

    for h in range(NV):
        hsl = slice(h * 128, (h + 1) * 128)
        def mm_mt(e, s, hsl=hsl):
            ii = e.transpose(pb[:, 0:128], Pi_b[:, hsl], c_identb[:])
            ii.then_inc(s, 1)
        st([("tensor", mm_mt, 1)], rd=[f"Pi{h}"], wr=["pb"])
        wOb = w_Ob2[h % 2]
        def ev_mt(e, s, hsl=hsl, wOb=wOb):
            e.tensor_copy(out=wOb[:, 0:128], in_=pb[:, 0:128]).then_inc(s, 1)
            e.tensor_copy(out=wOb[:, 128:256], in_=Qs_f[:, hsl]).then_inc(s, 1)
        st([("vector", ev_mt, 2)], rd=["pb", f"Qs{h}"], wr=[f"Ob{h % 2}"])
        def st_mt(e, s, h=h, wOb=wOb):
            dma(e, s, cc_in[h * 128:(h + 1) * 128, :], wOb[:, 0:128])
            dma(e, s, cc_in[(NV + h) * 128:(NV + h + 1) * 128, :], wOb[:, 128:256])
        st([("gpsimd", st_mt, 32)], rd=[f"Ob{h % 2}"], wr=["ccin"])
    if SIM1:
        def coll(e, s):
            for j in range(NCORE):
                dma(e, s, cc_out[j * 2 * NV * 128:(j + 1) * 2 * NV * 128, :], cc_in)
        st([("gpsimd", coll, 16 * NCORE)], rd=["ccin"], wr=["ccout"])
    else:
        def coll(e, s):
            e.collective_compute("AllGather", ALU.bypass,
                                 replica_groups=[list(range(NCORE))],
                                 ins=[cc_in.opt()], outs=[cc_out.opt()]).then_inc(s)
        st([("gpsimd", coll, 1, "c")], rd=["ccin"], wr=["ccout"])

    bld.barrier()
    for g in reversed(ab_guards):
        g.__exit__(None, None, None)
    Sin = sb("Sin", [128, NV * 128], BF16)
    gat = sb("gat", [128, NV * 512], BF16)
    xres = sb("xres", [128, 512], F32)
    sums = sb("sums", [1, 512], F32)
    sumss = sb("sumss", [1, 512], F32)
    rstd = sb("rstd", [1, 512], F32)
    obt = sb("obt", [128, 512], BF16)
    h5t = sb("h5t", [128, 512], BF16)
    sq2 = sb("sq2", [128, 512], BF16)
    x1t = sb("x1t", [128, 512], F32)
    csq = sb("csq", [128, 512], F32)
    routb = sb("routb", [128, NEXP * NDC], BF16)
    wtok = sb("wtok", [128, 16 * 8], F32)
    wtokb = sb("wtokb", [128, 16 * 8], BF16)
    wrow = sb("wrow", [1, TPC], BF16)
    lg8 = sb("lg8", [128, 8], F32)
    pr8 = sb("pr8", [128, 8], F32)
    m18 = sb("m18", [128, 1], F32)
    m28 = sb("m28", [128, 1], F32)
    ssum = sb("ssum", [128, 1], F32)
    sug = sb("sug", [128, 512], F32)
    fot = sb("fot", [128, 512], F32)
    pc_guards = []
    def sbc(n, s, d):
        g = nc.sbuf_tensor(n, s, d)
        ap = g.__enter__()
        pc_guards.append(g)
        return ap
    MtJ = sbc("MtJ", [128, NV * 128], BF16)
    NJ = sbc("NJ", [128, NV * 128], BF16)
    Sacc = sbc("Sacc", [128, NV * 128], BF16)

    def init_sin(e, s):
        e.memset(Sacc[:], 0.0).then_inc(s, 1)
        e.memset(Sin[:], 0.0).then_inc(s, 1)
    st([("vector", init_sin, 2)], wr=["SaccA", "SinA"])
    for j in range(NCORE - 1):
        def ld_mj(e, s, j=j):
            dma(e, s, MtJ[:], cc_out[j * 8192:(j * 8192 + 4096), :]
                .rearrange("(h p) c -> p h c", p=128))
            dma(e, s, NJ[:], cc_out[(j * 8192 + 4096):(j + 1) * 8192, :]
                .rearrange("(h p) c -> p h c", p=128))
        st([("gpsimd", ld_mj, 32)], rd=["ccout"] + [f"Sc{h}" for h in range(NV)], wr=["MtJ", "NJ"])
        for h in range(NV):
            hsl = slice(h * 128, (h + 1) * 128)
            def mm_pc(e, s, hsl=hsl):
                ii = e.matmul(p[2][:, 0:128], MtJ[:, hsl], Sacc[:, hsl],
                         start=True, stop=True).then_inc(s, 1)
            st([("tensor", mm_pc, 1)], rd=["MtJ", f"Sc{h}", "SaccA"], wr=["p2"])
            def ev_pc(e, s, hsl=hsl):
                e.tensor_tensor(out=Sacc[:, hsl], in0=NJ[:, hsl], in1=p[2][:, 0:128],
                                op=ALU.add).then_inc(s, 1)
            st([("vector", ev_pc, 1)], rd=["NJ", "p2", "SaccA"], wr=[f"Sc{h}"])
            def ev_pc2(e, s, hsl=hsl, j=j):
                e.scalar_tensor_tensor(out=Sin[:, hsl], in0=Sacc[:, hsl],
                                       scalar=c_onehot[:, j + 1: j + 2],
                                       in1=Sin[:, hsl], op0=ALU.mult,
                                       op1=ALU.add).then_inc(s, 1)
            st([("vector", ev_pc2, 1)], rd=[f"Sc{h}", "SinA"], wr=[f"Si{h}"])

    bld.barrier()
    for g in reversed(pc_guards):
        g.__exit__(None, None, None)
    sut2 = [sb("sutA", [128, NFB * 512], BF16), sb("sutB", [128, NFB * 512], BF16)]
    wgu_g = nc.sbuf_tensor("wguA", [128, 2 * 2048], BF16)
    wgu_g2 = nc.sbuf_tensor("wguB", [128, 2 * 2048], BF16)
    wgu2 = [wgu_g.__enter__(), wgu_g2.__enter__()]
    if "sin" in taps:
        st([("gpsimd", lambda e, s: dma(e, s, taps["sin"], Sin[:]), 16)],
           rd=[f"Si{h}" for h in range(NV)])

    for tt in range(NR):
        for vc in range(NV):
            def ld_oh(e, s, vc=vc, tt=tt):
                base = vc * NCH + tt * KR
                dma(e, s, obt[:], sp_ob[base * 128:(base + KR) * 128, :]
                    .rearrange("(cc p) t -> p cc t", p=128))
                dma(e, s, h5t[:], sp_h5[base * 128:(base + KR) * 128, :]
                    .rearrange("(cc p) t -> p cc t", p=128))
            st([("gpsimd", ld_oh, 32)], rd=["spob", "sph5"], wr=["obt", "h5t"])
            def mm_corr(e, s, vc=vc):
                for cc in range(KR):
                    sl = slice(cc * C, (cc + 1) * C)
                    ii = e.matmul(p[3][:, sl], Sin[:, vc * 128:(vc + 1) * 128],
                             h5t[:, sl], start=True, stop=True)
                ii.then_inc(s, 1)
            st([("tensor", mm_corr, 1)], rd=[f"Si{vc}", "h5t"], wr=["p3"])
            def ev_corr(e, s, vc=vc):
                e.tensor_tensor(out=gat[:, vc * RT:(vc + 1) * RT], in0=obt[:],
                                in1=p[3][:], op=ALU.add).then_inc(s, 1)
            st([("vector", ev_corr, 1)], rd=["obt", "p3"], wr=[f"gat{vc}"])
        for vc in range(NV):
            def sqv(e, s, vc=vc):
                e.tensor_tensor(out=sq2[:], in0=gat[:, vc * RT:(vc + 1) * RT],
                                in1=gat[:, vc * RT:(vc + 1) * RT],
                                op=ALU.mult).then_inc(s, 1)
            st([("vector", sqv, 1)], rd=[f"gat{vc}"], wr=["sq2"])
            def mm_sq(e, s, vc=vc):
                ii = e.matmul(p[5][0:1, 0:512], c_onescb[:], sq2[:],
                         start=(vc == 0), stop=(vc == NV - 1)).then_inc(s, 1)
            st([("tensor", mm_sq, 1)], rd=["sq2"], wr=["p5"])
        def rstd1(e, s):
            e.activation(out=rstd[:], in_=p[5][0:1, 0:512], func=AF.Sqrt,
                         bias=1e-6, scale=1.0 / (NV * DV)).then_inc(s, 1)
        st([("scalar", rstd1, 1)], rd=["p5"], wr=["rstd"])
        st([("vector", lambda e, s: e.reciprocal(out=rstd[:],
                                                 in_=rstd[:]).then_inc(s, 1), 1)],
           rd=["rstd"], wr=["rstd"])
        def mm_rb2(e, s):
            ii = e.matmul(p[5][:, 0:512], c_onesf[:], rstd[:],
                     start=True, stop=True).then_inc(s, 1)
        st([("tensor", mm_rb2, 1)], rd=["rstd"], wr=["p5"])
        def ld_xc(e, s, tt=tt):
            for k in range(NDC):
                dma(e, s, xtb[:, k * RT:(k + 1) * RT],
                    xT[k * 128:(k + 1) * 128, 3 + tt * RT: 3 + (tt + 1) * RT])
        st([("gpsimd", ld_xc, 16 * NDC)], wr=["xtb"])
        for vc in range(NV):
            buf = vc % 2
            def ld_wz(e, s, vc=vc, buf=buf):
                dma(e, s, wjb[:, buf * 2048:(buf + 1) * 2048],
                    WzT[:, vc * 128:(vc + 1) * 128]
                    .rearrange("(k p) j -> p k j", p=128))
            st([("gpsimd", ld_wz, 16)], wr=[f"wjb{buf}"])
            def mm_z(e, s, buf=buf):
                for k in range(NDC):
                    ii = e.matmul(p[4][:],
                             wjb[:, buf * 2048 + k * 128: buf * 2048 + (k + 1) * 128],
                             xtb[:, k * RT:(k + 1) * RT],
                             start=(k == 0), stop=(k == NDC - 1))
                ii.then_inc(s, 1)
            st([("tensor", mm_z, 1)], rd=[f"wjb{buf}", "xtb"], wr=["p4"])
            st([("scalar", lambda e, s: e.activation(
                out=csq[:], in_=p[4][:], func=AF.Silu).then_inc(s, 1), 1)],
               rd=["p4"], wr=["csq"])
            def gate2(e, s, vc=vc):
                e.scalar_tensor_tensor(out=gat[:, vc * RT:(vc + 1) * RT],
                                       in0=gat[:, vc * RT:(vc + 1) * RT],
                                       scalar=c_normw[:, vc:vc + 1],
                                       in1=p[5][:, 0:512], op0=ALU.mult,
                                       op1=ALU.mult).then_inc(s, 1)
            st([("vector", gate2, 1)], rd=[f"gat{vc}", "p5"], wr=[f"gat{vc}"])
            def gate3(e, s, vc=vc):
                e.tensor_tensor(out=gat[:, vc * RT:(vc + 1) * RT],
                                in0=gat[:, vc * RT:(vc + 1) * RT],
                                in1=csq[:], op=ALU.mult).then_inc(s, 1)
            st([("vector", gate3, 1)], rd=[f"gat{vc}", "csq"], wr=[f"gat{vc}"])
        if ("gat_t%d" % tt) in taps:
            st([("gpsimd", lambda e, s, tt=tt: dma(e, s, taps["gat_t%d" % tt],
                                                   gat[:]), 16)],
               rd=[f"gat{v}" for v in range(NV)])
        for dc in range(NDC):
            wob = wjb if dc % 2 == 0 else wgu2[0]
            wobn = ["wjb0", "wjb1"] if dc % 2 == 0 else ["wgu0"]
            def ld_wo(e, s, dc=dc, wob=wob):
                dma(e, s, wob[:, 0:2048], WoutT[:, dc * 128:(dc + 1) * 128]
                    .rearrange("(v p) j -> p v j", p=128)[:, 0:16, :])
                dma(e, s, wob[:, 2048:4096], WoutT[:, dc * 128:(dc + 1) * 128]
                    .rearrange("(v p) j -> p v j", p=128)[:, 16:32, :])
            st([("gpsimd", ld_wo, 32)], wr=wobn)
            def ld_xr(e, s, dc=dc, tt=tt):
                dma(e, s, xres[:], xT[dc * 128:(dc + 1) * 128,
                                      3 + tt * RT: 3 + (tt + 1) * RT])
            st([("sync", ld_xr, 16)], wr=["xres"])
            def mm_wo(e, s, wob=wob, dc=dc):
                for vc in range(NV):
                    ii = e.matmul(p[4 - dc % 2][:], wob[:, vc * 128:(vc + 1) * 128],
                             gat[:, vc * RT:(vc + 1) * RT],
                             start=(vc == 0), stop=(vc == NV - 1))
                ii.then_inc(s, 1)
            st([("tensor", mm_wo, 1)], rd=wobn + [f"gat{v}" for v in range(NV)],
               wr=[f"p{4 - dc % 2}"])
            def ev_wo(e, s, dc=dc):
                e.tensor_tensor(out=x1t[:], in0=xres[:], in1=p[4 - dc % 2][:],
                                op=ALU.add).then_inc(s, 1)
            st([("vector", ev_wo, 1)], rd=["xres", f"p{4 - dc % 2}"], wr=["x1t"])
            st([("vector", lambda e, s: e.tensor_copy(out=sq2[:],
                                                      in_=x1t[:]).then_inc(s, 1), 1)],
               rd=["x1t"], wr=["sq2"])
            def spw(e, s, dc=dc, tt=tt):
                dma(e, s, sp_x1[dc * 128:(dc + 1) * 128, tt * RT:(tt + 1) * RT], x1t[:])
            st([("gpsimd", spw, 16)], rd=["x1t"], wr=["spx1"])
            def mm_st1(e, s, dc=dc):
                ii = e.matmul(p[5][0:1, 0:512], c_onescb[:], sq2[:],
                         start=(dc == 0), stop=(dc == NDC - 1)).then_inc(s, 1)
            st([("tensor", mm_st1, 1)], rd=["sq2"], wr=["p5"])
            def sq_x(e, s):
                e.tensor_tensor(out=csq[:], in0=sq2[:], in1=sq2[:],
                                op=ALU.mult).then_inc(s, 1)
            st([("vector", sq_x, 1)], rd=["sq2"], wr=["csq"])
            def mm_st2(e, s, dc=dc):
                ii = e.matmul(p[2][0:1, 0:512], c_trii[:, 127:128], csq[:],
                         start=(dc == 0), stop=(dc == NDC - 1)).then_inc(s, 1)
            st([("tensor", mm_st2, 1)], rd=["csq"], wr=["p2"])
        def ln_f1(e, s):
            e.tensor_scalar(out=sums[:], in0=p[5][0:1, 0:512], scalar1=1.0 / D,
                            scalar2=None, op0=ALU.mult).then_inc(s, 1)
            e.tensor_scalar(out=sumss[:], in0=p[2][0:1, 0:512], scalar1=1.0 / D,
                            scalar2=None, op0=ALU.mult).then_inc(s, 1)
        st([("vector", ln_f1, 2)], rd=["p5", "p2"], wr=["sums", "sumss"])
        st([("vector", lambda e, s: e.tensor_tensor(out=rstd[:], in0=sums[:],
                                                    in1=sums[:],
                                                    op=ALU.mult).then_inc(s, 1), 1)],
           rd=["sums"], wr=["rstd"])
        st([("vector", lambda e, s: e.tensor_tensor(out=rstd[:], in0=sumss[:],
                                                    in1=rstd[:],
                                                    op=ALU.subtract).then_inc(s, 1), 1)],
           rd=["sumss", "rstd"], wr=["rstd"])
        st([("scalar", lambda e, s: e.activation(out=rstd[:], in_=rstd[:],
                                                 func=AF.Sqrt, bias=1e-5
                                                 ).then_inc(s, 1), 1)],
           rd=["rstd"], wr=["rstd"])
        st([("vector", lambda e, s: e.reciprocal(out=rstd[:],
                                                 in_=rstd[:]).then_inc(s, 1), 1)],
           rd=["rstd"], wr=["rstd"])
        def mm_lb2(e, s):
            ii = e.matmul(p[4][:, 0:512], c_onesf[:], sums[:], start=True, stop=True)
            ii = e.matmul(p[3][:, 0:512], c_onesf[:], rstd[:], start=True, stop=True)
            ii.then_inc(s, 1)
        st([("tensor", mm_lb2, 1)], rd=["sums", "rstd"], wr=["p4", "p3"])
        for dc in range(NDC):
            def ld_x1(e, s, dc=dc, tt=tt):
                dma(e, s, xres[:], sp_x1[dc * 128:(dc + 1) * 128,
                                         tt * RT:(tt + 1) * RT])
            st([("sync", ld_x1, 16)], rd=["spx1"], wr=["xres"])
            def nx1(e, s):
                e.tensor_tensor(out=x1t[:], in0=xres[:], in1=p[4][:, 0:512],
                                op=ALU.subtract).then_inc(s, 1)
            st([("vector", nx1, 1)], rd=["xres", "p4"], wr=["x1t"])
            def nx2(e, s):
                e.tensor_tensor(out=x1t[:], in0=x1t[:], in1=p[3][:, 0:512],
                                op=ALU.mult).then_inc(s, 1)
            st([("vector", nx2, 1)], rd=["x1t", "p3"], wr=["x1t"])
            def spw2(e, s, dc=dc, tt=tt):
                dma(e, s, sp_x1[dc * 128:(dc + 1) * 128, tt * RT:(tt + 1) * RT], x1t[:])
            st([("gpsimd", spw2, 16)], rd=["x1t"], wr=["spx1"])
            st([("vector", lambda e, s: e.tensor_copy(out=sq2[:],
                                                      in_=x1t[:]).then_inc(s, 1), 1)],
               rd=["x1t"], wr=["sq2"])
            def sq_y(e, s):
                e.tensor_tensor(out=csq[:], in0=sq2[:], in1=sq2[:],
                                op=ALU.mult).then_inc(s, 1)
            st([("vector", sq_y, 1)], rd=["sq2"], wr=["csq"])
            def mm_st3(e, s, dc=dc):
                ii = e.matmul(p[2][0:1, 0:512], c_trii[:, 127:128], csq[:],
                         start=(dc == 0), stop=(dc == NDC - 1)).then_inc(s, 1)
            st([("tensor", mm_st3, 1)], rd=["csq"], wr=["p2"])
        st([("scalar", lambda e, s: e.activation(out=rstd[:], in_=p[2][0:1, 0:512],
                                                 func=AF.Sqrt, bias=1e-6,
                                                 scale=1.0 / D).then_inc(s, 1), 1)],
           rd=["p2"], wr=["rstd"])
        st([("vector", lambda e, s: e.reciprocal(out=rstd[:],
                                                 in_=rstd[:]).then_inc(s, 1), 1)],
           rd=["rstd"], wr=["rstd"])
        def mm_rb3(e, s):
            ii = e.matmul(p[3][:, 0:512], c_onesf[:], rstd[:],
                     start=True, stop=True).then_inc(s, 1)
        st([("tensor", mm_rb3, 1)], rd=["rstd"], wr=["p3"])
        for dc in range(NDC):
            def ld_x2(e, s, dc=dc, tt=tt):
                dma(e, s, xres[:], sp_x1[dc * 128:(dc + 1) * 128,
                                         tt * RT:(tt + 1) * RT])
            st([("sync", ld_x2, 16)], rd=["spx1"], wr=["xres"])
            def hmk(e, s, dc=dc, tt=tt):
                e.scalar_tensor_tensor(
                    out=qkv[:, (dc * NR + tt) * RT:(dc * NR + tt + 1) * RT],
                    in0=xres[:], scalar=c_rmsw[:, dc:dc + 1], in1=p[3][:, 0:512],
                    op0=ALU.mult, op1=ALU.mult).then_inc(s, 1)
            st([("vector", hmk, 1)], rd=["xres", "p3"], wr=[f"qk{dc * NR + tt}"])

    def htile(dc, tt):
        return qkv[:, (dc * NR + tt) * RT:(dc * NR + tt + 1) * RT]

    if "hfull" in taps:
        st([("gpsimd", lambda e, s: dma(e, s, taps["hfull"], qkv[:]), 16)],
           rd=[f"qk{j}" for j in range(NJB)])

    st([("gpsimd", lambda e, s: dma(e, s, routb[:],
                                    routw.rearrange("(k p) e -> p k e", p=128)), 16)],
       wr=["routb"])
    for tg in range(16):
        tt, cc = tg // KR, tg % KR
        def mm_ro(e, s, tt=tt, cc=cc):
            for k in range(NDC):
                ii = e.matmul(p[0][:, 0:8], htile(k, tt)[:, cc * C:(cc + 1) * C],
                         routb[:, k * 8:(k + 1) * 8],
                         start=(k == 0), stop=(k == NDC - 1))
            ii.then_inc(s, 1)
        st([("tensor", mm_ro, 1)], rd=["routb"] + [f"qk{k * NR + tt}" for k in range(NDC)], wr=["p0"])
        def ro1(e, s):
            e.reduce_max(out=m18[:], in_=p[0][:, 0:8], axis=AX.X).then_inc(s, 1)
        st([("vector", ro1, 1)], rd=["p0"], wr=["m18"])
        def ro1b(e, s):
            e.tensor_scalar(out=m28[:], in0=m18[:], scalar1=-1.0, scalar2=None,
                            op0=ALU.mult).then_inc(s, 1)
        st([("vector", ro1b, 1)], rd=["m18"], wr=["m28"])
        def ro2(e, s):
            e.activation(out=lg8[:], in_=p[0][:, 0:8], func=AF.Exp,
                         bias=m28[:]).then_inc(s, 1)
        st([("scalar", ro2, 1)], rd=["p0", "m28"], wr=["lg8"])
        st([("vector", lambda e, s: e.reduce_sum(out=ssum[:], in_=lg8[:],
                                                 axis=AX.X).then_inc(s, 1), 1)],
           rd=["lg8"], wr=["ssum"])
        st([("vector", lambda e, s: e.reciprocal(out=ssum[:],
                                                 in_=ssum[:]).then_inc(s, 1), 1)],
           rd=["ssum"], wr=["ssum"])
        st([("vector", lambda e, s: e.tensor_scalar(
            out=pr8[:], in0=lg8[:], scalar1=ssum[:], scalar2=None,
            op0=ALU.mult).then_inc(s, 1), 1)],
           rd=["lg8", "ssum"], wr=["pr8"])
        st([("vector", lambda e, s: e.reduce_max(out=m18[:], in_=pr8[:],
                                                 axis=AX.X).then_inc(s, 1), 1)],
           rd=["pr8"], wr=["m18"])
        def ro7(e, s):
            e.tensor_scalar(out=lg8[:], in0=pr8[:], scalar1=m18[:], scalar2=-2.0,
                            op0=ALU.is_ge, op1=ALU.mult).then_inc(s, 1)
        st([("vector", ro7, 1)], rd=["pr8", "m18"], wr=["lg8"])
        st([("vector", lambda e, s: e.tensor_tensor(out=lg8[:], in0=pr8[:],
                                                    in1=lg8[:],
                                                    op=ALU.add).then_inc(s, 1), 1)],
           rd=["pr8", "lg8"], wr=["lg8"])
        st([("vector", lambda e, s: e.reduce_max(out=m28[:], in_=lg8[:],
                                                 axis=AX.X).then_inc(s, 1), 1)],
           rd=["lg8"], wr=["m28"])
        def ro10(e, s):
            e.tensor_scalar(out=lg8[:], in0=pr8[:], scalar1=m28[:], scalar2=None,
                            op0=ALU.is_ge).then_inc(s, 1)
        st([("vector", ro10, 1)], rd=["pr8", "m28"], wr=["lg8"])
        st([("vector", lambda e, s: e.tensor_tensor(out=m18[:], in0=m18[:],
                                                    in1=m28[:],
                                                    op=ALU.add).then_inc(s, 1), 1)],
           rd=["m18", "m28"], wr=["m18"])
        st([("vector", lambda e, s: e.reciprocal(out=m18[:],
                                                 in_=m18[:]).then_inc(s, 1), 1)],
           rd=["m18"], wr=["m18"])
        st([("vector", lambda e, s: e.tensor_tensor(out=lg8[:], in0=pr8[:],
                                                    in1=lg8[:],
                                                    op=ALU.mult).then_inc(s, 1), 1)],
           rd=["pr8", "lg8"], wr=["lg8"])
        def ro14(e, s, tg=tg):
            e.tensor_scalar(out=wtok[:, tg * 8:(tg + 1) * 8], in0=lg8[:],
                            scalar1=m18[:], scalar2=None,
                            op0=ALU.mult).then_inc(s, 1)
        st([("vector", ro14, 1)], rd=["lg8", "m18"], wr=["wtok"])
    st([("vector", lambda e, s: e.tensor_copy(out=wtokb[:],
                                              in_=wtok[:]).then_inc(s, 1), 1)],
       rd=["wtok"], wr=["wtokb"])
    def mm_wt(e, s):
        ii = e.transpose(pb[:, 0:128], wtokb[:], c_identb[:])
        ii.then_inc(s, 1)
    st([("tensor", mm_wt, 1)], rd=["wtokb"], wr=["pb"])
    st([("vector", lambda e, s: e.tensor_copy(out=wtokb[:],
                                              in_=pb[:, 0:128]).then_inc(s, 1), 1)],
       rd=["pb"], wr=["wtokb"])
    def wr_w(e, s):
        dma(e, s, arena_wr, wtokb[:])
    st([("gpsimd", wr_w, 16)], rd=["wtokb"], wr=["Awr"])
    if "wrow" in taps:
        def tap_wr(e, s):
            dma(e, s, taps["wrow"], arena_wr)
        st([("gpsimd", tap_wr, 16)], rd=["Awr"])

    for e_i in range(NEXP + 1):
        if e_i < NEXP:
            def ld_wrow(e, s, e_i=e_i):
                dma(e, s, wrow[0:1, :],
                    arena_wr.rearrange("(tg e) t -> e tg t", e=8)[e_i:e_i + 1, :, :])
            st([("gpsimd", ld_wrow, 16)], rd=["Awr"], wr=["wrow"])
        for fb in range(NFB):
            gbuf = (e_i * NFB + fb) % 2
            wgu = wgu2[gbuf]
            def ld_wgu(e, s, e_i=e_i, fb=fb, wgu=wgu):
                if e_i < NEXP:
                    src_g = wg[e_i, :, fb * 128:(fb + 1) * 128]
                    src_u = wu[e_i, :, fb * 128:(fb + 1) * 128]
                else:
                    src_g = swg[:, fb * 128:(fb + 1) * 128]
                    src_u = swu[:, fb * 128:(fb + 1) * 128]
                dma(e, s, wgu[:, 0:2048], src_g.rearrange("(k p) j -> p k j", p=128))
                dma(e, s, wgu[:, 2048:4096], src_u.rearrange("(k p) j -> p k j", p=128))
            st([("gpsimd", ld_wgu, 32)], wr=[f"wgu{gbuf}"])
            for tt in range(NR):
                def mm_gu(e, s, tt=tt, wgu=wgu):
                    for k in range(NDC):
                        ii = e.matmul(p[0][:], wgu[:, k * 128:(k + 1) * 128],
                                 htile(k, tt), start=(k == 0), stop=(k == NDC - 1))
                    for k in range(NDC):
                        ii = e.matmul(p[1][:], wgu[:, 2048 + k * 128: 2048 + (k + 1) * 128],
                                 htile(k, tt), start=(k == 0), stop=(k == NDC - 1))
                    ii.then_inc(s, 1)
                st([("tensor", mm_gu, 1)], rd=[f"wgu{gbuf}"] + [f"qk{k * NR + tt}" for k in range(NDC)], wr=["p0", "p1"])
                st([("scalar", lambda e, s: e.activation(
                    out=sug[:], in_=p[0][:], func=AF.Silu).then_inc(s, 1), 1)],
                   rd=["p0"], wr=["sug"])
                if e_i < NEXP:
                    def mm_wb2(e, s, tt=tt):
                        ii = e.matmul(p[2][:, 0:512], c_onesb[:],
                                 wrow[0:1, tt * RT:(tt + 1) * RT],
                                 start=True, stop=True).then_inc(s, 1)
                    st([("tensor", mm_wb2, 1)], rd=["wrow"], wr=["p2"])
                    def ev_gu2(e, s):
                        e.tensor_tensor(out=sug[:], in0=sug[:], in1=p[1][:],
                                        op=ALU.mult).then_inc(s, 1)
                    st([("vector", ev_gu2, 1)], rd=["sug", "p1"], wr=["sug"])
                    def ev_gu3(e, s):
                        e.tensor_tensor(out=sq2[:], in0=sug[:], in1=p[2][:, 0:512],
                                        op=ALU.mult).then_inc(s, 1)
                    st([("vector", ev_gu3, 1)], rd=["sug", "p2"], wr=["sq2"])
                else:
                    def ev_gu2s(e, s):
                        e.tensor_tensor(out=sq2[:], in0=sug[:], in1=p[1][:],
                                        op=ALU.mult).then_inc(s, 1)
                    st([("vector", ev_gu2s, 1)], rd=["sug", "p1"], wr=["sq2"])
                def sp_su_w(e, s, e_i=e_i, fb=fb, tt=tt):
                    dma(e, s, sp_su[(e_i * NFB + fb) * 128:(e_i * NFB + fb + 1) * 128,
                                    tt * RT:(tt + 1) * RT], sq2[:])
                st([("gpsimd", sp_su_w, 16)], rd=["sq2"], wr=["spsu"])

    bld.barrier()
    wgu_g2.__exit__(None, None, None)
    wgu_g.__exit__(None, None, None)
    wdb2 = [sb("wdbA", [128, 4096], BF16), sb("wdbB", [128, 4096], BF16)]
    pbg.__exit__(None, None, None)
    p7 = psum("p7", [128, 512], F32)
    pall = p + [p7]
    for half in range(2):
        for tt in range(NR):
            for e_i in range(NEXP + 1):
                sbuf_i = e_i % 2
                sut = sut2[sbuf_i]
                def ld_su(e, s, e_i=e_i, tt=tt, sut=sut):
                    dma(e, s, sut[:], sp_su[e_i * NFB * 128:(e_i + 1) * NFB * 128,
                                            tt * RT:(tt + 1) * RT]
                        .rearrange("(f p) t -> p f t", p=128))
                st([("gpsimd", ld_su, 16)], rd=["spsu"], wr=[f"sut{sbuf_i}"])
                for fh in range(2):
                    dbuf = (e_i * 2 + fh) % 2
                    wdb = wdb2[dbuf]
                    def ld_wd(e, s, e_i=e_i, half=half, fh=fh, wdb=wdb):
                        src_ = (wd[e_i, fh * 512:(fh + 1) * 512,
                                   half * 1024:(half + 1) * 1024] if e_i < NEXP
                                else swd[fh * 512:(fh + 1) * 512,
                                         half * 1024:(half + 1) * 1024])
                        dma(e, s, wdb[:], src_.rearrange("(f p) j -> p f j", p=128))
                    st([("gpsimd", ld_wd, 16)], wr=[f"wdb{dbuf}"])
                    def mm_dn(e, s, e_i=e_i, fh=fh, wdb=wdb, sut=sut):
                        for dc in range(8):
                            for f in range(4):
                                ii = e.matmul(pall[dc][:],
                                         wdb[:, (f * 8 + dc) * 128:(f * 8 + dc + 1) * 128],
                                         sut[:, (fh * 4 + f) * RT:(fh * 4 + f + 1) * RT],
                                         start=(e_i == 0 and fh == 0 and f == 0),
                                         stop=(e_i == NEXP and fh == 1 and f == 3))
                        ii.then_inc(s, 1)
                    st([("tensor", mm_dn, 1)], rd=[f"wdb{dbuf}", f"sut{sbuf_i}"], wr=[f"pd{i}" for i in range(8)])
            for dc in range(8):
                def ev_dn(e, s, dc=dc):
                    e.tensor_copy(out=fot[:], in_=pall[dc][:]).then_inc(s, 1)
                st([("vector", ev_dn, 1)], rd=[f"pd{dc}"], wr=["fot"])
                def st_dn(e, s, dc=dc, half=half, tt=tt):
                    dcg = half * 8 + dc
                    dma(e, s, sp_ffn[dcg * 128:(dcg + 1) * 128,
                                     tt * RT:(tt + 1) * RT], fot[:])
                st([("gpsimd", st_dn, 16)], rd=["fot"], wr=["spffn"])

    for tt in range(NR):
        for dc in range(NDC):
            def ld_f(e, s, dc=dc, tt=tt):
                dma(e, s, xres[:], sp_x1[dc * 128:(dc + 1) * 128,
                                         tt * RT:(tt + 1) * RT])
                dma(e, s, fot[:], sp_ffn[dc * 128:(dc + 1) * 128,
                                         tt * RT:(tt + 1) * RT])
            st([("sync", ld_f, 32)], rd=["spx1", "spffn"], wr=["xres", "fot"])
            def add_f(e, s):
                e.tensor_tensor(out=x1t[:], in0=xres[:], in1=fot[:],
                                op=ALU.add).then_inc(s, 1)
            st([("vector", add_f, 1)], rd=["xres", "fot"], wr=["x1t"])
            def spf(e, s, dc=dc, tt=tt):
                dma(e, s, sp_ffn[dc * 128:(dc + 1) * 128,
                                 tt * RT:(tt + 1) * RT], x1t[:])
            st([("gpsimd", spf, 16)], rd=["x1t"], wr=["spffn"])
            st([("vector", lambda e, s: e.tensor_copy(out=sq2[:],
                                                      in_=x1t[:]).then_inc(s, 1), 1)],
               rd=["x1t"], wr=["sq2"])
            def mm_sf1(e, s, dc=dc):
                ii = e.matmul(p[5][0:1, 0:512], c_onescb[:], sq2[:],
                         start=(dc == 0), stop=(dc == NDC - 1)).then_inc(s, 1)
            st([("tensor", mm_sf1, 1)], rd=["sq2"], wr=["p5"])
            def sq_f(e, s):
                e.tensor_tensor(out=csq[:], in0=sq2[:], in1=sq2[:],
                                op=ALU.mult).then_inc(s, 1)
            st([("vector", sq_f, 1)], rd=["sq2"], wr=["csq"])
            def mm_sf2(e, s, dc=dc):
                ii = e.matmul(p[2][0:1, 0:512], c_trii[:, 127:128], csq[:],
                         start=(dc == 0), stop=(dc == NDC - 1)).then_inc(s, 1)
            st([("tensor", mm_sf2, 1)], rd=["csq"], wr=["p2"])
        def ln2_f1(e, s):
            e.tensor_scalar(out=sums[:], in0=p[5][0:1, 0:512], scalar1=1.0 / D,
                            scalar2=None, op0=ALU.mult).then_inc(s, 1)
            e.tensor_scalar(out=sumss[:], in0=p[2][0:1, 0:512], scalar1=1.0 / D,
                            scalar2=None, op0=ALU.mult).then_inc(s, 1)
        st([("vector", ln2_f1, 2)], rd=["p5", "p2"], wr=["sums", "sumss"])
        st([("vector", lambda e, s: e.tensor_tensor(out=rstd[:], in0=sums[:],
                                                    in1=sums[:],
                                                    op=ALU.mult).then_inc(s, 1), 1)],
           rd=["sums"], wr=["rstd"])
        st([("vector", lambda e, s: e.tensor_tensor(out=rstd[:], in0=sumss[:],
                                                    in1=rstd[:],
                                                    op=ALU.subtract).then_inc(s, 1), 1)],
           rd=["sumss", "rstd"], wr=["rstd"])
        st([("scalar", lambda e, s: e.activation(out=rstd[:], in_=rstd[:],
                                                 func=AF.Sqrt,
                                                 bias=1e-5).then_inc(s, 1), 1)],
           rd=["rstd"], wr=["rstd"])
        st([("vector", lambda e, s: e.reciprocal(out=rstd[:],
                                                 in_=rstd[:]).then_inc(s, 1), 1)],
           rd=["rstd"], wr=["rstd"])
        def mm_lbf(e, s):
            ii = e.matmul(p[4][:, 0:512], c_onesf[:], sums[:], start=True, stop=True)
            ii = e.matmul(p[3][:, 0:512], c_onesf[:], rstd[:], start=True, stop=True)
            ii.then_inc(s, 1)
        st([("tensor", mm_lbf, 1)], rd=["sums", "rstd"], wr=["p4", "p3"])
        for dc in range(NDC):
            def ld_f2(e, s, dc=dc, tt=tt):
                dma(e, s, fot[:], sp_ffn[dc * 128:(dc + 1) * 128,
                                         tt * RT:(tt + 1) * RT])
            st([("sync", ld_f2, 16)], rd=["spffn"], wr=["fot"])
            def of1(e, s):
                e.tensor_tensor(out=x1t[:], in0=fot[:], in1=p[4][:, 0:512],
                                op=ALU.subtract).then_inc(s, 1)
            st([("vector", of1, 1)], rd=["fot", "p4"], wr=["x1t"])
            def of2(e, s):
                e.tensor_tensor(out=x1t[:], in0=x1t[:], in1=p[3][:, 0:512],
                                op=ALU.mult).then_inc(s, 1)
            st([("vector", of2, 1)], rd=["x1t", "p3"], wr=["x1t"])
            def of3(e, s, dc=dc, tt=tt):
                dma(e, s, outT[dc * 128:(dc + 1) * 128, tt * RT:(tt + 1) * RT], x1t[:])
            st([("gpsimd", of3, 16)], rd=["x1t"], wr=["outT"])

    return _finish(nc, bld)


def _finish(nc, bld):
    for si, val in sorted(bld.cnt.items()):
        if val > 0:
            bld.prog["gpsimd"].append(
                lambda e, si=si, val=val: e.wait_ge(bld.sems[si], val))

    def run_for(name):
        def _f(e):
            for fn in bld.prog[name]:
                fn(e)
        return _f
    with nc.Block() as block:
        block.sync(run_for("sync"))
        block.tensor(run_for("tensor"))
        block.vector(run_for("vector"))
        block.scalar(run_for("scalar"))
        block.gpsimd(run_for("gpsimd"))
    return nc


def _prep_inputs(inputs):
    f32 = np.float32
    x = np.ascontiguousarray(np.asarray(inputs["L0_attn_x"], f32).reshape(S, D))
    xTf = np.ascontiguousarray(x.T)
    Wqkvz = np.asarray(inputs["L0_attn_W_qkvz"], f32)
    WqkvT = np.ascontiguousarray(Wqkvz[:8192].T)
    WzT = np.ascontiguousarray(Wqkvz[8192:].T)
    WbaT = np.ascontiguousarray(np.asarray(inputs["L0_attn_W_ba"], f32).T)
    convw_ = np.ascontiguousarray(np.asarray(inputs["L0_attn_conv_w"], f32)[:, 0, :])
    dtb_ = np.ascontiguousarray(
        np.tile(np.asarray(inputs["L0_attn_dt_bias"], f32)[None, :], (128, 1)))
    alog_ = np.ascontiguousarray(np.asarray(inputs["L0_attn_A_log"], f32)[None, :])
    normw_ = np.ascontiguousarray(
        np.asarray(inputs["L0_attn_norm_w"], f32).reshape(NV, 128).T)
    WoutT = np.ascontiguousarray(np.asarray(inputs["L0_attn_W_out"], f32).T)
    rmsw_ = np.ascontiguousarray(
        np.asarray(inputs["L0_ffn_rms_weight"], f32).reshape(NDC, 128).T)
    routw_ = np.ascontiguousarray(np.asarray(inputs["L0_ffn_router_w"], f32))
    wg_ = np.ascontiguousarray(np.asarray(inputs["L0_ffn_w_gate"], f32))
    wu_ = np.ascontiguousarray(np.asarray(inputs["L0_ffn_w_up"], f32))
    wd_ = np.ascontiguousarray(np.asarray(inputs["L0_ffn_w_down"], f32))
    swg_ = np.ascontiguousarray(np.asarray(inputs["L0_ffn_sw_gate"], f32))
    swu_ = np.ascontiguousarray(np.asarray(inputs["L0_ffn_sw_up"], f32))
    swd_ = np.ascontiguousarray(np.asarray(inputs["L0_ffn_sw_down"], f32))

    bf = ml_dtypes.bfloat16
    ii, jj = np.arange(128)[:, None], np.arange(128)[None, :]
    mn_nat = np.where(jj < ii, 0.0, MASKNEG).astype(f32)
    mn_T = np.where(ii < jj, 0.0, MASKNEG).astype(f32)
    mn_iT = np.where(ii <= jj, 0.0, MASKNEG).astype(f32)
    consts = {
        "identb": np.eye(128, dtype=bf),
        "identb32": np.tile(np.eye(128, dtype=bf), (1, NV)),
        "identf": np.eye(128, dtype=f32),
        "trii": np.triu(np.ones((128, 128), f32)),
        "onesf": np.ones((1, 128), f32),
        "monesf": -np.ones((1, 128), f32),
        "onesb": np.ones((1, 128), bf),
        "fconst": np.tile(np.array([[1e-6, 1e-5, 1.0, 1.0 / 4096.0, 1.0 / 2048.0,
                                     -1.0]], f32), (128, 1)),
        "onescb": np.ones((128, 1), bf),
        "identb4": np.tile(np.eye(128, dtype=bf), (1, 4)),
        "mneg_nat4": np.tile(mn_nat, (1, 4)),
        "mneg_T4": np.tile(mn_T, (1, 4)),
        "mneg_iT4": np.tile(mn_iT, (1, 4)),
        "twoI4": np.tile(2.0 * np.eye(128, dtype=f32), (1, 4)),
    }
    in_maps = []
    for cix in range(NCORE):
        lo = cix * TPC
        halo = np.zeros((D, 3), f32)
        if lo >= 3:
            halo = xTf[:, lo - 3:lo]
        xslice = np.ascontiguousarray(np.concatenate([halo, xTf[:, lo:lo + TPC]], 1))
        oh = np.zeros((128, NCORE), f32)
        oh[:, cix] = 1.0
        m = {
            "xT": xslice, "WqkvT": WqkvT, "WzT": WzT, "WbaT": WbaT,
            "convw": convw_, "dtb": dtb_, "alog": alog_, "normw": normw_,
            "WoutT": WoutT, "rmsw": rmsw_, "routw": routw_,
            "wg": wg_, "wu": wu_, "wd": wd_, "swg": swg_, "swu": swu_, "swd": swd_,
            "onehot": np.ascontiguousarray(oh),
        }
        m.update(consts)
        in_maps.append(m)
    return in_maps


def kernel(**inputs):
    in_maps = _prep_inputs(inputs)
    nc = build_program()
    res = run_bass_kernel_spmd(nc, in_maps, core_ids=list(range(NCORE)))
    out = np.empty((S, D), np.float32)
    for cix in range(NCORE):
        out[cix * TPC:(cix + 1) * TPC, :] = res.results[cix]["outT"].T
    return out.reshape(1, S, D)


# version 11
# speedup vs baseline: 1.0690x; 1.0080x over previous
import sys
sys.path.insert(0, "/opt/trn_rl_repo")
import numpy as np
import ml_dtypes
import concourse.bass as bass
import concourse.mybir as mybir
from concourse.bass_utils import run_bass_kernel_spmd

F32 = mybir.dt.float32
BF16 = mybir.dt.bfloat16
AF = mybir.ActivationFunctionType
ALU = mybir.AluOpType
AX = mybir.AxisListType

S, D = 16384, 2048
NCORE = 8
TPC = S // NCORE
NK, NV, DK, DV = 16, 32, 128, 128
C = 128
NCH = TPC // C
RT = 512
NR = TPC // RT
KR = RT // C
NJB = 64
NDC = D // 128
NEWTON = 6
MASKNEG = -1.0e6
NEXP = 8
F = 1024
NFB = F // 128

STAGE = "full"
SIM1 = False
DEV_TAPS = ()


class Builder:

    def __init__(self, nc):
        self.nc = nc
        self.semg = [nc.semaphore(f"tk{i}") for i in range(64)]
        self.sems = [g.__enter__() for g in self.semg]
        self.rng = {"tensor": list(range(0, 12)), "vector": list(range(12, 24)),
                    "scalar": list(range(24, 30)), "s": list(range(30, 40)),
                    "g": list(range(40, 63)), "c": [63]}
        self.pos = {k: 0 for k in self.rng}
        self.cnt = {i: 0 for i in range(64)}
        self.res = {}
        self.floor = []
        self.waited = {k: {} for k in ("sync", "tensor", "vector", "scalar", "gpsimd")}
        self.prog = {k: [] for k in ("sync", "tensor", "vector", "scalar", "gpsimd")}

    def _sem(self, cls):
        if cls in ("g", "s"):
            self.pos[cls] = (self.pos[cls] + 1) % len(self.rng[cls])
            return self.rng[cls][self.pos[cls]]
        idx = self.rng[cls][self.pos[cls]]
        if self.cnt[idx] > 20000:
            self.pos[cls] += 1
            idx = self.rng[cls][self.pos[cls]]
        return idx

    def step(self, emits, rd=(), wr=()):
        deps = {}
        def add(tok):
            for (si, v) in tok:
                if deps.get(si, -1) < v:
                    deps[si] = v
        add(self.floor)
        for n in rd:
            e = self.res.get(n)
            if e is None:
                self.res[n] = e = {"w": [], "r": []}
            add(e["w"])
        for n in wr:
            e = self.res.get(n)
            if e is None:
                self.res[n] = e = {"w": [], "r": []}
            add(e["w"])
            for t in e["r"]:
                add(t)
        toks = {}
        seen = set()
        for em in emits:
            ename, fn, ninc = em[0], em[1], em[2]
            cls = em[3] if len(em) > 3 else (
                "g" if ename == "gpsimd" else ("s" if ename == "sync" else ename))
            si = self._sem(cls)
            sem = self.sems[si]
            if cls in ("g", "s", "c") and self.cnt[si] > 0:
                if deps.get(si, -1) < self.cnt[si]:
                    deps[si] = self.cnt[si]
            if ename not in seen:
                seen.add(ename)
                dep_list = sorted(deps.items())
                w = self.waited[ename]
                for (dsi, dval) in dep_list:
                    if w.get(dsi, -1) >= dval:
                        continue
                    w[dsi] = dval
                    dsem = self.sems[dsi]
                    self.prog[ename].append(
                        lambda e, dsem=dsem, dval=dval: e.wait_ge(dsem, dval))
            self.prog[ename].append(lambda e, fn=fn, sem=sem: fn(e, sem))
            self.cnt[si] += ninc
            toks[si] = self.cnt[si]
        token = sorted(toks.items())
        for n in wr:
            e = self.res[n]
            e["w"] = token
            e["r"] = []
        for n in rd:
            self.res[n]["r"].append(token)
        self.last = token
        return token

    def barrier(self):
        tok = [(si, c) for si, c in self.cnt.items() if c > 0]
        self.floor = tok
        self.res = {}
        return tok


def build_program():
    nc = bass.Bass("TRN2", target_bir_lowering=False)
    dt = lambda n, s, d, k="ExternalInput": nc.dram_tensor(n, s, d, kind=k).ap()

    xT = dt("xT", [D, TPC + 3], F32)
    WqkvT = dt("WqkvT", [D, 8192], F32)
    WzT = dt("WzT", [D, 4096], F32)
    WbaT = dt("WbaT", [D, 64], F32)
    convw = dt("convw", [8192, 4], F32)
    dtb = dt("dtb", [128, NV], F32)
    alog = dt("alog", [1, NV], F32)
    normw = dt("normw", [128, NV], F32)
    WoutT = dt("WoutT", [4096, D], F32)
    rmsw = dt("rmsw", [128, NDC], F32)
    routw = dt("routw", [D, NEXP], F32)
    wg = dt("wg", [NEXP, D, F], F32)
    wu = dt("wu", [NEXP, D, F], F32)
    wd = dt("wd", [NEXP, F, D], F32)
    swg = dt("swg", [D, F], F32)
    swu = dt("swu", [D, F], F32)
    swd = dt("swd", [F, D], F32)
    identb = dt("identb", [128, 128], BF16)
    identb32 = dt("identb32", [128, NV * 128], BF16)
    identf = dt("identf", [128, 128], F32)
    trii = dt("trii", [128, 128], F32)
    onesf = dt("onesf", [1, 128], F32)
    monesf = dt("monesf", [1, 128], F32)
    mneg_nat4 = dt("mneg_nat4", [128, 512], F32)
    mneg_T4 = dt("mneg_T4", [128, 512], F32)
    mneg_iT4 = dt("mneg_iT4", [128, 512], F32)
    twoI4 = dt("twoI4", [128, 512], F32)
    onehot = dt("onehot", [128, NCORE], F32)
    onesb = dt("onesb", [1, 128], BF16)
    fconst = dt("fconst", [128, 6], F32)
    onescb = dt("onescb", [128, 1], BF16)
    identb4 = dt("identb4", [128, 512], BF16)

    outT = dt("outT", [D, TPC], F32, "ExternalOutput")

    arena_la = dt("arena_la", [NV, TPC], F32, "Internal")
    arena_lb = dt("arena_lb", [NV, TPC], F32, "Internal")
    arena_wr = dt("arena_wr", [128, 128], F32, "Internal")
    sp_ob = dt("sp_ob", [NV * NCH * 128, 128], BF16, "Internal")
    sp_h5 = dt("sp_h5", [NV * NCH * 128, 128], BF16, "Internal")
    sp_x1 = dt("sp_x1", [NDC * 128, TPC], F32, "Internal")
    sp_su = dt("sp_su", [(NEXP + 1) * NFB * 128, TPC], BF16, "Internal")
    sp_ffn = dt("sp_ffn", [NDC * 128, TPC], F32, "Internal")
    cc_in = dt("cc_in", [2 * NV * 128, 128], BF16, "Internal")
    cc_out = nc.dram_tensor("cc_out", [NCORE * 2 * NV * 128, 128], BF16,
                        kind="Internal", addr_space="Shared").ap()

    taps = {}
    for name, shape, dtp in DEV_TAPS:
        taps[name] = dt("tap_" + name, list(shape), dtp, "ExternalOutput")

    sb = lambda n, s, d: nc.alloc_sbuf_tensor(n, s, d).ap()
    c_identb = sb("c_identb", [128, 128], BF16)
    c_identf = sb("c_identf", [128, 128], F32)
    c_trii = sb("c_trii", [128, 128], F32)
    c_onesf = sb("c_onesf", [1, 128], F32)
    c_monesf = sb("c_monesf", [1, 128], F32)
    c_mnegN = sb("c_mnegN", [128, 512], F32)
    c_mnegT = sb("c_mnegT", [128, 512], F32)
    c_mnegI = sb("c_mnegI", [128, 512], F32)
    c_twoI4 = sb("c_twoI4", [128, 512], F32)
    c_dtb = sb("c_dtb", [128, NV], F32)
    c_normw = sb("c_normw", [128, NV], F32)
    c_rmsw = sb("c_rmsw", [128, NDC], F32)
    c_onehot = sb("c_onehot", [128, NCORE], F32)
    c_alog = sb("c_alog", [1, NV], F32)
    exprow = sb("exprow", [1, NV], F32)
    expA = sb("expA", [128, NV], F32)
    convw_sb = sb("convw_sb", [128, NJB * 4], F32)
    wba_sb = sb("wba_sb", [128, 16 * 64], BF16)

    xtb = sb("xtb", [128, NDC * RT], BF16)
    xtail = sb("xtail", [128, NDC * 3], BF16)
    wjb = sb("wjb", [128, 2 * 2048], BF16)
    qkv = sb("qkv", [128, NJB * RT], BF16)
    c_onesb = sb("c_onesb", [1, 128], BF16)
    c_fconst = sb("c_fconst", [128, 6], F32)
    c_onescb = sb("c_onescb", [128, 1], BF16)
    c_identb4 = sb("c_identb4", [128, 512], BF16)
    for _i, _v in enumerate([1e-6, 1e-5, 1.0, 1.0 / 4096.0, 1.0 / 2048.0, -1.0]):
        nc.const_aps.register(F32, _v, c_fconst[:, _i:_i + 1])
    ab_guards = []
    def sba(n, s, d):
        g = nc.sbuf_tensor(n, s, d)
        ap = g.__enter__()
        ab_guards.append(g)
        return ap
    cpre = sba("cpre", [128, 515], F32)
    cacc = [sba("cacc0", [128, 512], F32), sba("cacc1", [128, 512], F32)]
    tails = sba("tails", [128, NJB * 3], F32)
    batile = cacc[1][:, 0:256]
    bcols = sba("bcols", [128, KR * NV], F32)
    gcols = sba("gcols", [128, KR * NV], F32)
    spa = cacc[1][:, 256:384]
    spb = cacc[1][:, 384:512]
    lacols = sba("lacols", [128, KR * NV], F32)
    nlacols = sba("nlacols", [128, KR * NV], F32)
    lbcols = sba("lbcols", [128, KR * NV], F32)
    elacols = sba("elacols", [128, KR * NV], F32)
    bwcols = sba("bwcols", [128, KR * NV], F32)
    ewbcols = sba("ewbcols", [128, KR * NV], F32)
    accols = sba("accols", [128, KR * NV], F32)
    lacrow = sba("lacrow", [1, NV], F32)
    rowstage = sba("rowstage", [32, 2 * 128], F32)
    laro2 = [sba("laroPA", [1, RT], F32), sba("laroPB", [1, RT], F32)]
    lbro2 = [sba("lbroPA", [1, RT], F32), sba("lbroPB", [1, RT], F32)]
    sqtile = cacc[0]
    rsqrow = sba("rsqrow", [1, 512], F32)

    W2 = lambda n, s, d: [sba(n + "A", s, d), sba(n + "B", s, d)]
    w_eT2 = W2("w_eT", [128, 512], BF16)
    w_eN2 = W2("w_eN", [128, 512], BF16)
    w_An2 = W2("w_An", [128, 512], BF16)
    w_X2 = W2("w_X", [128, 512], BF16)
    w_Xt2 = W2("w_Xt", [128, 512], BF16)
    w_Z2 = W2("w_Z", [128, 512], BF16)
    w_PmT2 = W2("w_PmT", [128, 512], BF16)
    w_QwT2 = W2("w_QwT", [128, 512], BF16)
    w_Kr2 = W2("w_Kr", [128, 512], BF16)
    w_Vr2 = W2("w_Vr", [128, 512], BF16)
    w_RV2 = W2("w_RV", [128, 1024], BF16)
    w_Wb2 = W2("w_Wb", [128, 512], BF16)
    w_UG2 = W2("w_UG", [128, 1024], BF16)
    w_McT2 = W2("w_McT", [128, 512], BF16)
    w_Nc2 = W2("w_Nc", [128, 512], F32)
    w_H32 = W2("w_H3", [128, 512], BF16)
    w_Ob2 = W2("w_Ob", [128, 512], BF16)
    w_H52 = W2("w_H5", [128, 512], BF16)
    kkS2 = W2("kkS", [128, 512], BF16)
    kqS2 = W2("kqS", [128, 512], BF16)
    elarow2 = W2("elarowP", [1, RT], F32)
    Qs_f = sba("Qs_f", [128, NV * 128], F32)
    Qs_b = sba("Qs_b", [128, NV * 128], BF16)
    Pi_b = sba("Pi_b", [128, NV * 128], BF16)

    psg = []
    def psum(n, shape, dtp):
        g = nc.psum_tensor(n, shape, dtp)
        ap = g.__enter__()
        psg.append(g)
        return ap
    p = [psum(f"p{i}", [128, 512], F32) for i in range(7)]
    pbg = nc.psum_tensor("pb", [128, 1024], BF16)
    pb = pbg.__enter__()

    bld = Builder(nc)
    st = bld.step

    def dma(e, sem, out, in_):
        return e.dma_start(out=out, in_=in_).then_inc(sem, 16)

    def ld_consts(e, sem):
        dma(e, sem, c_identb[:], identb)
        dma(e, sem, c_identf[:], identf)
        dma(e, sem, c_trii[:], trii)
        dma(e, sem, c_onesf[:], onesf)
        dma(e, sem, c_monesf[:], monesf)
        dma(e, sem, c_mnegN[:], mneg_nat4)
        dma(e, sem, c_mnegT[:], mneg_T4)
        dma(e, sem, c_mnegI[:], mneg_iT4)
        dma(e, sem, c_twoI4[:], twoI4)
        dma(e, sem, c_dtb[:], dtb)
        dma(e, sem, c_normw[:], normw)
        dma(e, sem, c_rmsw[:], rmsw)
        dma(e, sem, c_onehot[:], onehot)
        dma(e, sem, c_alog[:], alog)
        dma(e, sem, convw_sb[:], convw.rearrange("(j p) t -> p j t", p=128))
        dma(e, sem, Pi_b[:], identb32)
        dma(e, sem, c_onesb[:], onesb)
        dma(e, sem, c_fconst[:], fconst)
        dma(e, sem, c_onescb[:], onescb)
        dma(e, sem, c_identb4[:], identb4)

    def ld_casts(e, sem):
        dma(e, sem, wba_sb[:], WbaT.rearrange("(k p) j -> p k j", p=128))
        dma(e, sem, xtail[:], xT[:, 0:3].rearrange("(k p) t -> p k t", p=128))
    st([("sync", ld_consts, 16 * 20), ("gpsimd", ld_casts, 32)])
    bld.barrier()

    st([("scalar", lambda e, s: e.activation(out=exprow[:], in_=c_alog[:],
                                             func=AF.Exp).then_inc(s, 1), 1)],
       wr=["exprow"])
    st([("tensor", lambda e, s: e.matmul(p[0][:, 0:NV], c_onesf[:], exprow[:],
                                         start=True, stop=True).then_inc(s, 1), 1)],
       rd=["exprow"], wr=["p0"])
    def setup_v(e, s):
        e.tensor_scalar(out=expA[:], in0=p[0][:, 0:NV], scalar1=-1.0, scalar2=None,
                        op0=ALU.mult).then_inc(s, 1)
        e.tensor_scalar(out=c_normw[:], in0=c_normw[:], scalar1=1.0, scalar2=None,
                        op0=ALU.add).then_inc(s, 1)
        e.memset(Qs_f[:], 0.0).then_inc(s, 1)
        e.memset(Qs_b[:], 0.0).then_inc(s, 1)
        e.memset(tails[:], 0.0).then_inc(s, 1)
    st([("vector", setup_v, 5)], rd=["p0"], wr=["expA"])
    bld.barrier()

    for r in range(NR):
        c0 = 3 + r * RT

        def ld_x(e, sem, c0=c0):
            for k in range(NDC):
                dma(e, sem, xtb[:, k * RT:(k + 1) * RT],
                    xT[k * 128:(k + 1) * 128, c0:c0 + RT])
        st([("gpsimd", ld_x, 16 * NDC)], wr=["xtb"])

        for cc in range(KR):
            def mm_ba(e, sem, cc=cc):
                for k in range(NDC):
                    ii = e.matmul(p[0][:, 0:64],
                             xtb[:, k * RT + cc * C: k * RT + cc * C + C],
                             wba_sb[:, k * 64:(k + 1) * 64],
                             start=(k == 0), stop=(k == NDC - 1))
                ii.then_inc(sem, 1)
            st([("tensor", mm_ba, 1)], rd=["xtb"], wr=["p0"])
            st([("vector", lambda e, s, cc=cc: e.tensor_copy(
                out=batile[:, cc * 64:(cc + 1) * 64], in_=p[0][:, 0:64]).then_inc(s, 1), 1)],
               rd=["p0"], wr=["cacc1"])
        def act_beta(e, s):
            for cc in range(KR):
                e.activation(out=bcols[:, cc * NV:(cc + 1) * NV],
                             in_=batile[:, cc * 64: cc * 64 + NV],
                             func=AF.Sigmoid).then_inc(s, 1)
        st([("scalar", act_beta, KR)], rd=["cacc1"], wr=["bcols"])
        def sp1(e, s):
            for cc in range(KR):
                e.tensor_tensor(out=spa[:, cc * NV:(cc + 1) * NV],
                                in0=batile[:, cc * 64 + NV: cc * 64 + 64],
                                in1=c_dtb[:], op=ALU.add).then_inc(s, 1)
        st([("vector", sp1, KR)], rd=["cacc1"], wr=["cacc1"])
        st([("scalar", lambda e, s: e.activation(out=spb[:], in_=spa[:],
                                                 func=AF.Abs).then_inc(s, 1), 1)],
           rd=["cacc1"], wr=["cacc1"])
        st([("scalar", lambda e, s: e.activation(out=spb[:], in_=spb[:],
                                                 func=AF.Exp,
                                                 scale=-1.0).then_inc(s, 1), 1)],
           rd=["cacc1"], wr=["cacc1"])
        st([("scalar", lambda e, s: e.activation(out=spb[:], in_=spb[:], func=AF.Ln,
                                                 bias=1.0).then_inc(s, 1), 1)],
           rd=["cacc1"], wr=["cacc1"])
        st([("vector", lambda e, s: e.scalar_tensor_tensor(
            out=spa[:], in0=spa[:], scalar=0.0, in1=spb[:],
            op0=ALU.max, op1=ALU.add).then_inc(s, 1), 1)],
           rd=["cacc1"], wr=["cacc1"])
        def gmul(e, s):
            for cc in range(KR):
                e.tensor_tensor(out=gcols[:, cc * NV:(cc + 1) * NV],
                                in0=spa[:, cc * NV:(cc + 1) * NV],
                                in1=expA[:], op=ALU.mult).then_inc(s, 1)
        st([("vector", gmul, KR)], rd=["cacc1", "expA"], wr=["gcols"])
        for cc in range(KR):
            def mm_la(e, s, cc=cc):
                ii = e.matmul(p[1][:, 0:NV], c_trii[:], gcols[:, cc * NV:(cc + 1) * NV],
                         start=True, stop=True)
                ii = e.matmul(p[2][0:NV, 0:128], gcols[:, cc * NV:(cc + 1) * NV], c_trii[:],
                         start=True, stop=True)
                ii = e.matmul(p[3][0:1, 0:NV], c_trii[:, 127:128],
                         gcols[:, cc * NV:(cc + 1) * NV], start=True, stop=True)
                ii.then_inc(s, 1)
            st([("tensor", mm_la, 1)], rd=["gcols"], wr=["p1", "p2", "p3"])
            def ev_la(e, s, cc=cc):
                e.tensor_copy(out=lacols[:, cc * NV:(cc + 1) * NV],
                              in_=p[1][:, 0:NV]).then_inc(s, 1)
                e.tensor_copy(out=rowstage[0:NV, 0:128],
                              in_=p[2][0:NV, 0:128]).then_inc(s, 1)
                e.tensor_copy(out=lacrow[:], in_=p[3][0:1, 0:NV]).then_inc(s, 1)
            st([("vector", ev_la, 3)], rd=["p1", "p2", "p3"], wr=["lacols", "rowstage", "lacrow"])
            def aw_la(e, s, cc=cc, r=r):
                dma(e, s, arena_la[:, r * RT + cc * C: r * RT + (cc + 1) * C],
                    rowstage[0:NV, 0:128])
            st([("gpsimd", aw_la, 16)], rd=["rowstage"], wr=["Ala"])
            def mm_acb(e, s):
                ii = e.matmul(p[3][:, 0:NV], c_onesf[:], lacrow[:],
                         start=True, stop=True).then_inc(s, 1)
            st([("tensor", mm_acb, 1)], rd=["lacrow"], wr=["p3"])
            def ev_acb(e, s, cc=cc):
                e.tensor_tensor(out=ewbcols[:, cc * NV:(cc + 1) * NV],
                                in0=p[3][:, 0:NV], in1=lacols[:, cc * NV:(cc + 1) * NV],
                                op=ALU.subtract).then_inc(s, 1)
                e.tensor_copy(out=accols[:, cc * NV:(cc + 1) * NV],
                              in_=p[3][:, 0:NV]).then_inc(s, 1)
            st([("vector", ev_acb, 2)], rd=["p3", "lacols"], wr=["ewbcols", "accols"])
        def d1(e, s):
            e.tensor_scalar(out=nlacols[:], in0=lacols[:], scalar1=-1.0, scalar2=None,
                            op0=ALU.mult).then_inc(s, 1)
        st([("vector", d1, 1)], rd=["lacols"], wr=["nlacols"])
        def d2(e, s):
            e.activation(out=elacols[:], in_=lacols[:], func=AF.Exp).then_inc(s, 1)
            e.activation(out=lbcols[:], in_=bcols[:], func=AF.Ln).then_inc(s, 1)
            e.activation(out=ewbcols[:], in_=ewbcols[:], func=AF.Exp).then_inc(s, 1)
            e.activation(out=accols[:], in_=accols[:], func=AF.Exp).then_inc(s, 1)
        st([("scalar", d2, 4)], rd=["lacols", "bcols", "ewbcols", "accols"], wr=["elacols", "lbcols", "ewbcols", "accols"])
        def d3(e, s):
            e.tensor_tensor(out=lbcols[:], in0=lbcols[:], in1=lacols[:],
                            op=ALU.add).then_inc(s, 1)
            e.tensor_tensor(out=bwcols[:], in0=bcols[:], in1=elacols[:],
                            op=ALU.mult).then_inc(s, 1)
        st([("vector", d3, 2)], rd=["lbcols", "lacols", "bcols", "elacols"], wr=["lbcols", "bwcols"])
        for cc in range(KR):
            def mm_lbr(e, s, cc=cc):
                ii = e.matmul(p[2][0:NV, 0:128], lbcols[:, cc * NV:(cc + 1) * NV],
                         c_identf[:], start=True, stop=True).then_inc(s, 1)
            st([("tensor", mm_lbr, 1)], rd=["lbcols"], wr=["p2"])
            st([("vector", lambda e, s, cc=cc: e.tensor_copy(
                out=rowstage[0:NV, 128:256],
                in_=p[2][0:NV, 0:128]).then_inc(s, 1), 1)],
               rd=["p2"], wr=["rowstage"])
            def aw_lb(e, s, cc=cc, r=r):
                dma(e, s, arena_lb[:, r * RT + cc * C: r * RT + (cc + 1) * C],
                    rowstage[0:NV, 128:256])
            st([("gpsimd", aw_lb, 16)], rd=["rowstage"], wr=["Alb"])

        for jb in range(NJB):
            buf = jb % 2
            def ld_w(e, sem, jb=jb, buf=buf):
                dma(e, sem, wjb[:, buf * 2048:(buf + 1) * 2048],
                    WqkvT[:, jb * 128:(jb + 1) * 128]
                    .rearrange("(k p) j -> p k j", p=128))
            st([("gpsimd", ld_w, 16)], wr=[f"wjb{buf}"])
            pp = p[4] if jb % 2 == 0 else p[3]
            ppn = "p4" if jb % 2 == 0 else "p3"
            def mm_proj(e, sem, buf=buf, r=r, pp=pp):
                for k in range(NDC):
                    ii = e.matmul(pp[:],
                             wjb[:, buf * 2048 + k * 128: buf * 2048 + (k + 1) * 128],
                             xtb[:, k * RT:(k + 1) * RT],
                             start=(k == 0), stop=(k == NDC - 1))
                if r == 0:
                    for k in range(NDC):
                        ii = e.matmul(p[5][:, 0:3],
                                 wjb[:, buf * 2048 + k * 128: buf * 2048 + (k + 1) * 128],
                                 xtail[:, k * 3:(k + 1) * 3],
                                 start=(k == 0), stop=(k == NDC - 1))
                ii.then_inc(sem, 1)
            st([("tensor", mm_proj, 1)], rd=[f"wjb{buf}", "xtb", "xtail"], wr=[ppn] + (["p5"] if r == 0 else []))
            if r == 0:
                st([("vector", lambda e, s, jb=jb: e.tensor_copy(
                    out=tails[:, jb * 3:(jb + 1) * 3],
                    in_=p[5][:, 0:3]).then_inc(s, 1), 1)],
                   rd=["p5"], wr=[f"tl{jb}"])
            def conv1(e, s, jb=jb, pp=pp):
                e.tensor_copy(out=cpre[:, 0:3],
                              in_=tails[:, jb * 3:(jb + 1) * 3]).then_inc(s, 1)
                e.tensor_copy(out=cpre[:, 3:515], in_=pp[:]).then_inc(s, 1)
            st([("vector", conv1, 2)], rd=[f"tl{jb}", ppn], wr=["cpre"])
            def conv2a(e, s, jb=jb):
                e.tensor_scalar(out=cacc[jb % 2][:], in0=cpre[:, 0:512],
                                scalar1=convw_sb[:, jb * 4: jb * 4 + 1], scalar2=None,
                                op0=ALU.mult).then_inc(s, 1)
            st([("vector", conv2a, 1)], rd=["cpre"], wr=[f"cacc{jb % 2}"])
            for tap in (1, 2, 3):
                def conv2b(e, s, jb=jb, tap=tap):
                    e.scalar_tensor_tensor(
                        out=cacc[jb % 2][:], in0=cpre[:, tap:tap + 512],
                        scalar=convw_sb[:, jb * 4 + tap: jb * 4 + tap + 1],
                        in1=cacc[jb % 2][:], op0=ALU.mult, op1=ALU.add).then_inc(s, 1)
                st([("vector", conv2b, 1)], rd=["cpre", f"cacc{jb % 2}"], wr=[f"cacc{jb % 2}"])
            def tails_upd(e, s, jb=jb):
                e.tensor_copy(out=tails[:, jb * 3:(jb + 1) * 3],
                              in_=cpre[:, 512:515]).then_inc(s, 1)
            st([("vector", tails_upd, 1)], rd=["cpre"], wr=[f"tl{jb}"])
            def conv3(e, s, jb=jb):
                e.activation(out=qkv[:, jb * RT:(jb + 1) * RT],
                             in_=cacc[jb % 2][:], func=AF.Silu).then_inc(s, 1)
            st([("scalar", conv3, 1)], rd=[f"cacc{jb % 2}"], wr=[f"qk{jb}"])

        rsq2 = [rsqrow, laro2[0]]
        rsqn = ["rsqrow", "laro0"]
        sqt2 = [cacc[0], cacc[1]]
        sqtn = ["cacc0", "cacc1"]
        def emit_l2(jb, stq):
            st = lambda emits, rd=(), wr=(): stq.append((emits, rd, wr))
            jp = jb % 2
            def sq(e, s, jb=jb, jp=jp):
                e.tensor_tensor(out=sqt2[jp][:], in0=qkv[:, jb * RT:(jb + 1) * RT],
                                in1=qkv[:, jb * RT:(jb + 1) * RT],
                                op=ALU.mult).then_inc(s, 1)
            st([("vector", sq, 1)], rd=[f"qk{jb}"], wr=[sqtn[jp]])
            def mm_ss(e, s, jp=jp):
                e.matmul(p[5 - jp][0:1, 0:512], c_trii[:, 127:128], sqt2[jp][:],
                         start=True, stop=True).then_inc(s, 1)
            st([("tensor", mm_ss, 1)], rd=[sqtn[jp]], wr=[f"p{5 - jp}"])
            def rsq1(e, s, jp=jp):
                e.activation(out=rsq2[jp][0:1, :], in_=p[5 - jp][0:1, 0:512],
                             func=AF.Sqrt, bias=1e-6).then_inc(s, 1)
            st([("scalar", rsq1, 1)], rd=[f"p{5 - jp}"], wr=[rsqn[jp]])
            def rsq2f(e, s, jp=jp):
                e.reciprocal(out=rsq2[jp][0:1, :], in_=rsq2[jp][0:1, :]).then_inc(s, 1)
            st([("vector", rsq2f, 1)], rd=[rsqn[jp]], wr=[rsqn[jp]])
            def mm_rb(e, s, jp=jp):
                e.matmul(p[5 - jp][:, 0:512], c_onesf[:], rsq2[jp][0:1, :],
                         start=True, stop=True).then_inc(s, 1)
            st([("tensor", mm_rb, 1)], rd=[rsqn[jp]], wr=[f"p{5 - jp}"])
            def l2m(e, s, jb=jb, jp=jp):
                e.tensor_tensor(out=qkv[:, jb * RT:(jb + 1) * RT],
                                in0=qkv[:, jb * RT:(jb + 1) * RT],
                                in1=p[5 - jp][:, 0:512], op=ALU.mult).then_inc(s, 1)
            st([("vector", l2m, 1)], rd=[f"p{5 - jp}", f"qk{jb}"], wr=[f"qk{jb}"])
        for jb0 in range(0, 32, 2):
            la_, lb_ = [], []
            emit_l2(jb0, la_)
            emit_l2(jb0 + 1, lb_)
            for i in range(len(la_)):
                bld.step(la_[i][0], rd=la_[i][1], wr=la_[i][2])
                bld.step(lb_[i][0], rd=lb_[i][1], wr=lb_[i][2])

        if ("qkv_r%d" % r) in taps:
            st([("gpsimd", lambda e, s, r=r: dma(e, s, taps["qkv_r%d" % r], qkv[:]), 16)], rd=[f"qk{j}" for j in range(NJB)])
        if ("gla_r%d" % r) in taps:
            def tap_gla(e, s, r=r):
                dma(e, s, taps["gla_r%d" % r][0:128, :], gcols[:])
                dma(e, s, taps["gla_r%d" % r][128:256, :], lacols[:])
                dma(e, s, taps["gla_r%d" % r][256:384, :], bcols[:])
                dma(e, s, taps["gla_r%d" % r][384:512, :], ewbcols[:])
            st([("gpsimd", tap_gla, 64)], rd=["gcols", "lacols", "bcols", "ewbcols"])
        if STAGE == "A":
            continue

        def emit_head(h, stq):
            st = lambda emits, rd=(), wr=(): stq.append((emits, rd, wr))
            par = h % 2
            w_eT, w_eN, w_An, w_X, w_Xt = (w_eT2[par], w_eN2[par], w_An2[par],
                                           w_X2[par], w_Xt2[par])
            w_Z, w_PmT, w_QwT, w_Kr, w_Vr = (w_Z2[par], w_PmT2[par], w_QwT2[par],
                                             w_Kr2[par], w_Vr2[par])
            w_RV, w_Wb, w_UG, w_McT = w_RV2[par], w_Wb2[par], w_UG2[par], w_McT2[par]
            w_Nc, w_H3, w_Ob, w_H5 = w_Nc2[par], w_H32[par], w_Ob2[par], w_H52[par]
            kkS, kqS, elarow = kkS2[par], kqS2[par], elarow2[par]
            P = lambda n: n + str(par)
            pArg = p[2] if par == 0 else p[6]
            nArg = "p2" if par == 0 else "p6"
            def ld_rows(e, s, r=r, h=h):
                dma(e, s, laro2[par][0:1, :], arena_la[h:h + 1, r * RT:(r + 1) * RT])
                dma(e, s, lbro2[par][0:1, :], arena_lb[h:h + 1, r * RT:(r + 1) * RT])
            st([("sync", ld_rows, 32)], rd=["Ala", "Alb"], wr=[P("laro"), P("lbro")])
            kh = h // 2
            qT = qkv[:, kh * RT:(kh + 1) * RT]
            kT = qkv[:, (16 + kh) * RT:(16 + kh + 1) * RT]
            vT = qkv[:, (32 + h) * RT:(32 + h + 1) * RT]
            hc0 = h * NCH + r * KR

            def mm_kq(e, s, qT=qT, kT=kT):
                for cc in range(KR):
                    ii = e.matmul(p[0][:, cc * C:(cc + 1) * C], kT[:, cc * C:(cc + 1) * C],
                                  qT[:, cc * C:(cc + 1) * C], start=True, stop=True)
                    ii = e.matmul(p[1][:, cc * C:(cc + 1) * C], kT[:, cc * C:(cc + 1) * C],
                                  kT[:, cc * C:(cc + 1) * C], start=True, stop=True)
                ii.then_inc(s, 1)
            st([("tensor", mm_kq, 1)], rd=[f"qk{kh}", f"qk{16 + kh}"], wr=["p0", "p1"])
            def ev_kq(e, s, kqS=kqS, kkS=kkS):
                e.copy(out=kqS[:], in_=p[0][:]).then_inc(s, 1)
                e.copy(out=kkS[:], in_=p[1][:]).then_inc(s, 1)
            st([("scalar", ev_kq, 2)], rd=["p0", "p1"], wr=[P("kq"), P("kk")])

            def mm_argT(e, s, h=h, pArg=pArg, par=par):
                ii = e.matmul(pArg[:], c_onesf[:], lbro2[par][0:1, :],
                              start=True, stop=False)
                ii = e.matmul(pArg[:], c_identf[:], c_mnegT[:], start=False, stop=True)
                ii.then_inc(s, 1)
            st([("tensor", mm_argT, 1)], rd=[P("lbro")], wr=[nArg])
            def act_eT(e, s, h=h, w_eT=w_eT, pArg=pArg):
                for cc in range(KR):
                    e.activation(out=w_eT[:, cc * C:(cc + 1) * C],
                                 in_=pArg[:, cc * C:(cc + 1) * C], func=AF.Exp,
                                 bias=nlacols[:, cc * NV + h: cc * NV + h + 1]
                                 ).then_inc(s, 1)
            st([("scalar", act_eT, KR)], rd=[nArg, "nlacols"], wr=[P("eT")])
            def mm_argN(e, s, h=h, pArg=pArg, par=par):
                ii = e.matmul(pArg[:], c_monesf[:], laro2[par][0:1, :],
                              start=True, stop=False)
                ii = e.matmul(pArg[:], c_identf[:], c_mnegN[:], start=False, stop=True)
                ii.then_inc(s, 1)
            st([("tensor", mm_argN, 1)], rd=[P("laro")], wr=[nArg])
            def act_eN(e, s, h=h, w_eN=w_eN, pArg=pArg):
                for cc in range(KR):
                    e.activation(out=w_eN[:, cc * C:(cc + 1) * C],
                                 in_=pArg[:, cc * C:(cc + 1) * C], func=AF.Exp,
                                 bias=lbcols[:, cc * NV + h: cc * NV + h + 1]
                                 ).then_inc(s, 1)
            st([("scalar", act_eN, KR)], rd=[nArg, "lbcols"], wr=[P("eN")])
            def tt_mb(e, s, w_eT=w_eT, w_eN=w_eN, kkS=kkS):
                e.tensor_tensor(out=w_eT[:], in0=w_eT[:], in1=kkS[:],
                                op=ALU.mult).then_inc(s, 1)
                e.tensor_tensor(out=w_eN[:], in0=w_eN[:], in1=kkS[:],
                                op=ALU.mult).then_inc(s, 1)
            st([("vector", tt_mb, 2)], rd=[P("eT"), P("eN"), P("kk")], wr=[P("eT"), P("eN")])
            def tt_axt(e, s, w_An=w_An, w_X=w_X, w_Xt=w_Xt, w_eT=w_eT, w_eN=w_eN):
                e.tensor_tensor(out=w_An[:], in0=c_identb4[:], in1=w_eN[:],
                                op=ALU.add).then_inc(s, 1)
                e.tensor_tensor(out=w_X[:], in0=c_identb4[:], in1=w_eT[:],
                                op=ALU.subtract).then_inc(s, 1)
                e.tensor_tensor(out=w_Xt[:], in0=c_identb4[:], in1=w_eN[:],
                                op=ALU.subtract).then_inc(s, 1)
            st([("vector", tt_axt, 3)], rd=[P("eT"), P("eN")], wr=[P("An"), P("X"), P("Xt")])
            def mm_argP(e, s, h=h, pArg=pArg, par=par):
                ii = e.matmul(pArg[:], c_onesf[:], laro2[par][0:1, :],
                              start=True, stop=False)
                ii = e.matmul(pArg[:], c_identf[:], c_mnegI[:], start=False, stop=True)
                ii.then_inc(s, 1)
            st([("tensor", mm_argP, 1)], rd=[P("laro")], wr=[nArg])
            def act_eP(e, s, h=h, w_eT=w_eT, pArg=pArg):
                for cc in range(KR):
                    e.activation(out=w_eT[:, cc * C:(cc + 1) * C],
                                 in_=pArg[:, cc * C:(cc + 1) * C], func=AF.Exp,
                                 bias=nlacols[:, cc * NV + h: cc * NV + h + 1]
                                 ).then_inc(s, 1)
            st([("scalar", act_eP, KR)], rd=[nArg, "nlacols"], wr=[P("eT")])
            def tt_pm(e, s, w_PmT=w_PmT, w_eT=w_eT, kqS=kqS):
                e.tensor_tensor(out=w_PmT[:], in0=w_eT[:], in1=kqS[:],
                                op=ALU.mult).then_inc(s, 1)
            st([("vector", tt_pm, 1)], rd=[P("eT"), P("kq")], wr=[P("PmT")])
            def act_elar(e, s, h=h, elarow=elarow, par=par):
                e.activation(out=elarow[:], in_=laro2[par][0:1, :],
                             func=AF.Exp).then_inc(s, 1)
            st([("scalar", act_elar, 1)], rd=[P("laro")], wr=[P("elar")])
            def mm_elab(e, s, elarow=elarow, pArg=pArg):
                e.matmul(pArg[:], c_onesf[:], elarow[:],
                         start=True, stop=True).then_inc(s, 1)
            st([("tensor", mm_elab, 1)], rd=[P("elar")], wr=[nArg])
            def tt_qw(e, s, qT=qT, w_QwT=w_QwT, pArg=pArg):
                e.tensor_tensor(out=w_QwT[:], in0=qT, in1=pArg[:],
                                op=ALU.mult).then_inc(s, 1)
            st([("vector", tt_qw, 1)], rd=[f"qk{kh}", nArg], wr=[P("QwT")])

            for it in range(NEWTON):
                def mm_n1(e, s, w_An=w_An, w_X=w_X):
                    for cc in range(KR):
                        sl = slice(cc * C, (cc + 1) * C)
                        ii = e.matmul(p[3][:, sl], w_An[:, sl], w_X[:, sl],
                                      start=True, stop=True)
                    ii.then_inc(s, 1)
                st([("tensor", mm_n1, 1)], rd=[P("An"), P("X")], wr=["p3"])
                def tt_z(e, s, w_Z=w_Z):
                    e.tensor_tensor(out=w_Z[:], in0=c_twoI4[:], in1=p[3][:],
                                    op=ALU.subtract).then_inc(s, 1)
                st([("vector", tt_z, 1)], rd=["p3"], wr=[P("Z")])
                def mm_n23(e, s, w_Xt=w_Xt, w_Z=w_Z):
                    for cc in range(KR):
                        sl = slice(cc * C, (cc + 1) * C)
                        ii = e.matmul(p[4][:, sl], w_Xt[:, sl], w_Z[:, sl],
                                      start=True, stop=True)
                        ii = e.matmul(p[5][:, sl], w_Z[:, sl], w_Xt[:, sl],
                                      start=True, stop=True)
                    ii.then_inc(s, 1)
                st([("tensor", mm_n23, 1)], rd=[P("Xt"), P("Z")], wr=["p4", "p5"])
                def ev_xxt(e, s, w_X=w_X, w_Xt=w_Xt):
                    e.copy(out=w_X[:], in_=p[4][:]).then_inc(s, 1)
                    e.copy(out=w_Xt[:], in_=p[5][:]).then_inc(s, 1)
                st([("scalar", ev_xxt, 2)], rd=["p4", "p5"], wr=[P("X"), P("Xt")])

            def mm_tr(e, s, kT=kT, vT=vT):
                for cc in range(KR):
                    ii = e.transpose(pb[:, cc * C:(cc + 1) * C],
                                     kT[:, cc * C:(cc + 1) * C], c_identb[:])
                    ii = e.transpose(pb[:, 512 + cc * C: 512 + (cc + 1) * C],
                                     vT[:, cc * C:(cc + 1) * C], c_identb[:])
                ii.then_inc(s, 1)
            st([("tensor", mm_tr, 1)], rd=[f"qk{16 + kh}", f"qk{32 + h}"], wr=["pb"])
            def ev_tr(e, s, w_Kr=w_Kr, w_Vr=w_Vr):
                e.copy(out=w_Kr[:], in_=pb[:, 0:512]).then_inc(s, 1)
                e.copy(out=w_Vr[:], in_=pb[:, 512:1024]).then_inc(s, 1)
            st([("scalar", ev_tr, 2)], rd=["pb"], wr=[P("Kr"), P("Vr")])
            def ts_rv(e, s, h=h, w_RV=w_RV, w_Wb=w_Wb, w_Kr=w_Kr, w_Vr=w_Vr):
                for cc in range(KR):
                    e.tensor_scalar(out=w_RV[:, cc * 256: cc * 256 + 128],
                                    in0=w_Vr[:, cc * C:(cc + 1) * C],
                                    scalar1=bcols[:, cc * NV + h: cc * NV + h + 1],
                                    scalar2=None, op0=ALU.mult).then_inc(s, 1)
                    e.tensor_scalar(out=w_RV[:, cc * 256 + 128: cc * 256 + 256],
                                    in0=w_Kr[:, cc * C:(cc + 1) * C],
                                    scalar1=bwcols[:, cc * NV + h: cc * NV + h + 1],
                                    scalar2=None, op0=ALU.mult).then_inc(s, 1)
                    e.tensor_scalar(out=w_Wb[:, cc * C:(cc + 1) * C],
                                    in0=w_Kr[:, cc * C:(cc + 1) * C],
                                    scalar1=ewbcols[:, cc * NV + h: cc * NV + h + 1],
                                    scalar2=None, op0=ALU.mult).then_inc(s, 1)
            st([("vector", ts_rv, 3 * KR)],
               rd=[P("Kr"), P("Vr"), "bcols", "bwcols", "ewbcols"], wr=[P("RV"), P("Wb")])
            def mm_ug(e, s, w_X=w_X, w_RV=w_RV):
                for cc in range(KR):
                    pd = p[0] if cc < 2 else p[1]
                    off = (cc % 2) * 256
                    ii = e.matmul(pd[:, off:off + 256], w_X[:, cc * C:(cc + 1) * C],
                                  w_RV[:, cc * 256:(cc + 1) * 256], start=True, stop=True)
                ii.then_inc(s, 1)
            st([("tensor", mm_ug, 1)], rd=[P("X"), P("RV")], wr=["p0", "p1"])
            def ev_ug(e, s, w_UG=w_UG):
                e.tensor_copy(out=w_UG[:, 0:512], in_=p[0][:]).then_inc(s, 1)
                e.tensor_copy(out=w_UG[:, 512:1024], in_=p[1][:]).then_inc(s, 1)
            st([("vector", ev_ug, 2)], rd=["p0", "p1"], wr=[P("UG")])
            def ugU(cc, w_UG=w_UG):
                return w_UG[:, cc * 256: cc * 256 + 128]
            def ugG(cc, w_UG=w_UG):
                return w_UG[:, cc * 256 + 128: cc * 256 + 256]
            def mm_mc(e, s, ugU=ugU, ugG=ugG, w_Wb=w_Wb):
                for cc in range(KR):
                    sl = slice(cc * C, (cc + 1) * C)
                    ii = e.matmul(p[0][:, sl], ugG(cc), w_Wb[:, sl], start=True, stop=True)
                    ii = e.matmul(p[1][:, sl], w_Wb[:, sl], ugU(cc), start=True, stop=True)
                ii.then_inc(s, 1)
            st([("tensor", mm_mc, 1)], rd=[P("UG"), P("Wb")], wr=["p0", "p1"])
            def ev_mc(e, s, h=h, w_McT=w_McT, w_Nc=w_Nc):
                for cc in range(KR):
                    sl = slice(cc * C, (cc + 1) * C)
                    e.scalar_tensor_tensor(
                        out=w_McT[:, sl], in0=c_identf[:],
                        scalar=accols[:, cc * NV + h: cc * NV + h + 1],
                        in1=p[0][:, sl], op0=ALU.mult, op1=ALU.subtract).then_inc(s, 1)
                e.tensor_copy(out=w_Nc[:], in_=p[1][:]).then_inc(s, 1)
            st([("vector", ev_mc, KR + 1)], rd=["p0", "p1", "accols"], wr=[P("McT"), P("Nc")])
            def mm_h3(e, s, ugG=ugG, w_PmT=w_PmT):
                for cc in range(KR):
                    sl = slice(cc * C, (cc + 1) * C)
                    ii = e.matmul(p[0][:, sl], ugG(cc), w_PmT[:, sl], start=True, stop=True)
                ii.then_inc(s, 1)
            st([("tensor", mm_h3, 1)], rd=[P("UG"), P("PmT")], wr=["p0"])
            def ev_h3(e, s, w_QwT=w_QwT, w_H3=w_H3):
                e.tensor_tensor(out=w_H3[:], in0=w_QwT[:], in1=p[0][:],
                                op=ALU.subtract).then_inc(s, 1)
            st([("vector", ev_h3, 1)], rd=[P("QwT"), "p0"], wr=[P("H3")])

            for cc in range(KR):
                sl = slice(cc * C, (cc + 1) * C)
                hsl = slice(h * 128, (h + 1) * 128)
                def mm_sw(e, s, cc=cc, sl=sl, hsl=hsl, ugU=ugU, w_PmT=w_PmT,
                          w_H3=w_H3, w_McT=w_McT):
                    ii = e.matmul(p[1][:, sl], Pi_b[:, hsl], w_H3[:, sl],
                                  start=True, stop=True)
                    ii = e.matmul(p[0][:, sl], ugU(cc), w_PmT[:, sl],
                                  start=True, stop=False)
                    ii = e.matmul(p[0][:, sl], Qs_b[:, hsl], w_H3[:, sl],
                                  start=False, stop=True)
                    ii = e.matmul(p[2][:, 0:128], w_McT[:, sl], Qs_b[:, hsl],
                                  start=True, stop=True)
                    ii = e.matmul(p[2][:, 128:256], w_McT[:, sl], Pi_b[:, hsl],
                                  start=True, stop=True)
                    ii.then_inc(s, 1)
                st([("tensor", mm_sw, 1)],
                   rd=[f"Pi{h}", f"Qb{h}", P("UG"), P("PmT"), P("H3"), P("McT")],
                   wr=["p0", "p1", "p2"])
                def ev_sw(e, s, cc=cc, sl=sl, hsl=hsl, w_H5=w_H5, w_Ob=w_Ob, w_Nc=w_Nc):
                    e.tensor_tensor(out=Qs_f[:, hsl], in0=w_Nc[:, sl],
                                    in1=p[2][:, 0:128], op=ALU.add).then_inc(s, 1)
                    e.tensor_copy(out=Pi_b[:, hsl], in_=p[2][:, 128:256]).then_inc(s, 1)
                st([("vector", ev_sw, 2)], rd=["p2", P("Nc")], wr=[f"Qs{h}", f"Pi{h}"])
                def ev_sw_s(e, s, cc=cc, sl=sl, w_H5=w_H5, w_Ob=w_Ob):
                    e.copy(out=w_H5[:, sl], in_=p[1][:, sl]).then_inc(s, 1)
                    e.copy(out=w_Ob[:, sl], in_=p[0][:, sl]).then_inc(s, 1)
                st([("scalar", ev_sw_s, 2)], rd=["p0", "p1"], wr=[P("H5"), P("Ob")])
                def ev_sw2(e, s, hsl=hsl):
                    e.copy(out=Qs_b[:, hsl], in_=Qs_f[:, hsl]).then_inc(s, 1)
                st([("scalar", ev_sw2, 1)], rd=[f"Qs{h}"], wr=[f"Qb{h}"])
            def sp_w(e, s, hc0=hc0, w_Ob=w_Ob, w_H5=w_H5):
                dma(e, s, sp_ob[hc0 * 128:(hc0 + KR) * 128, :]
                    .rearrange("(cc p) t -> p cc t", p=128), w_Ob[:])
                dma(e, s, sp_h5[hc0 * 128:(hc0 + KR) * 128, :]
                    .rearrange("(cc p) t -> p cc t", p=128), w_H5[:])
            st([("gpsimd", sp_w, 32)], rd=[P("Ob"), P("H5")], wr=["spob", "sph5"])
        RUNS = ([1, 2, 2, 2, 1, 1, 2, 1, 1, 1, 1] + [4] * NEWTON
                + [2, 1, 2, 2, 2] + [4] * KR + [1])
        for h0 in range(0, NV, 2):
            qa, qb = [], []
            emit_head(h0, qa)
            emit_head(h0 + 1, qb)
            assert len(qa) == sum(RUNS), (len(qa), sum(RUNS))
            pos = 0
            for rl in RUNS:
                for q in (qa, qb):
                    for i in range(pos, pos + rl):
                        bld.step(q[i][0], rd=q[i][1], wr=q[i][2])
                pos += rl


    if STAGE == "A":
        bld.barrier()
        st([("gpsimd", lambda e, s: dma(e, s, outT[0:128, 0:RT], xtb[:, 0:RT]), 16)])
        return _finish(nc, bld)

    if STAGE == "B":
        def tap_b(e, s):
            if "ob" in taps:
                dma(e, s, taps["ob"], sp_ob)
            if "h5" in taps:
                dma(e, s, taps["h5"], sp_h5)
            if "qsf" in taps:
                dma(e, s, taps["qsf"], Qs_f[:])
            if "pib" in taps:
                dma(e, s, taps["pib"], Pi_b[:])
        bld.barrier()
        st([("gpsimd", tap_b, 64)])
        st([("gpsimd", lambda e, s: dma(e, s, outT[0:128, 0:RT], xtb[:, 0:RT]), 16)])
        return _finish(nc, bld)

    for h in range(NV):
        hsl = slice(h * 128, (h + 1) * 128)
        def mm_mt(e, s, hsl=hsl):
            ii = e.transpose(pb[:, 0:128], Pi_b[:, hsl], c_identb[:])
            ii.then_inc(s, 1)
        st([("tensor", mm_mt, 1)], rd=[f"Pi{h}"], wr=["pb"])
        wOb = w_Ob2[h % 2]
        def ev_mt(e, s, hsl=hsl, wOb=wOb):
            e.tensor_copy(out=wOb[:, 0:128], in_=pb[:, 0:128]).then_inc(s, 1)
            e.tensor_copy(out=wOb[:, 128:256], in_=Qs_f[:, hsl]).then_inc(s, 1)
        st([("vector", ev_mt, 2)], rd=["pb", f"Qs{h}"], wr=[f"Ob{h % 2}"])
        def st_mt(e, s, h=h, wOb=wOb):
            dma(e, s, cc_in[h * 128:(h + 1) * 128, :], wOb[:, 0:128])
            dma(e, s, cc_in[(NV + h) * 128:(NV + h + 1) * 128, :], wOb[:, 128:256])
        st([("gpsimd", st_mt, 32)], rd=[f"Ob{h % 2}"], wr=["ccin"])
    if SIM1:
        def coll(e, s):
            for j in range(NCORE):
                dma(e, s, cc_out[j * 2 * NV * 128:(j + 1) * 2 * NV * 128, :], cc_in)
        st([("gpsimd", coll, 16 * NCORE)], rd=["ccin"], wr=["ccout"])
    else:
        def coll(e, s):
            e.collective_compute("AllGather", ALU.bypass,
                                 replica_groups=[list(range(NCORE))],
                                 ins=[cc_in.opt()], outs=[cc_out.opt()]).then_inc(s)
        st([("gpsimd", coll, 1, "c")], rd=["ccin"], wr=["ccout"])

    bld.barrier()
    for g in reversed(ab_guards):
        g.__exit__(None, None, None)
    Sin = sb("Sin", [128, NV * 128], BF16)
    gat = sb("gat", [128, NV * 512], BF16)
    xres = sb("xres", [128, 512], F32)
    sums = sb("sums", [1, 512], F32)
    sumss = sb("sumss", [1, 512], F32)
    rstd = sb("rstd", [1, 512], F32)
    obt = sb("obt", [128, 512], BF16)
    h5t = sb("h5t", [128, 512], BF16)
    sq2 = sb("sq2", [128, 512], BF16)
    x1t = sb("x1t", [128, 512], F32)
    csq = sb("csq", [128, 512], F32)
    routb = sb("routb", [128, NEXP * NDC], BF16)
    wtok = sb("wtok", [128, 16 * 8], F32)
    wtokb = sb("wtokb", [128, 16 * 8], BF16)
    wrow = sb("wrow", [1, TPC], BF16)
    lg8 = sb("lg8", [128, 8], F32)
    pr8 = sb("pr8", [128, 8], F32)
    m18 = sb("m18", [128, 1], F32)
    m28 = sb("m28", [128, 1], F32)
    ssum = sb("ssum", [128, 1], F32)
    sug = sb("sug", [128, 512], F32)
    fot = sb("fot", [128, 512], F32)
    pc_guards = []
    def sbc(n, s, d):
        g = nc.sbuf_tensor(n, s, d)
        ap = g.__enter__()
        pc_guards.append(g)
        return ap
    MtJ = sbc("MtJ", [128, NV * 128], BF16)
    NJ = sbc("NJ", [128, NV * 128], BF16)
    Sacc = sbc("Sacc", [128, NV * 128], BF16)

    def init_sin(e, s):
        e.memset(Sacc[:], 0.0).then_inc(s, 1)
        e.memset(Sin[:], 0.0).then_inc(s, 1)
    st([("vector", init_sin, 2)], wr=["SaccA", "SinA"])
    for j in range(NCORE - 1):
        def ld_mj(e, s, j=j):
            dma(e, s, MtJ[:], cc_out[j * 8192:(j * 8192 + 4096), :]
                .rearrange("(h p) c -> p h c", p=128))
            dma(e, s, NJ[:], cc_out[(j * 8192 + 4096):(j + 1) * 8192, :]
                .rearrange("(h p) c -> p h c", p=128))
        st([("gpsimd", ld_mj, 32)], rd=["ccout"] + [f"Sc{h}" for h in range(NV)], wr=["MtJ", "NJ"])
        for h in range(NV):
            hsl = slice(h * 128, (h + 1) * 128)
            def mm_pc(e, s, hsl=hsl):
                ii = e.matmul(p[2][:, 0:128], MtJ[:, hsl], Sacc[:, hsl],
                         start=True, stop=True).then_inc(s, 1)
            st([("tensor", mm_pc, 1)], rd=["MtJ", f"Sc{h}", "SaccA"], wr=["p2"])
            def ev_pc(e, s, hsl=hsl):
                e.tensor_tensor(out=Sacc[:, hsl], in0=NJ[:, hsl], in1=p[2][:, 0:128],
                                op=ALU.add).then_inc(s, 1)
            st([("vector", ev_pc, 1)], rd=["NJ", "p2", "SaccA"], wr=[f"Sc{h}"])
            def ev_pc2(e, s, hsl=hsl, j=j):
                e.scalar_tensor_tensor(out=Sin[:, hsl], in0=Sacc[:, hsl],
                                       scalar=c_onehot[:, j + 1: j + 2],
                                       in1=Sin[:, hsl], op0=ALU.mult,
                                       op1=ALU.add).then_inc(s, 1)
            st([("vector", ev_pc2, 1)], rd=[f"Sc{h}", "SinA"], wr=[f"Si{h}"])

    bld.barrier()
    for g in reversed(pc_guards):
        g.__exit__(None, None, None)
    sut2 = [sb("sutA", [128, NFB * 512], BF16), sb("sutB", [128, NFB * 512], BF16)]
    wgu_g = nc.sbuf_tensor("wguA", [128, 2 * 2048], BF16)
    wgu_g2 = nc.sbuf_tensor("wguB", [128, 2 * 2048], BF16)
    wgu2 = [wgu_g.__enter__(), wgu_g2.__enter__()]
    if "sin" in taps:
        st([("gpsimd", lambda e, s: dma(e, s, taps["sin"], Sin[:]), 16)],
           rd=[f"Si{h}" for h in range(NV)])

    for tt in range(NR):
        for vc in range(NV):
            def ld_oh(e, s, vc=vc, tt=tt):
                base = vc * NCH + tt * KR
                dma(e, s, obt[:], sp_ob[base * 128:(base + KR) * 128, :]
                    .rearrange("(cc p) t -> p cc t", p=128))
                dma(e, s, h5t[:], sp_h5[base * 128:(base + KR) * 128, :]
                    .rearrange("(cc p) t -> p cc t", p=128))
            st([("gpsimd", ld_oh, 32)], rd=["spob", "sph5"], wr=["obt", "h5t"])
            def mm_corr(e, s, vc=vc):
                for cc in range(KR):
                    sl = slice(cc * C, (cc + 1) * C)
                    ii = e.matmul(p[3][:, sl], Sin[:, vc * 128:(vc + 1) * 128],
                             h5t[:, sl], start=True, stop=True)
                ii.then_inc(s, 1)
            st([("tensor", mm_corr, 1)], rd=[f"Si{vc}", "h5t"], wr=["p3"])
            def ev_corr(e, s, vc=vc):
                e.tensor_tensor(out=gat[:, vc * RT:(vc + 1) * RT], in0=obt[:],
                                in1=p[3][:], op=ALU.add).then_inc(s, 1)
            st([("vector", ev_corr, 1)], rd=["obt", "p3"], wr=[f"gat{vc}"])
        for vc in range(NV):
            def sqv(e, s, vc=vc):
                e.tensor_tensor(out=sq2[:], in0=gat[:, vc * RT:(vc + 1) * RT],
                                in1=gat[:, vc * RT:(vc + 1) * RT],
                                op=ALU.mult).then_inc(s, 1)
            st([("vector", sqv, 1)], rd=[f"gat{vc}"], wr=["sq2"])
            def mm_sq(e, s, vc=vc):
                ii = e.matmul(p[5][0:1, 0:512], c_onescb[:], sq2[:],
                         start=(vc == 0), stop=(vc == NV - 1)).then_inc(s, 1)
            st([("tensor", mm_sq, 1)], rd=["sq2"], wr=["p5"])
        def rstd1(e, s):
            e.activation(out=rstd[:], in_=p[5][0:1, 0:512], func=AF.Sqrt,
                         bias=1e-6, scale=1.0 / (NV * DV)).then_inc(s, 1)
        st([("scalar", rstd1, 1)], rd=["p5"], wr=["rstd"])
        st([("vector", lambda e, s: e.reciprocal(out=rstd[:],
                                                 in_=rstd[:]).then_inc(s, 1), 1)],
           rd=["rstd"], wr=["rstd"])
        def mm_rb2(e, s):
            ii = e.matmul(p[5][:, 0:512], c_onesf[:], rstd[:],
                     start=True, stop=True).then_inc(s, 1)
        st([("tensor", mm_rb2, 1)], rd=["rstd"], wr=["p5"])
        def ld_xc(e, s, tt=tt):
            for k in range(NDC):
                dma(e, s, xtb[:, k * RT:(k + 1) * RT],
                    xT[k * 128:(k + 1) * 128, 3 + tt * RT: 3 + (tt + 1) * RT])
        st([("gpsimd", ld_xc, 16 * NDC)], wr=["xtb"])
        for vc in range(NV):
            buf = vc % 2
            def ld_wz(e, s, vc=vc, buf=buf):
                dma(e, s, wjb[:, buf * 2048:(buf + 1) * 2048],
                    WzT[:, vc * 128:(vc + 1) * 128]
                    .rearrange("(k p) j -> p k j", p=128))
            st([("gpsimd", ld_wz, 16)], wr=[f"wjb{buf}"])
            def mm_z(e, s, buf=buf):
                for k in range(NDC):
                    ii = e.matmul(p[4][:],
                             wjb[:, buf * 2048 + k * 128: buf * 2048 + (k + 1) * 128],
                             xtb[:, k * RT:(k + 1) * RT],
                             start=(k == 0), stop=(k == NDC - 1))
                ii.then_inc(s, 1)
            st([("tensor", mm_z, 1)], rd=[f"wjb{buf}", "xtb"], wr=["p4"])
            st([("scalar", lambda e, s: e.activation(
                out=csq[:], in_=p[4][:], func=AF.Silu).then_inc(s, 1), 1)],
               rd=["p4"], wr=["csq"])
            def gate2(e, s, vc=vc):
                e.scalar_tensor_tensor(out=gat[:, vc * RT:(vc + 1) * RT],
                                       in0=gat[:, vc * RT:(vc + 1) * RT],
                                       scalar=c_normw[:, vc:vc + 1],
                                       in1=p[5][:, 0:512], op0=ALU.mult,
                                       op1=ALU.mult).then_inc(s, 1)
            st([("vector", gate2, 1)], rd=[f"gat{vc}", "p5"], wr=[f"gat{vc}"])
            def gate3(e, s, vc=vc):
                e.tensor_tensor(out=gat[:, vc * RT:(vc + 1) * RT],
                                in0=gat[:, vc * RT:(vc + 1) * RT],
                                in1=csq[:], op=ALU.mult).then_inc(s, 1)
            st([("vector", gate3, 1)], rd=[f"gat{vc}", "csq"], wr=[f"gat{vc}"])
        if ("gat_t%d" % tt) in taps:
            st([("gpsimd", lambda e, s, tt=tt: dma(e, s, taps["gat_t%d" % tt],
                                                   gat[:]), 16)],
               rd=[f"gat{v}" for v in range(NV)])
        for dc in range(NDC):
            wob = wjb if dc % 2 == 0 else wgu2[0]
            wobn = ["wjb0", "wjb1"] if dc % 2 == 0 else ["wgu0"]
            def ld_wo(e, s, dc=dc, wob=wob):
                dma(e, s, wob[:, 0:2048], WoutT[:, dc * 128:(dc + 1) * 128]
                    .rearrange("(v p) j -> p v j", p=128)[:, 0:16, :])
                dma(e, s, wob[:, 2048:4096], WoutT[:, dc * 128:(dc + 1) * 128]
                    .rearrange("(v p) j -> p v j", p=128)[:, 16:32, :])
            st([("gpsimd", ld_wo, 32)], wr=wobn)
            def ld_xr(e, s, dc=dc, tt=tt):
                dma(e, s, xres[:], xT[dc * 128:(dc + 1) * 128,
                                      3 + tt * RT: 3 + (tt + 1) * RT])
            st([("sync", ld_xr, 16)], wr=["xres"])
            def mm_wo(e, s, wob=wob, dc=dc):
                for vc in range(NV):
                    ii = e.matmul(p[4 - dc % 2][:], wob[:, vc * 128:(vc + 1) * 128],
                             gat[:, vc * RT:(vc + 1) * RT],
                             start=(vc == 0), stop=(vc == NV - 1))
                ii.then_inc(s, 1)
            st([("tensor", mm_wo, 1)], rd=wobn + [f"gat{v}" for v in range(NV)],
               wr=[f"p{4 - dc % 2}"])
            def ev_wo(e, s, dc=dc):
                e.tensor_tensor(out=x1t[:], in0=xres[:], in1=p[4 - dc % 2][:],
                                op=ALU.add).then_inc(s, 1)
            st([("vector", ev_wo, 1)], rd=["xres", f"p{4 - dc % 2}"], wr=["x1t"])
            st([("vector", lambda e, s: e.tensor_copy(out=sq2[:],
                                                      in_=x1t[:]).then_inc(s, 1), 1)],
               rd=["x1t"], wr=["sq2"])
            def spw(e, s, dc=dc, tt=tt):
                dma(e, s, sp_x1[dc * 128:(dc + 1) * 128, tt * RT:(tt + 1) * RT], x1t[:])
            st([("gpsimd", spw, 16)], rd=["x1t"], wr=["spx1"])
            def mm_st1(e, s, dc=dc):
                ii = e.matmul(p[5][0:1, 0:512], c_onescb[:], sq2[:],
                         start=(dc == 0), stop=(dc == NDC - 1)).then_inc(s, 1)
            st([("tensor", mm_st1, 1)], rd=["sq2"], wr=["p5"])
            def sq_x(e, s):
                e.tensor_tensor(out=csq[:], in0=sq2[:], in1=sq2[:],
                                op=ALU.mult).then_inc(s, 1)
            st([("vector", sq_x, 1)], rd=["sq2"], wr=["csq"])
            def mm_st2(e, s, dc=dc):
                ii = e.matmul(p[2][0:1, 0:512], c_trii[:, 127:128], csq[:],
                         start=(dc == 0), stop=(dc == NDC - 1)).then_inc(s, 1)
            st([("tensor", mm_st2, 1)], rd=["csq"], wr=["p2"])
        def ln_f1(e, s):
            e.tensor_scalar(out=sums[:], in0=p[5][0:1, 0:512], scalar1=1.0 / D,
                            scalar2=None, op0=ALU.mult).then_inc(s, 1)
            e.tensor_scalar(out=sumss[:], in0=p[2][0:1, 0:512], scalar1=1.0 / D,
                            scalar2=None, op0=ALU.mult).then_inc(s, 1)
        st([("vector", ln_f1, 2)], rd=["p5", "p2"], wr=["sums", "sumss"])
        st([("vector", lambda e, s: e.tensor_tensor(out=rstd[:], in0=sums[:],
                                                    in1=sums[:],
                                                    op=ALU.mult).then_inc(s, 1), 1)],
           rd=["sums"], wr=["rstd"])
        st([("vector", lambda e, s: e.tensor_tensor(out=rstd[:], in0=sumss[:],
                                                    in1=rstd[:],
                                                    op=ALU.subtract).then_inc(s, 1), 1)],
           rd=["sumss", "rstd"], wr=["rstd"])
        st([("scalar", lambda e, s: e.activation(out=rstd[:], in_=rstd[:],
                                                 func=AF.Sqrt, bias=1e-5
                                                 ).then_inc(s, 1), 1)],
           rd=["rstd"], wr=["rstd"])
        st([("vector", lambda e, s: e.reciprocal(out=rstd[:],
                                                 in_=rstd[:]).then_inc(s, 1), 1)],
           rd=["rstd"], wr=["rstd"])
        def mm_lb2(e, s):
            ii = e.matmul(p[4][:, 0:512], c_onesf[:], sums[:], start=True, stop=True)
            ii = e.matmul(p[3][:, 0:512], c_onesf[:], rstd[:], start=True, stop=True)
            ii.then_inc(s, 1)
        st([("tensor", mm_lb2, 1)], rd=["sums", "rstd"], wr=["p4", "p3"])
        for dc in range(NDC):
            def ld_x1(e, s, dc=dc, tt=tt):
                dma(e, s, xres[:], sp_x1[dc * 128:(dc + 1) * 128,
                                         tt * RT:(tt + 1) * RT])
            st([("sync", ld_x1, 16)], rd=["spx1"], wr=["xres"])
            def nx1(e, s):
                e.tensor_tensor(out=x1t[:], in0=xres[:], in1=p[4][:, 0:512],
                                op=ALU.subtract).then_inc(s, 1)
            st([("vector", nx1, 1)], rd=["xres", "p4"], wr=["x1t"])
            def nx2(e, s):
                e.tensor_tensor(out=x1t[:], in0=x1t[:], in1=p[3][:, 0:512],
                                op=ALU.mult).then_inc(s, 1)
            st([("vector", nx2, 1)], rd=["x1t", "p3"], wr=["x1t"])
            def spw2(e, s, dc=dc, tt=tt):
                dma(e, s, sp_x1[dc * 128:(dc + 1) * 128, tt * RT:(tt + 1) * RT], x1t[:])
            st([("gpsimd", spw2, 16)], rd=["x1t"], wr=["spx1"])
            st([("vector", lambda e, s: e.tensor_copy(out=sq2[:],
                                                      in_=x1t[:]).then_inc(s, 1), 1)],
               rd=["x1t"], wr=["sq2"])
            def sq_y(e, s):
                e.tensor_tensor(out=csq[:], in0=sq2[:], in1=sq2[:],
                                op=ALU.mult).then_inc(s, 1)
            st([("vector", sq_y, 1)], rd=["sq2"], wr=["csq"])
            def mm_st3(e, s, dc=dc):
                ii = e.matmul(p[2][0:1, 0:512], c_trii[:, 127:128], csq[:],
                         start=(dc == 0), stop=(dc == NDC - 1)).then_inc(s, 1)
            st([("tensor", mm_st3, 1)], rd=["csq"], wr=["p2"])
        st([("scalar", lambda e, s: e.activation(out=rstd[:], in_=p[2][0:1, 0:512],
                                                 func=AF.Sqrt, bias=1e-6,
                                                 scale=1.0 / D).then_inc(s, 1), 1)],
           rd=["p2"], wr=["rstd"])
        st([("vector", lambda e, s: e.reciprocal(out=rstd[:],
                                                 in_=rstd[:]).then_inc(s, 1), 1)],
           rd=["rstd"], wr=["rstd"])
        def mm_rb3(e, s):
            ii = e.matmul(p[3][:, 0:512], c_onesf[:], rstd[:],
                     start=True, stop=True).then_inc(s, 1)
        st([("tensor", mm_rb3, 1)], rd=["rstd"], wr=["p3"])
        for dc in range(NDC):
            def ld_x2(e, s, dc=dc, tt=tt):
                dma(e, s, xres[:], sp_x1[dc * 128:(dc + 1) * 128,
                                         tt * RT:(tt + 1) * RT])
            st([("sync", ld_x2, 16)], rd=["spx1"], wr=["xres"])
            def hmk(e, s, dc=dc, tt=tt):
                e.scalar_tensor_tensor(
                    out=qkv[:, (dc * NR + tt) * RT:(dc * NR + tt + 1) * RT],
                    in0=xres[:], scalar=c_rmsw[:, dc:dc + 1], in1=p[3][:, 0:512],
                    op0=ALU.mult, op1=ALU.mult).then_inc(s, 1)
            st([("vector", hmk, 1)], rd=["xres", "p3"], wr=[f"qk{dc * NR + tt}"])

    def htile(dc, tt):
        return qkv[:, (dc * NR + tt) * RT:(dc * NR + tt + 1) * RT]

    if "hfull" in taps:
        st([("gpsimd", lambda e, s: dma(e, s, taps["hfull"], qkv[:]), 16)],
           rd=[f"qk{j}" for j in range(NJB)])

    st([("gpsimd", lambda e, s: dma(e, s, routb[:],
                                    routw.rearrange("(k p) e -> p k e", p=128)), 16)],
       wr=["routb"])
    for tg in range(16):
        tt, cc = tg // KR, tg % KR
        def mm_ro(e, s, tt=tt, cc=cc):
            for k in range(NDC):
                ii = e.matmul(p[0][:, 0:8], htile(k, tt)[:, cc * C:(cc + 1) * C],
                         routb[:, k * 8:(k + 1) * 8],
                         start=(k == 0), stop=(k == NDC - 1))
            ii.then_inc(s, 1)
        st([("tensor", mm_ro, 1)], rd=["routb"] + [f"qk{k * NR + tt}" for k in range(NDC)], wr=["p0"])
        def ro1(e, s):
            e.reduce_max(out=m18[:], in_=p[0][:, 0:8], axis=AX.X).then_inc(s, 1)
        st([("vector", ro1, 1)], rd=["p0"], wr=["m18"])
        def ro1b(e, s):
            e.tensor_scalar(out=m28[:], in0=m18[:], scalar1=-1.0, scalar2=None,
                            op0=ALU.mult).then_inc(s, 1)
        st([("vector", ro1b, 1)], rd=["m18"], wr=["m28"])
        def ro2(e, s):
            e.activation(out=lg8[:], in_=p[0][:, 0:8], func=AF.Exp,
                         bias=m28[:]).then_inc(s, 1)
        st([("scalar", ro2, 1)], rd=["p0", "m28"], wr=["lg8"])
        st([("vector", lambda e, s: e.reduce_sum(out=ssum[:], in_=lg8[:],
                                                 axis=AX.X).then_inc(s, 1), 1)],
           rd=["lg8"], wr=["ssum"])
        st([("vector", lambda e, s: e.reciprocal(out=ssum[:],
                                                 in_=ssum[:]).then_inc(s, 1), 1)],
           rd=["ssum"], wr=["ssum"])
        st([("vector", lambda e, s: e.tensor_scalar(
            out=pr8[:], in0=lg8[:], scalar1=ssum[:], scalar2=None,
            op0=ALU.mult).then_inc(s, 1), 1)],
           rd=["lg8", "ssum"], wr=["pr8"])
        st([("vector", lambda e, s: e.reduce_max(out=m18[:], in_=pr8[:],
                                                 axis=AX.X).then_inc(s, 1), 1)],
           rd=["pr8"], wr=["m18"])
        def ro7(e, s):
            e.tensor_scalar(out=lg8[:], in0=pr8[:], scalar1=m18[:], scalar2=-2.0,
                            op0=ALU.is_ge, op1=ALU.mult).then_inc(s, 1)
        st([("vector", ro7, 1)], rd=["pr8", "m18"], wr=["lg8"])
        st([("vector", lambda e, s: e.tensor_tensor(out=lg8[:], in0=pr8[:],
                                                    in1=lg8[:],
                                                    op=ALU.add).then_inc(s, 1), 1)],
           rd=["pr8", "lg8"], wr=["lg8"])
        st([("vector", lambda e, s: e.reduce_max(out=m28[:], in_=lg8[:],
                                                 axis=AX.X).then_inc(s, 1), 1)],
           rd=["lg8"], wr=["m28"])
        def ro10(e, s):
            e.tensor_scalar(out=lg8[:], in0=pr8[:], scalar1=m28[:], scalar2=None,
                            op0=ALU.is_ge).then_inc(s, 1)
        st([("vector", ro10, 1)], rd=["pr8", "m28"], wr=["lg8"])
        st([("vector", lambda e, s: e.tensor_tensor(out=m18[:], in0=m18[:],
                                                    in1=m28[:],
                                                    op=ALU.add).then_inc(s, 1), 1)],
           rd=["m18", "m28"], wr=["m18"])
        st([("vector", lambda e, s: e.reciprocal(out=m18[:],
                                                 in_=m18[:]).then_inc(s, 1), 1)],
           rd=["m18"], wr=["m18"])
        st([("vector", lambda e, s: e.tensor_tensor(out=lg8[:], in0=pr8[:],
                                                    in1=lg8[:],
                                                    op=ALU.mult).then_inc(s, 1), 1)],
           rd=["pr8", "lg8"], wr=["lg8"])
        def ro14(e, s, tg=tg):
            e.tensor_scalar(out=wtok[:, tg * 8:(tg + 1) * 8], in0=lg8[:],
                            scalar1=m18[:], scalar2=None,
                            op0=ALU.mult).then_inc(s, 1)
        st([("vector", ro14, 1)], rd=["lg8", "m18"], wr=["wtok"])
    st([("vector", lambda e, s: e.tensor_copy(out=wtokb[:],
                                              in_=wtok[:]).then_inc(s, 1), 1)],
       rd=["wtok"], wr=["wtokb"])
    def mm_wt(e, s):
        ii = e.transpose(pb[:, 0:128], wtokb[:], c_identb[:])
        ii.then_inc(s, 1)
    st([("tensor", mm_wt, 1)], rd=["wtokb"], wr=["pb"])
    st([("vector", lambda e, s: e.tensor_copy(out=wtokb[:],
                                              in_=pb[:, 0:128]).then_inc(s, 1), 1)],
       rd=["pb"], wr=["wtokb"])
    def wr_w(e, s):
        dma(e, s, arena_wr, wtokb[:])
    st([("gpsimd", wr_w, 16)], rd=["wtokb"], wr=["Awr"])
    if "wrow" in taps:
        def tap_wr(e, s):
            dma(e, s, taps["wrow"], arena_wr)
        st([("gpsimd", tap_wr, 16)], rd=["Awr"])

    for e_i in range(NEXP + 1):
        if e_i < NEXP:
            def ld_wrow(e, s, e_i=e_i):
                dma(e, s, wrow[0:1, :],
                    arena_wr.rearrange("(tg e) t -> e tg t", e=8)[e_i:e_i + 1, :, :])
            st([("gpsimd", ld_wrow, 16)], rd=["Awr"], wr=["wrow"])
        for fb in range(NFB):
            gbuf = (e_i * NFB + fb) % 2
            wgu = wgu2[gbuf]
            def ld_wgu(e, s, e_i=e_i, fb=fb, wgu=wgu):
                if e_i < NEXP:
                    src_g = wg[e_i, :, fb * 128:(fb + 1) * 128]
                    src_u = wu[e_i, :, fb * 128:(fb + 1) * 128]
                else:
                    src_g = swg[:, fb * 128:(fb + 1) * 128]
                    src_u = swu[:, fb * 128:(fb + 1) * 128]
                dma(e, s, wgu[:, 0:2048], src_g.rearrange("(k p) j -> p k j", p=128))
                dma(e, s, wgu[:, 2048:4096], src_u.rearrange("(k p) j -> p k j", p=128))
            st([("gpsimd", ld_wgu, 32)], wr=[f"wgu{gbuf}"])
            for tt in range(NR):
                def mm_gu(e, s, tt=tt, wgu=wgu):
                    for k in range(NDC):
                        ii = e.matmul(p[0][:], wgu[:, k * 128:(k + 1) * 128],
                                 htile(k, tt), start=(k == 0), stop=(k == NDC - 1))
                    for k in range(NDC):
                        ii = e.matmul(p[1][:], wgu[:, 2048 + k * 128: 2048 + (k + 1) * 128],
                                 htile(k, tt), start=(k == 0), stop=(k == NDC - 1))
                    ii.then_inc(s, 1)
                st([("tensor", mm_gu, 1)], rd=[f"wgu{gbuf}"] + [f"qk{k * NR + tt}" for k in range(NDC)], wr=["p0", "p1"])
                st([("scalar", lambda e, s: e.activation(
                    out=sug[:], in_=p[0][:], func=AF.Silu).then_inc(s, 1), 1)],
                   rd=["p0"], wr=["sug"])
                if e_i < NEXP:
                    def mm_wb2(e, s, tt=tt):
                        ii = e.matmul(p[2][:, 0:512], c_onesb[:],
                                 wrow[0:1, tt * RT:(tt + 1) * RT],
                                 start=True, stop=True).then_inc(s, 1)
                    st([("tensor", mm_wb2, 1)], rd=["wrow"], wr=["p2"])
                    def ev_gu2(e, s):
                        e.tensor_tensor(out=sug[:], in0=sug[:], in1=p[1][:],
                                        op=ALU.mult).then_inc(s, 1)
                    st([("vector", ev_gu2, 1)], rd=["sug", "p1"], wr=["sug"])
                    def ev_gu3(e, s):
                        e.tensor_tensor(out=sq2[:], in0=sug[:], in1=p[2][:, 0:512],
                                        op=ALU.mult).then_inc(s, 1)
                    st([("vector", ev_gu3, 1)], rd=["sug", "p2"], wr=["sq2"])
                else:
                    def ev_gu2s(e, s):
                        e.tensor_tensor(out=sq2[:], in0=sug[:], in1=p[1][:],
                                        op=ALU.mult).then_inc(s, 1)
                    st([("vector", ev_gu2s, 1)], rd=["sug", "p1"], wr=["sq2"])
                def sp_su_w(e, s, e_i=e_i, fb=fb, tt=tt):
                    dma(e, s, sp_su[(e_i * NFB + fb) * 128:(e_i * NFB + fb + 1) * 128,
                                    tt * RT:(tt + 1) * RT], sq2[:])
                st([("gpsimd", sp_su_w, 16)], rd=["sq2"], wr=["spsu"])

    bld.barrier()
    wgu_g2.__exit__(None, None, None)
    wgu_g.__exit__(None, None, None)
    wdb2 = [sb("wdbA", [128, 4096], BF16), sb("wdbB", [128, 4096], BF16)]
    pbg.__exit__(None, None, None)
    p7 = psum("p7", [128, 512], F32)
    pall = p + [p7]
    for half in range(2):
        for tt in range(NR):
            for e_i in range(NEXP + 1):
                sbuf_i = e_i % 2
                sut = sut2[sbuf_i]
                def ld_su(e, s, e_i=e_i, tt=tt, sut=sut):
                    dma(e, s, sut[:], sp_su[e_i * NFB * 128:(e_i + 1) * NFB * 128,
                                            tt * RT:(tt + 1) * RT]
                        .rearrange("(f p) t -> p f t", p=128))
                st([("gpsimd", ld_su, 16)], rd=["spsu"], wr=[f"sut{sbuf_i}"])
                for fh in range(2):
                    dbuf = (e_i * 2 + fh) % 2
                    wdb = wdb2[dbuf]
                    def ld_wd(e, s, e_i=e_i, half=half, fh=fh, wdb=wdb):
                        src_ = (wd[e_i, fh * 512:(fh + 1) * 512,
                                   half * 1024:(half + 1) * 1024] if e_i < NEXP
                                else swd[fh * 512:(fh + 1) * 512,
                                         half * 1024:(half + 1) * 1024])
                        dma(e, s, wdb[:], src_.rearrange("(f p) j -> p f j", p=128))
                    st([("gpsimd", ld_wd, 16)], wr=[f"wdb{dbuf}"])
                    def mm_dn(e, s, e_i=e_i, fh=fh, wdb=wdb, sut=sut):
                        for dc in range(8):
                            for f in range(4):
                                ii = e.matmul(pall[dc][:],
                                         wdb[:, (f * 8 + dc) * 128:(f * 8 + dc + 1) * 128],
                                         sut[:, (fh * 4 + f) * RT:(fh * 4 + f + 1) * RT],
                                         start=(e_i == 0 and fh == 0 and f == 0),
                                         stop=(e_i == NEXP and fh == 1 and f == 3))
                        ii.then_inc(s, 1)
                    st([("tensor", mm_dn, 1)], rd=[f"wdb{dbuf}", f"sut{sbuf_i}"], wr=[f"pd{i}" for i in range(8)])
            for dc in range(8):
                def ev_dn(e, s, dc=dc):
                    e.tensor_copy(out=fot[:], in_=pall[dc][:]).then_inc(s, 1)
                st([("vector", ev_dn, 1)], rd=[f"pd{dc}"], wr=["fot"])
                def st_dn(e, s, dc=dc, half=half, tt=tt):
                    dcg = half * 8 + dc
                    dma(e, s, sp_ffn[dcg * 128:(dcg + 1) * 128,
                                     tt * RT:(tt + 1) * RT], fot[:])
                st([("gpsimd", st_dn, 16)], rd=["fot"], wr=["spffn"])

    for tt in range(NR):
        for dc in range(NDC):
            def ld_f(e, s, dc=dc, tt=tt):
                dma(e, s, xres[:], sp_x1[dc * 128:(dc + 1) * 128,
                                         tt * RT:(tt + 1) * RT])
                dma(e, s, fot[:], sp_ffn[dc * 128:(dc + 1) * 128,
                                         tt * RT:(tt + 1) * RT])
            st([("sync", ld_f, 32)], rd=["spx1", "spffn"], wr=["xres", "fot"])
            def add_f(e, s):
                e.tensor_tensor(out=x1t[:], in0=xres[:], in1=fot[:],
                                op=ALU.add).then_inc(s, 1)
            st([("vector", add_f, 1)], rd=["xres", "fot"], wr=["x1t"])
            def spf(e, s, dc=dc, tt=tt):
                dma(e, s, sp_ffn[dc * 128:(dc + 1) * 128,
                                 tt * RT:(tt + 1) * RT], x1t[:])
            st([("gpsimd", spf, 16)], rd=["x1t"], wr=["spffn"])
            st([("vector", lambda e, s: e.tensor_copy(out=sq2[:],
                                                      in_=x1t[:]).then_inc(s, 1), 1)],
               rd=["x1t"], wr=["sq2"])
            def mm_sf1(e, s, dc=dc):
                ii = e.matmul(p[5][0:1, 0:512], c_onescb[:], sq2[:],
                         start=(dc == 0), stop=(dc == NDC - 1)).then_inc(s, 1)
            st([("tensor", mm_sf1, 1)], rd=["sq2"], wr=["p5"])
            def sq_f(e, s):
                e.tensor_tensor(out=csq[:], in0=sq2[:], in1=sq2[:],
                                op=ALU.mult).then_inc(s, 1)
            st([("vector", sq_f, 1)], rd=["sq2"], wr=["csq"])
            def mm_sf2(e, s, dc=dc):
                ii = e.matmul(p[2][0:1, 0:512], c_trii[:, 127:128], csq[:],
                         start=(dc == 0), stop=(dc == NDC - 1)).then_inc(s, 1)
            st([("tensor", mm_sf2, 1)], rd=["csq"], wr=["p2"])
        def ln2_f1(e, s):
            e.tensor_scalar(out=sums[:], in0=p[5][0:1, 0:512], scalar1=1.0 / D,
                            scalar2=None, op0=ALU.mult).then_inc(s, 1)
            e.tensor_scalar(out=sumss[:], in0=p[2][0:1, 0:512], scalar1=1.0 / D,
                            scalar2=None, op0=ALU.mult).then_inc(s, 1)
        st([("vector", ln2_f1, 2)], rd=["p5", "p2"], wr=["sums", "sumss"])
        st([("vector", lambda e, s: e.tensor_tensor(out=rstd[:], in0=sums[:],
                                                    in1=sums[:],
                                                    op=ALU.mult).then_inc(s, 1), 1)],
           rd=["sums"], wr=["rstd"])
        st([("vector", lambda e, s: e.tensor_tensor(out=rstd[:], in0=sumss[:],
                                                    in1=rstd[:],
                                                    op=ALU.subtract).then_inc(s, 1), 1)],
           rd=["sumss", "rstd"], wr=["rstd"])
        st([("scalar", lambda e, s: e.activation(out=rstd[:], in_=rstd[:],
                                                 func=AF.Sqrt,
                                                 bias=1e-5).then_inc(s, 1), 1)],
           rd=["rstd"], wr=["rstd"])
        st([("vector", lambda e, s: e.reciprocal(out=rstd[:],
                                                 in_=rstd[:]).then_inc(s, 1), 1)],
           rd=["rstd"], wr=["rstd"])
        def mm_lbf(e, s):
            ii = e.matmul(p[4][:, 0:512], c_onesf[:], sums[:], start=True, stop=True)
            ii = e.matmul(p[3][:, 0:512], c_onesf[:], rstd[:], start=True, stop=True)
            ii.then_inc(s, 1)
        st([("tensor", mm_lbf, 1)], rd=["sums", "rstd"], wr=["p4", "p3"])
        for dc in range(NDC):
            def ld_f2(e, s, dc=dc, tt=tt):
                dma(e, s, fot[:], sp_ffn[dc * 128:(dc + 1) * 128,
                                         tt * RT:(tt + 1) * RT])
            st([("sync", ld_f2, 16)], rd=["spffn"], wr=["fot"])
            def of1(e, s):
                e.tensor_tensor(out=x1t[:], in0=fot[:], in1=p[4][:, 0:512],
                                op=ALU.subtract).then_inc(s, 1)
            st([("vector", of1, 1)], rd=["fot", "p4"], wr=["x1t"])
            def of2(e, s):
                e.tensor_tensor(out=x1t[:], in0=x1t[:], in1=p[3][:, 0:512],
                                op=ALU.mult).then_inc(s, 1)
            st([("vector", of2, 1)], rd=["x1t", "p3"], wr=["x1t"])
            def of3(e, s, dc=dc, tt=tt):
                dma(e, s, outT[dc * 128:(dc + 1) * 128, tt * RT:(tt + 1) * RT], x1t[:])
            st([("gpsimd", of3, 16)], rd=["x1t"], wr=["outT"])

    return _finish(nc, bld)


def _finish(nc, bld):
    for si, val in sorted(bld.cnt.items()):
        if val > 0:
            bld.prog["gpsimd"].append(
                lambda e, si=si, val=val: e.wait_ge(bld.sems[si], val))

    def run_for(name):
        def _f(e):
            for fn in bld.prog[name]:
                fn(e)
        return _f
    with nc.Block() as block:
        block.sync(run_for("sync"))
        block.tensor(run_for("tensor"))
        block.vector(run_for("vector"))
        block.scalar(run_for("scalar"))
        block.gpsimd(run_for("gpsimd"))
    return nc


def _prep_inputs(inputs):
    f32 = np.float32
    x = np.ascontiguousarray(np.asarray(inputs["L0_attn_x"], f32).reshape(S, D))
    xTf = np.ascontiguousarray(x.T)
    Wqkvz = np.asarray(inputs["L0_attn_W_qkvz"], f32)
    WqkvT = np.ascontiguousarray(Wqkvz[:8192].T)
    WzT = np.ascontiguousarray(Wqkvz[8192:].T)
    WbaT = np.ascontiguousarray(np.asarray(inputs["L0_attn_W_ba"], f32).T)
    convw_ = np.ascontiguousarray(np.asarray(inputs["L0_attn_conv_w"], f32)[:, 0, :])
    dtb_ = np.ascontiguousarray(
        np.tile(np.asarray(inputs["L0_attn_dt_bias"], f32)[None, :], (128, 1)))
    alog_ = np.ascontiguousarray(np.asarray(inputs["L0_attn_A_log"], f32)[None, :])
    normw_ = np.ascontiguousarray(
        np.asarray(inputs["L0_attn_norm_w"], f32).reshape(NV, 128).T)
    WoutT = np.ascontiguousarray(np.asarray(inputs["L0_attn_W_out"], f32).T)
    rmsw_ = np.ascontiguousarray(
        np.asarray(inputs["L0_ffn_rms_weight"], f32).reshape(NDC, 128).T)
    routw_ = np.ascontiguousarray(np.asarray(inputs["L0_ffn_router_w"], f32))
    wg_ = np.ascontiguousarray(np.asarray(inputs["L0_ffn_w_gate"], f32))
    wu_ = np.ascontiguousarray(np.asarray(inputs["L0_ffn_w_up"], f32))
    wd_ = np.ascontiguousarray(np.asarray(inputs["L0_ffn_w_down"], f32))
    swg_ = np.ascontiguousarray(np.asarray(inputs["L0_ffn_sw_gate"], f32))
    swu_ = np.ascontiguousarray(np.asarray(inputs["L0_ffn_sw_up"], f32))
    swd_ = np.ascontiguousarray(np.asarray(inputs["L0_ffn_sw_down"], f32))

    bf = ml_dtypes.bfloat16
    ii, jj = np.arange(128)[:, None], np.arange(128)[None, :]
    mn_nat = np.where(jj < ii, 0.0, MASKNEG).astype(f32)
    mn_T = np.where(ii < jj, 0.0, MASKNEG).astype(f32)
    mn_iT = np.where(ii <= jj, 0.0, MASKNEG).astype(f32)
    consts = {
        "identb": np.eye(128, dtype=bf),
        "identb32": np.tile(np.eye(128, dtype=bf), (1, NV)),
        "identf": np.eye(128, dtype=f32),
        "trii": np.triu(np.ones((128, 128), f32)),
        "onesf": np.ones((1, 128), f32),
        "monesf": -np.ones((1, 128), f32),
        "onesb": np.ones((1, 128), bf),
        "fconst": np.tile(np.array([[1e-6, 1e-5, 1.0, 1.0 / 4096.0, 1.0 / 2048.0,
                                     -1.0]], f32), (128, 1)),
        "onescb": np.ones((128, 1), bf),
        "identb4": np.tile(np.eye(128, dtype=bf), (1, 4)),
        "mneg_nat4": np.tile(mn_nat, (1, 4)),
        "mneg_T4": np.tile(mn_T, (1, 4)),
        "mneg_iT4": np.tile(mn_iT, (1, 4)),
        "twoI4": np.tile(2.0 * np.eye(128, dtype=f32), (1, 4)),
    }
    in_maps = []
    for cix in range(NCORE):
        lo = cix * TPC
        halo = np.zeros((D, 3), f32)
        if lo >= 3:
            halo = xTf[:, lo - 3:lo]
        xslice = np.ascontiguousarray(np.concatenate([halo, xTf[:, lo:lo + TPC]], 1))
        oh = np.zeros((128, NCORE), f32)
        oh[:, cix] = 1.0
        m = {
            "xT": xslice, "WqkvT": WqkvT, "WzT": WzT, "WbaT": WbaT,
            "convw": convw_, "dtb": dtb_, "alog": alog_, "normw": normw_,
            "WoutT": WoutT, "rmsw": rmsw_, "routw": routw_,
            "wg": wg_, "wu": wu_, "wd": wd_, "swg": swg_, "swu": swu_, "swd": swd_,
            "onehot": np.ascontiguousarray(oh),
        }
        m.update(consts)
        in_maps.append(m)
    return in_maps


def kernel(**inputs):
    in_maps = _prep_inputs(inputs)
    nc = build_program()
    res = run_bass_kernel_spmd(nc, in_maps, core_ids=list(range(NCORE)))
    out = np.empty((S, D), np.float32)
    for cix in range(NCORE):
        out[cix * TPC:(cix + 1) * TPC, :] = res.results[cix]["outT"].T
    return out.reshape(1, S, D)
